# version 17
# speedup vs baseline: 1.0943x; 1.0112x over previous
import numpy as np
from contextlib import ExitStack

import concourse.bass as bass
import concourse.mybir as mybir
from concourse.bass import IndirectOffsetOnAxis
from concourse.bass_utils import run_bass_kernel_spmd
import ml_dtypes

F32 = mybir.dt.float32
BF16 = mybir.dt.bfloat16
I32 = mybir.dt.int32
AF = mybir.ActivationFunctionType
ALU = mybir.AluOpType
AX = mybir.AxisListType

NCORES = 8
B, S, D = 4, 8192, 1024
TPC = S // 2
H0, HD0 = 16, 64
MLP0 = 4096
FFN0 = 3072
H1, HD1 = 4, 256
WIN = 512
NPM = 64
HID1 = 4096
E_EXP, TOPK, CAP = 64, 8, 512
I_EXP, I_SH = 512, 2048
EPS = 1e-5

bfdt = ml_dtypes.bfloat16



class Seq:

    def __init__(self, name, sem, ops, chain=True):
        self.name = name
        self.sem = sem
        self.n = 0
        self.chain = chain
        self.ops = ops

    def wait(self, mark):
        if isinstance(mark, list):
            for m in mark:
                self.wait(m)
            return
        sem, val = mark
        if val > 0:
            self.ops.append(lambda e, s=sem, v=val: e.wait_ge(s, v))

    def mark(self):
        return (self.sem, self.n)

    def __getattr__(self, name):
        if name.startswith("_"):
            raise AttributeError(name)

        def wrapper(*a, **kw):
            need_wait = self.chain and self.n > 0
            nprev = self.n
            sem = self.sem

            def run(e):
                if need_wait:
                    e.wait_ge(sem, nprev)
                getattr(e, name)(*a, **kw).then_inc(sem, 1)

            self.ops.append(run)
            self.n += 1

        return wrapper


class Dma:

    def __init__(self, name, sem, ops):
        self.name = name
        self.sem = sem
        self.counts = {id(sem): 0}
        self.sems = {id(sem): sem}
        self.ops = ops

    def wait(self, mark):
        if isinstance(mark, list):
            for m in mark:
                self.wait(m)
            return
        sem, val = mark
        if val > 0:
            self.ops.append(lambda e, s=sem, v=val: e.wait_ge(s, v))

    def mark(self, sem=None):
        sem = sem if sem is not None else self.sem
        return (sem, 16 * self.counts.get(id(sem), 0))

    def _bump(self, sem):
        self.sems[id(sem)] = sem
        self.counts[id(sem)] = self.counts.get(id(sem), 0) + 1

    def start(self, out, in_, sem=None, **kw):
        sem = sem if sem is not None else self.sem

        def run(e):
            e.dma_start(out=out, in_=in_, **kw).then_inc(sem, 16)

        self.ops.append(run)
        self._bump(sem)

    def indirect(self, out, out_offset, in_, in_offset, sem=None, **kw):
        sem = sem if sem is not None else self.sem

        def run(e):
            e.indirect_dma_start(out=out, out_offset=out_offset, in_=in_,
                                 in_offset=in_offset, **kw).then_inc(sem, 16)

        self.ops.append(run)
        self._bump(sem)

    def final_wait(self):
        for sid, sem in self.sems.items():
            n = self.counts[sid]
            if n > 0:
                self.ops.append(lambda e, s=sem, v=16 * n: e.wait_ge(s, v))


class K:
    def __init__(self):
        self.nc = bass.Bass("TRN2", target_bir_lowering=False, debug=False)
        self.es = ExitStack()
        self._uid = 0
        self.ident = None

    def build_seqs(self):
        nc, es = self.nc, self.es
        sem = lambda n: es.enter_context(nc.semaphore(n))
        gp_ops = []
        s_ops = []
        self.t = Seq("t", sem("sem_t"), [], chain=False)
        self.v = Seq("v", sem("sem_v"), [])
        self.s = Seq("s", sem("sem_s"), s_ops)
        self.g = Seq("g", sem("sem_g"), gp_ops)
        self.d = Dma("d", sem("sem_d"), [])
        self.gd = Dma("gd", sem("sem_gd"), gp_ops)
        self.sd = Dma("sd", sem("sem_sd"), s_ops)
        self.ld_sems = [sem("sem_ld0"), sem("sem_ld1"), sem("sem_ld2")]
        self.ldr_sems = [sem("sem_ldr0"), sem("sem_ldr1"), sem("sem_ldr2")]
        self.tpd_sems = [sem("sem_tpd0"), sem("sem_tpd1")]
        self.id_sem = sem("sem_id")
        self.st_sems = [sem("sem_st0"), sem("sem_st1")]
        self.gld_sems = [sem("sem_gld0"), sem("sem_gld1")]
        self.st2_sems = [sem("sem_st2a"), sem("sem_st2b")]
        self.sd_sems = [sem("sem_sda"), sem("sem_sdb")]

    def finalize(self):
        self.d.final_wait()
        self.gd.final_wait()
        self.sd.final_wait()
        with self.nc.Block() as block:
            block.tensor(lambda e: [f(e) for f in self.t.ops])
            block.vector(lambda e: [f(e) for f in self.v.ops])
            block.scalar(lambda e: [f(e) for f in self.s.ops])
            block.sync(lambda e: [f(e) for f in self.d.ops])
            block.gpsimd(lambda e: [f(e) for f in self.g.ops])

    def uname(self, base):
        self._uid += 1
        return f"{base}_{self._uid}"

    def sb(self, name, shape, dtype, scope=None):
        return (scope or self.es).enter_context(
            self.nc.sbuf_tensor(self.uname(name), shape, dtype))

    def ps(self, scope, name, shape, dtype=F32):
        return scope.enter_context(self.nc.psum_tensor(self.uname(name), shape, dtype))

    def din(self, name, shape, dtype=F32):
        return self.nc.dram_tensor(name, shape, dtype, kind="ExternalInput").ap()

    def dout(self, name, shape, dtype=F32):
        return self.nc.dram_tensor(name, shape, dtype, kind="ExternalOutput").ap()

    def dint(self, name, shape, dtype=F32):
        return self.nc.dram_tensor(name, shape, dtype).ap()

    def init_common(self, ident_in):
        self.ident = self.sb("identsb", [128, 128], BF16)
        self.d.start(self.ident[:], ident_in, sem=self.id_sem)
        self.m_ident = self.d.mark(self.id_sem)
        self.t.wait(self.m_ident)
        self.ones_r = self.sb("ones_r", [1, 128], BF16)
        self.v.memset(self.ones_r[:], 1.0)



def load_cast(k, w_hbm, sb, nchunk, width, col0=0):
    for i in range(nchunk):
        k.gd.start(sb[:, col0 + i * width:col0 + (i + 1) * width],
                   w_hbm[i * 128:(i + 1) * 128, :])
    return k.gd.mark()


def stmarks(k):
    ms = [k.d.mark(k.st_sems[0]), k.d.mark(k.st_sems[1])]
    if hasattr(k, "st2_sems"):
        ms += [k.d.mark(k.st2_sems[0]), k.d.mark(k.st2_sems[1])]
    return ms


def guard_weights(k):
    k.gd.wait(k.v.mark())
    k.gd.wait(k.s.mark())
    k.gd.wait(k.t.mark())
    k.gd.wait(stmarks(k))


def load_dequant(k, w_hbm, sb, nchunk, width, wint, scale=1.0 / 127.0):
    for i0 in range(0, nchunk, 2):
        nn = min(2, nchunk - i0)
        for j in range(nn):
            k.gd.start(wint[:, j * width:(j + 1) * width],
                       w_hbm[(i0 + j) * 128:(i0 + j + 1) * 128, :])
        m_l = k.gd.mark()
        k.v.wait(m_l)
        for j in range(nn):
            k.v.tensor_scalar(out=sb[:, (i0 + j) * width:(i0 + j + 1) * width],
                              in0=wint[:, j * width:(j + 1) * width],
                              scalar1=scale, scalar2=None, op0=ALU.mult)
        k.gd.wait(k.v.mark())
    return k.v.mark()


def pass_gemm(k, src, T, Kd, N, wsb, wcol0=0, wstride=None, bias_sb=None,
              bias_col0=0, out=None, out_dtype=BF16, epilogue=None, res_srcs=(),
              src_mark=None, wait_marks=(), src_dtype=BF16,
              store_transposed=False):
    t, v, s, d = k.t, k.v, k.s, k.d
    nk = Kd // 128
    nj = N // 512
    wstride = wstride if wstride is not None else N
    assert N % 512 == 0 and 1 <= nj <= 4
    dbl = (nj <= 2)
    ntiles = T // 128
    es = ExitStack()
    with es:
        xin = [k.sb("gx", [128, Kd], src_dtype, es) for _ in range(3)]
        xT = [k.sb("gxT", [128, Kd], BF16, es) for _ in range(2)]
        osb = [k.sb("gout", [128, N], out_dtype, es) for _ in range(2)]
        tbo = ([k.sb("gtb", [128, N], out_dtype, es) for _ in range(2)]
               if store_transposed else None)
        resb = [[k.sb("gres", [128, rs.shape[1]], dt_, es) for _ in range(3)]
                for (rs, dt_) in res_srcs]
        p_tp = [k.ps(es, "gp_tp", [128, 1024], src_dtype) for _ in range(2)]
        p_out = [k.ps(es, "gp_out", [128, N], F32) for _ in range(2 if dbl else 1)]
        identx = k.ident[:]
        m_idf = None
        if src_dtype == F32:
            identf_g = k.sb("gidf", [128, 128], F32, es)
            k.v.wait(k.m_ident)
            k.v.tensor_copy(out=identf_g[:], in_=k.ident[:])
            m_idf = k.v.mark()
            identx = identf_g[:]

        entry_st = stmarks(k)

        def load_x(i):
            if src_mark is not None:
                d.wait(src_mark)
            for m in wait_marks:
                d.wait(m)
            d.wait(entry_st)
            sl = k.ld_sems[i % 3]
            d.start(xin[i % 3][:], src[i * 128:(i + 1) * 128, :], sem=sl)
            return d.mark(sl)

        def load_res(i):
            if not res_srcs:
                return None
            d.wait(entry_st)
            sl = k.ldr_sems[i % 3]
            for ri, (rs, _) in enumerate(res_srcs):
                d.start(resb[ri][i % 3][:], rs[i * 128:(i + 1) * 128, :], sem=sl)
            return d.mark(sl)

        marks_x = {}
        marks_r = {}
        for i in range(min(3, ntiles)):
            marks_x[i] = load_x(i)
            marks_r[i] = load_res(i)

        m_mm = {}
        m_xT = {}
        m_ep = {}
        m_out_dma = [entry_st, entry_st]

        last_round = (nk + 7) // 8 - 1

        def emit_transposes(i):
            if src_dtype == BF16:
                d.wait(marks_x[i])
                if i - 2 >= 0:
                    d.wait(m_mm[i - 2])
                sl = k.tpd_sems[i % 2]
                d.start(xT[i % 2][:].rearrange("p (c f) -> p c f", c=nk),
                        xin[i % 3][:].rearrange("p (c f) -> p c f", c=nk),
                        transpose=True, sem=sl)
                m_xT[(i, last_round)] = d.mark(sl)
                return
            t.wait(marks_x[i])
            if m_idf is not None:
                t.wait(m_idf)
            for r in range((nk + 7) // 8):
                if (i - 2, r) in m_xT:
                    t.wait(m_xT[(i - 2, r)])
                if (i, r - 1) in m_xT:
                    t.wait(m_xT[(i, r - 1)])
                c0, c1 = r * 8, min(r * 8 + 8, nk)
                for c in range(c0, c1):
                    t.transpose(p_tp[i % 2][:, (c - c0) * 128:(c - c0 + 1) * 128],
                                xin[i % 3][:, c * 128:(c + 1) * 128], identx)
                m_tpr = t.mark()
                v.wait(m_tpr)
                if i - 2 >= 0:
                    v.wait(m_mm[i - 2])
                v.tensor_copy(out=xT[i % 2][:, c0 * 128:c1 * 128],
                              in_=p_tp[i % 2][:, 0:(c1 - c0) * 128])
                m_xT[(i, r)] = v.mark()

        emit_transposes(0)
        if ntiles > 1:
            emit_transposes(1)

        for it in range(ntiles):
            if it + 3 < ntiles:
                d.wait(m_xT[(it, last_round)])
                marks_x[it + 3] = load_x(it + 3)
            pb = p_out[it % 2] if dbl else p_out[0]
            t.wait(m_xT[(it, last_round)])
            gate = m_ep.get(it - 2 if dbl else it - 1)
            if gate is not None:
                t.wait(gate)
            for c in range(nk):
                st = xT[it % 2][:, c * 128:(c + 1) * 128]
                for j in range(nj):
                    t.matmul(pb[:, j * 512:(j + 1) * 512], st,
                             wsb[:, wcol0 + c * wstride + j * 512:
                                 wcol0 + c * wstride + (j + 1) * 512],
                             start=(c == 0),
                             stop=(bias_sb is None and c == nk - 1))
            if bias_sb is not None:
                for j in range(nj):
                    t.matmul(pb[:, j * 512:(j + 1) * 512], k.ones_r[0:1, :],
                             bias_sb[0:1, bias_col0 + j * 512:bias_col0 + (j + 1) * 512],
                             start=False, stop=True)
            m_mm[it] = t.mark()
            if it + 2 < ntiles:
                emit_transposes(it + 2)
            aux = {"res": [rb[it % 3][:] for rb in resb], "tile": it,
                   "m_in": [marks_x[it]] + ([marks_r[it]] if res_srcs else [])}
            if res_srcs:
                v.wait(marks_r[it])
            v.wait(m_out_dma[it % 2])
            s.wait(m_out_dma[it % 2])
            if epilogue is None:
                v.wait(m_mm[it])
                if resb:
                    v.tensor_tensor(out=osb[it % 2][:], in0=pb[:],
                                    in1=aux["res"][0], op=ALU.add)
                    for rb in aux["res"][1:]:
                        v.tensor_tensor(out=osb[it % 2][:], in0=osb[it % 2][:],
                                        in1=rb, op=ALU.add)
                else:
                    v.tensor_copy(out=osb[it % 2][:], in_=pb[:])
                m_ep[it] = k.v.mark()
            else:
                m_ep[it] = epilogue(k, m_mm[it], pb[:], osb[it % 2][:], aux)
            d.wait(m_ep[it])
            if store_transposed:
                nkN = N // 128
                d.wait(m_out_dma[it % 2])
                d.start(tbo[it % 2][:].rearrange("p (c f) -> p c f", c=nkN),
                        osb[it % 2][:].rearrange("p (c f) -> p c f", c=nkN),
                        transpose=True, sem=k.st2_sems[it % 2])
                m_tb = d.mark(k.st2_sems[it % 2])
                d.wait(m_tb)
                d.start(out[:, it * 128:(it + 1) * 128]
                        .rearrange("(c p) f -> p c f", c=nkN),
                        tbo[it % 2][:].rearrange("p (c f) -> p c f", c=nkN),
                        sem=k.st_sems[it % 2])
            else:
                d.start(out[it * 128:(it + 1) * 128, :], osb[it % 2][:],
                        sem=k.st_sems[it % 2])
            m_out_dma[it % 2] = d.mark(k.st_sems[it % 2])
            if it + 3 < ntiles:
                marks_r[it + 3] = load_res(it + 3)
    return stmarks(k)


def emit_ln_core(k, x_ap, width, sm, scr_ap, rstd_ap, want_var=False):
    v, s = k.v, k.s
    v.tensor_reduce(out=sm[:, 0:1], in_=x_ap, axis=AX.X, op=ALU.add)
    v.tensor_scalar(out=sm[:, 0:1], in0=sm[:, 0:1], scalar1=1.0 / width,
                    scalar2=None, op0=ALU.mult)
    s.wait(k.v.mark())
    s.activation(out=scr_ap, in_=x_ap, func=AF.Square, accum_out=sm[:, 1:2])
    v.wait(k.s.mark())
    v.tensor_scalar(out=sm[:, 2:3], in0=sm[:, 1:2], scalar1=1.0 / width,
                    scalar2=None, op0=ALU.mult)
    v.tensor_tensor(out=sm[:, 3:4], in0=sm[:, 0:1], in1=sm[:, 0:1], op=ALU.mult)
    v.tensor_tensor(out=sm[:, 2:3], in0=sm[:, 2:3], in1=sm[:, 3:4], op=ALU.subtract)
    v.tensor_scalar(out=sm[:, 6:7], in0=sm[:, 2:3], scalar1=EPS, scalar2=None,
                    op0=ALU.add)
    s.wait(k.v.mark())
    s.activation(out=rstd_ap, in_=sm[:, 6:7], func=AF.Sqrt)
    v.wait(k.s.mark())
    v.reciprocal(out=rstd_ap, in_=rstd_ap)
    return k.v.mark()



def build_launch_A(T=TPC):
    k = K()
    ntiles = T // 128

    x0 = k.din("x0", [T, D])
    wq = k.din("wq", [D, D]); wk = k.din("wk", [D, D]); wv = k.din("wv", [D, D])
    woH = k.din("woH", [64, 16 * D])
    fcw = k.din("fcw", [D, MLP0]); projw = k.din("projw", [MLP0, D])
    wgT = k.din("wgT", [D, FFN0], I32); wuT = k.din("wuT", [D, FFN0], I32)
    wdT = k.din("wdT", [FFN0, D], I32)
    g1b = k.din("g1b", [128, D]); b1b = k.din("b1b", [128, D])
    g2b = k.din("g2b", [128, D]); b2b = k.din("b2b", [128, D])
    bq_r = k.din("bq_r", [1, D], BF16); bk_r = k.din("bk_r", [1, D], BF16)
    bv_r = k.din("bv_r", [1, D], BF16); bo_r = k.din("bo_r", [1, D], BF16)
    fcb_r = k.din("fcb_r", [1, MLP0], BF16); projb_r = k.din("projb_r", [1, D], BF16)
    gatew = k.din("gatew", [D, 65])
    identb = k.din("identb", [128, 128], BF16)
    bdm = k.din("bdm", [128, 128])
    iota64 = k.din("iota64", [128, 64])

    o_x1 = k.dout("x1", [T, D])
    o_n1 = k.dout("n1", [T, D], BF16)
    o_idx8 = k.dout("idx8", [T, TOPK])
    o_w8 = k.dout("w8", [T, TOPK])
    o_counts = k.dout("counts", [1, 64])
    o_sgate = k.dout("sgate", [T, 1])

    n0f_dram = k.dint("n0f_dram", [T, D])
    n0b_dram = k.dint("n0b_dram", [T, D], BF16)
    xn2_dram = k.dint("xn2_dram", [T, D], BF16)
    fq_dram = k.dint("fq_dram", [T, D], BF16)
    fk_dram = k.dint("fk_dram", [T, D], BF16)
    v_dram = k.dint("v_dram", [T, D], BF16)
    ea_dram = k.dint("ea_dram", [ntiles * 64, 2048], BF16)
    a_dram = k.dint("a_dram", [T, D])
    t2_dram = k.dint("t2_dram", [T, D], BF16)
    gg_dram = k.dint("gg_dram", [T, MLP0], BF16)
    brA_dram = k.dint("brA_dram", [T, D])
    gu_dram = k.dint("gu_dram", [T, FFN0], BF16)
    sg_dram = k.dint("sg_dram", [T, FFN0], BF16)

    with k.es:
        k.build_seqs()
        t, v, s, g, d, gd = k.t, k.v, k.s, k.g, k.d, k.gd
        k.init_common(identb)

        bd_sb = k.sb("bd", [128, 128], F32)
        iota_sb = k.sb("iota64s", [128, 64], F32)
        g1_sb = k.sb("g1s", [128, D], F32); b1_sb = k.sb("b1s", [128, D], F32)
        g2_sb = k.sb("g2s", [128, D], F32); b2_sb = k.sb("b2s", [128, D], F32)
        bias_q = k.sb("biasq", [1, D], BF16); bias_k = k.sb("biask", [1, D], BF16)
        bias_v = k.sb("biasv", [1, D], BF16); bias_o = k.sb("biaso", [1, D], BF16)
        bias_fc = k.sb("biasfc", [1, MLP0], BF16); bias_pj = k.sb("biaspj", [1, D], BF16)
        for dst, srcin in ((bd_sb, bdm), (iota_sb, iota64), (g1_sb, g1b), (b1_sb, b1b),
                           (g2_sb, g2b), (b2_sb, b2b), (bias_q, bq_r), (bias_k, bk_r),
                           (bias_v, bv_r), (bias_o, bo_r), (bias_fc, fcb_r),
                           (bias_pj, projb_r)):
            d.start(dst[:], srcin)
        m_const = d.mark()
        v.wait(m_const)
        eluscr = k.sb("eluscr", [128, D], F32)
        ln9 = {}
        ln9["scr"] = k.sb("p9scr", [128, D], F32)
        ln9["sm"] = k.sb("p9sm", [128, 8], F32)
        ln9["rstd"] = k.sb("p9rstd", [128, 1], F32)
        ln9["n1b"] = [k.sb("p9n1b", [128, D], BF16) for _ in range(2)]
        ln9["m_n1dma"] = [None, None]

        with ExitStack() as es1:
            xin = [k.sb("p1x", [128, D], F32, es1) for _ in range(2)]
            n0f = k.sb("p1n0f", [128, D], F32, es1)
            n0b = k.sb("p1n0b", [128, D], BF16, es1)
            xn2 = k.sb("p1xn2", [128, D], BF16, es1)
            scr = k.sb("p1scr", [128, D], F32, es1)
            sm = k.sb("p1sm", [128, 8], F32, es1)
            rstd = k.sb("p1rstd", [128, 2], F32, es1)
            entry_st1 = stmarks(k)
            d.wait(entry_st1)
            d.start(xin[0][:], x0[0:128, :], sem=k.ld_sems[0])
            mk = [d.mark(k.ld_sems[0])]
            if ntiles > 1:
                d.start(xin[1][:], x0[128:256, :], sem=k.ld_sems[1])
                mk.append(d.mark(k.ld_sems[1]))
            m_st = [None, None]
            for it in range(ntiles):
                xs = xin[it % 2]
                v.wait(mk[it])
                if it >= 1 and m_st[(it - 1) % 2] is not None:
                    v.wait(m_st[(it - 1) % 2])
                    s.wait(m_st[(it - 1) % 2])
                emit_ln_core(k, xs[:], D, sm, scr[:], rstd[:, 0:1])
                v.tensor_scalar(out=n0f[:], in0=xs[:], scalar1=sm[:, 0:1],
                                scalar2=rstd[:, 0:1], op0=ALU.subtract, op1=ALU.mult)
                v.tensor_copy(out=n0b[:], in_=n0f[:])
                v.tensor_scalar(out=sm[:, 4:5], in0=sm[:, 2:3], scalar1=EPS,
                                scalar2=None, op0=ALU.add)
                v.reciprocal(out=sm[:, 5:6], in_=sm[:, 4:5])
                v.tensor_tensor(out=sm[:, 5:6], in0=sm[:, 2:3], in1=sm[:, 5:6],
                                op=ALU.mult)
                v.tensor_scalar(out=sm[:, 5:6], in0=sm[:, 5:6], scalar1=EPS,
                                scalar2=None, op0=ALU.add)
                s.wait(v.mark())
                s.activation(out=rstd[:, 1:2], in_=sm[:, 5:6], func=AF.Sqrt)
                v.wait(s.mark())
                v.reciprocal(out=rstd[:, 1:2], in_=rstd[:, 1:2])
                v.tensor_scalar(out=scr[:], in0=n0f[:], scalar1=rstd[:, 1:2],
                                scalar2=None, op0=ALU.mult)
                v.tensor_tensor(out=scr[:], in0=scr[:], in1=g1_sb[:], op=ALU.mult)
                v.tensor_tensor(out=xn2[:], in0=scr[:], in1=b1_sb[:], op=ALU.add)
                m_v = v.mark()
                d.wait(m_v)
                ss = k.st_sems[it % 2]
                d.start(n0f_dram[it * 128:(it + 1) * 128, :], n0f[:], sem=ss)
                d.start(n0b_dram[it * 128:(it + 1) * 128, :], n0b[:], sem=ss)
                d.start(xn2_dram[it * 128:(it + 1) * 128, :], xn2[:], sem=ss)
                m_st[it % 2] = d.mark(ss)
                if it + 2 < ntiles:
                    d.wait(m_v)
                    d.start(xin[it % 2][:], x0[(it + 2) * 128:(it + 3) * 128, :],
                            sem=k.ld_sems[it % 2])
                    mk.append(d.mark(k.ld_sems[it % 2]))
            m_p1 = [d.mark(k.st_sems[0]), d.mark(k.st_sems[1])]

        es2w = ExitStack()
        wq_sb = k.sb("wq_sb", [128, 8 * D], BF16, es2w)
        wk_sb = k.sb("wk_sb", [128, 8 * D], BF16, es2w)
        wv_sb = k.sb("wv_sb", [128, 8 * D], BF16, es2w)
        guard_weights(k)
        load_cast(k, wq, wq_sb, 8, D)
        load_cast(k, wk, wk_sb, 8, D)
        load_cast(k, wv, wv_sb, 8, D)
        m_wqkv = gd.mark()
        t.wait(m_wqkv)
        t.wait(m_const)

        def mk_elu_epi():
            scr1 = eluscr

            def epi(k2, m_mm, psum_ap, out_ap, aux):
                v, s = k2.v, k2.s
                v.wait(m_mm)
                v.tensor_scalar(out=scr1[:], in0=psum_ap, scalar1=0.0, scalar2=None,
                                op0=ALU.min)
                s.wait(v.mark())
                s.activation(out=out_ap, in_=scr1[:], func=AF.Exp)
                v.wait(s.mark())
                v.tensor_scalar(out=scr1[:], in0=psum_ap, scalar1=0.0, scalar2=None,
                                op0=ALU.max)
                v.tensor_tensor(out=out_ap, in0=out_ap, in1=scr1[:], op=ALU.add)
                return k2.v.mark()

            return epi

        pass_gemm(k, xn2_dram, T, D, D, wq_sb, bias_sb=bias_q, out=fq_dram,
                  epilogue=mk_elu_epi(), src_mark=m_p1)
        pass_gemm(k, xn2_dram, T, D, D, wk_sb, bias_sb=bias_k, out=fk_dram,
                  epilogue=mk_elu_epi(), src_mark=m_p1)
        m_p2v = pass_gemm(k, xn2_dram, T, D, D, wv_sb, bias_sb=bias_v, out=v_dram,
                          src_mark=m_p1, out_dtype=BF16)
        m_p2 = stmarks(k)
        es2w.close()

        with ExitStack() as es3:
            fqi = [k.sb("p3fq", [128, D], BF16, es3) for _ in range(2)]
            fki = [k.sb("p3fk", [128, D], BF16, es3) for _ in range(2)]
            vvi = [k.sb("p3v", [128, D], BF16, es3) for _ in range(2)]
            FK = k.sb("p3FK", [64, 16 * 128], BF16, es3)
            FQ = k.sb("p3FQ", [64, 16 * 128], BF16, es3)
            VE = k.sb("p3VE", [64, 16 * 128], BF16, es3)
            Mmask = [k.sb("p3Mm", [128, 128], BF16, es3) for _ in range(2)]
            V8o = [k.sb("p3V8", [128, 72], BF16, es3) for _ in range(2)]
            r8 = k.sb("p3r8", [128, 1], F32, es3)
            attn = k.sb("p3attn", [128, 16 * 64], BF16, es3)
            EA = k.sb("p3EA", [64, 2048], BF16, es3)
            p_hT = k.ps(es3, "p3phT", [64, 2048], BF16)
            p_M = [k.ps(es3, "p3pM", [128, 128], F32) for _ in range(2)]
            p_v8 = [k.ps(es3, "p3pv8", [128, 64], BF16) for _ in range(2)]
            p_qkv = [k.ps(es3, "p3pqkv", [128, 72], F32) for _ in range(2)]
            p_eT = p_hT

            v.memset(V8o[0][:], 0.0)
            v.memset(V8o[1][:], 0.0)
            v.memset(V8o[0][:, 64:65], 1.0)
            v.memset(V8o[1][:, 64:65], 1.0)

            entry_st3 = stmarks(k)

            def load3(i):
                d.wait(m_p2)
                d.wait(entry_st3)
                sl = k.ld_sems[i % 2]
                d.start(fqi[i % 2][:], fq_dram[i * 128:(i + 1) * 128, :], sem=sl)
                d.start(fki[i % 2][:], fk_dram[i * 128:(i + 1) * 128, :], sem=sl)
                d.start(vvi[i % 2][:], v_dram[i * 128:(i + 1) * 128, :], sem=sl)
                return d.mark(sl)

            mk3 = [load3(0)]
            if ntiles > 1:
                mk3.append(load3(1))

            m_head_evac = None
            m_mask = [None, None]
            m_v8e = [None, None]
            m_qke = [None, None]
            m_ea_dma = None
            m_eT_prev = None
            for it in range(ntiles):
                t.wait(mk3[it])
                if m_ea_dma is not None:
                    t.wait(m_ea)
                for (srcb, dstb) in ((fki[it % 2], FK), (fqi[it % 2], FQ),
                                     (vvi[it % 2], VE)):
                    if m_head_evac is not None:
                        t.wait(m_head_evac)
                    for h in range(16):
                        t.transpose(p_hT[:, h * 128:(h + 1) * 128],
                                    srcb[:, h * 64:(h + 1) * 64], k.ident[:])
                    m_hT = t.mark()
                    v.wait(m_hT)
                    src_ap = p_hT[:].rearrange("p (h tk) -> p tk h", h=16)
                    dst_ap = dstb[:].rearrange("p (tk h) -> p tk h", h=16)
                    if it > 0:
                        v.wait(m_groups_done)
                    v.tensor_copy(out=dst_ap, in_=src_ap)
                    m_head_evac = v.mark()

                for gi in range(17):
                    pp = gi % 2
                    if gi < 16:
                        if m_mask[pp] is not None:
                            t.wait(m_mask[pp])
                        t.wait(m_head_evac)
                        t.matmul(p_M[pp][:], FK[:, gi * 128:(gi + 1) * 128],
                                 FQ[:, gi * 128:(gi + 1) * 128], start=True, stop=True)
                        if m_v8e[pp] is not None:
                            t.wait(m_v8e[pp])
                        t.transpose(p_v8[pp][:], VE[:, gi * 128:(gi + 1) * 128],
                                    k.ident[0:64, 0:64])
                        m_pe = t.mark()
                        v.wait(m_pe)
                        v.tensor_tensor(out=Mmask[pp][:], in0=p_M[pp][:], in1=bd_sb[:],
                                        op=ALU.mult)
                        m_mask[pp] = v.mark()
                        v.tensor_copy(out=V8o[pp][:, 0:64], in_=p_v8[pp][:])
                        m_v8e[pp] = v.mark()
                    if gi >= 1:
                        gj = gi - 1
                        qq = gj % 2
                        t.wait(m_mask[qq])
                        t.wait(m_v8e[qq])
                        if m_qke[qq] is not None:
                            t.wait(m_qke[qq])
                        t.matmul(p_qkv[qq][:], Mmask[qq][:], V8o[qq][:],
                                 start=True, stop=True)
                        m_qk = t.mark()
                        v.wait(m_qk)
                        if gj == 0 and m_eT_prev is not None:
                            v.wait(m_eT_prev)
                        v.tensor_scalar(out=r8[:], in0=p_qkv[qq][:, 64:65],
                                        scalar1=1e-6, scalar2=None, op0=ALU.add)
                        v.reciprocal(out=r8[:], in_=r8[:])
                        v.tensor_scalar(out=attn[:, gj * 64:(gj + 1) * 64],
                                        in0=p_qkv[qq][:, 0:64], scalar1=r8[:],
                                        scalar2=None, op0=ALU.mult)
                        m_qke[qq] = v.mark()
                m_groups_done = k.v.mark()

                t.wait(m_groups_done)
                if m_ea_dma is not None:
                    t.wait(m_ea)
                for gi2 in range(16):
                    t.transpose(p_eT[:, gi2 * 128:(gi2 + 1) * 128],
                                attn[:, gi2 * 64:(gi2 + 1) * 64], k.ident[:])
                m_eT = t.mark()
                m_eT_prev = m_eT
                v.wait(m_eT)
                if m_ea_dma is not None:
                    v.wait(m_ea_dma)
                src_ap = p_eT[:].rearrange("p (gg tk h) -> p h gg tk", gg=16, tk=8)
                dst_ap = EA[:].rearrange("p (h gg tk) -> p h gg tk", h=16, gg=16)
                v.tensor_copy(out=dst_ap, in_=src_ap)
                m_ea = v.mark()
                d.wait(m_ea)
                d.start(ea_dram[it * 64:(it + 1) * 64, :], EA[:], sem=k.st_sems[0])
                m_ea_dma = d.mark(k.st_sems[0])
                if it + 2 < ntiles:
                    t_ok = k.t.mark()
                    d.wait(t_ok)
                    mk3.append(load3(it + 2))
            m_p3 = k.d.mark(k.st_sems[0])

        es4w = ExitStack()
        woH_sb = k.sb("woH_sb", [64, 16 * D], BF16, es4w)
        guard_weights(k)
        gd.start(woH_sb[:], woH)
        m_wo = gd.mark()
        with ExitStack() as es4:
            eain = [k.sb("p4ea", [64, 2048], BF16, es4) for _ in range(2)]
            resb = [k.sb("p4res", [128, D], F32, es4) for _ in range(2)]
            osb = [k.sb("p4o", [128, D], F32, es4) for _ in range(2)]
            p_out = k.ps(es4, "p4pout", [128, D], F32)

            entry_st4 = stmarks(k)

            def load4(i):
                d.wait(entry_st4)
                d.wait(m_p1)
                sl = k.ld_sems[i % 2]
                d.start(eain[i % 2][:], ea_dram[i * 64:(i + 1) * 64, :], sem=sl)
                d.start(resb[i % 2][:], n0f_dram[i * 128:(i + 1) * 128, :], sem=sl)
                return d.mark(sl)

            mk4 = [load4(0)]
            if ntiles > 1:
                mk4.append(load4(1))
            t.wait(m_wo)
            m_ep = None
            m_odma = [None, None]
            for it in range(ntiles):
                t.wait(mk4[it])
                if m_ep is not None:
                    t.wait(m_ep)
                for h in range(16):
                    st = eain[it % 2][:, h * 128:(h + 1) * 128]
                    t.matmul(p_out[:, 0:512], st, woH_sb[:, h * D:h * D + 512],
                             start=(h == 0), stop=False)
                    t.matmul(p_out[:, 512:1024], st, woH_sb[:, h * D + 512:(h + 1) * D],
                             start=(h == 0), stop=False)
                t.matmul(p_out[:, 0:512], k.ones_r[0:1, :], bias_o[0:1, 0:512],
                         start=False, stop=True)
                t.matmul(p_out[:, 512:1024], k.ones_r[0:1, :], bias_o[0:1, 512:1024],
                         start=False, stop=True)
                m_mm = t.mark()
                v.wait(m_mm)
                if m_odma[it % 2] is not None:
                    v.wait(m_odma[it % 2])
                v.tensor_tensor(out=osb[it % 2][:], in0=p_out[:], in1=resb[it % 2][:],
                                op=ALU.add)
                m_ep = v.mark()
                d.wait(m_ep)
                d.start(a_dram[it * 128:(it + 1) * 128, :], osb[it % 2][:],
                        sem=k.st_sems[it % 2])
                m_odma[it % 2] = d.mark(k.st_sems[it % 2])
                if it + 2 < ntiles:
                    d.wait(m_mm)
                    d.wait(m_ep)
                    mk4.append(load4(it + 2))
            m_p4 = [k.d.mark(k.st_sems[0]), k.d.mark(k.st_sems[1])]
        es4w.close()

        with ExitStack() as es5:
            ain = [k.sb("p5a", [128, D], F32, es5) for _ in range(2)]
            t2 = k.sb("p5t2", [128, D], BF16, es5)
            scr = k.sb("p5scr", [128, D], F32, es5)
            sm = k.sb("p5sm", [128, 8], F32, es5)
            rstd = k.sb("p5rstd", [128, 1], F32, es5)
            entry_st5 = stmarks(k)
            d.wait(entry_st5)
            d.start(ain[0][:], a_dram[0:128, :], sem=k.ld_sems[0])
            mk5 = [d.mark(k.ld_sems[0])]
            if ntiles > 1:
                d.start(ain[1][:], a_dram[128:256, :], sem=k.ld_sems[1])
                mk5.append(d.mark(k.ld_sems[1]))
            m_st = [None, None]
            for it in range(ntiles):
                xs = ain[it % 2]
                v.wait(mk5[it])
                if it >= 1 and m_st[(it - 1) % 2] is not None:
                    v.wait(m_st[(it - 1) % 2])
                    s.wait(m_st[(it - 1) % 2])
                emit_ln_core(k, xs[:], D, sm, scr[:], rstd[:])
                v.tensor_scalar(out=scr[:], in0=xs[:], scalar1=sm[:, 0:1],
                                scalar2=rstd[:], op0=ALU.subtract, op1=ALU.mult)
                v.tensor_tensor(out=scr[:], in0=scr[:], in1=g2_sb[:], op=ALU.mult)
                v.tensor_tensor(out=t2[:], in0=scr[:], in1=b2_sb[:], op=ALU.add)
                m_v = v.mark()
                d.wait(m_v)
                d.start(t2_dram[it * 128:(it + 1) * 128, :], t2[:],
                        sem=k.st_sems[it % 2])
                m_st[it % 2] = d.mark(k.st_sems[it % 2])
                if it + 2 < ntiles:
                    d.wait(m_v)
                    d.start(ain[it % 2][:], a_dram[(it + 2) * 128:(it + 3) * 128, :],
                            sem=k.ld_sems[it % 2])
                    mk5.append(d.mark(k.ld_sems[it % 2]))
            m_p5 = [d.mark(k.st_sems[0]), d.mark(k.st_sems[1])]

        es6w = ExitStack()
        fc_sb = k.sb("fc_sb", [128, 8 * MLP0], BF16, es6w)
        guard_weights(k)
        load_cast(k, fcw, fc_sb, 8, MLP0)
        m_wfc = gd.mark()
        t.wait(m_wfc)

        def gelu_epi(k2, m_mm, psum_ap, out_ap, aux):
            k2.s.wait(m_mm)
            k2.s.activation(out=out_ap, in_=psum_ap, func=AF.Gelu)
            return k2.s.mark()

        for half in range(2):
            pass_gemm(k, t2_dram, T, D, 2048, fc_sb, wcol0=half * 2048, wstride=MLP0,
                      bias_sb=bias_fc, bias_col0=half * 2048,
                      out=gg_dram[:, half * 2048:half * 2048 + 2048],
                      epilogue=gelu_epi, src_mark=m_p5)
        m_p6 = stmarks(k)
        es6w.close()

        es7w = ExitStack()
        pj_sb = k.sb("pj_sb", [128, 32 * D], BF16, es7w)
        guard_weights(k)
        load_cast(k, projw, pj_sb, 32, D)
        m_wpj = gd.mark()
        t.wait(m_wpj)
        pass_gemm(k, gg_dram, T, MLP0, D, pj_sb, bias_sb=bias_pj, out=brA_dram,
                  out_dtype=F32, res_srcs=[(a_dram, F32)], src_mark=m_p6)
        m_p7 = stmarks(k)
        es7w.close()

        es8w = ExitStack()
        wg_sb = k.sb("wg_sb", [128, 8 * FFN0], BF16, es8w)
        wu_sb = k.sb("wu_sb", [128, 8 * FFN0], BF16, es8w)
        wint = k.sb("wint", [128, 2 * FFN0], I32, es8w)
        guard_weights(k)
        load_dequant(k, wgT, wg_sb, 8, FFN0, wint)
        m_wg = k.v.mark()
        load_dequant(k, wuT, wu_sb, 8, FFN0, wint)
        m_wu = k.v.mark()
        t.wait(m_wg)
        t.wait(m_wu)

        def silu_epi(k2, m_mm, psum_ap, out_ap, aux):
            k2.s.wait(m_mm)
            k2.s.activation(out=out_ap, in_=psum_ap, func=AF.Silu)
            return k2.s.mark()

        for half in range(2):
            pass_gemm(k, n0b_dram, T, D, 1536, wg_sb, wcol0=half * 1536, wstride=FFN0,
                      out=sg_dram[:, half * 1536:half * 1536 + 1536],
                      epilogue=silu_epi, src_mark=m_p1)
        m_p8a = stmarks(k)

        def upmul_epi(k2, m_mm, psum_ap, out_ap, aux):
            k2.v.wait(m_mm)
            k2.v.tensor_tensor(out=out_ap, in0=psum_ap, in1=aux["res"][0], op=ALU.mult)
            return k2.v.mark()

        for half in range(2):
            pass_gemm(k, n0b_dram, T, D, 1536, wu_sb, wcol0=half * 1536, wstride=FFN0,
                      out=gu_dram[:, half * 1536:half * 1536 + 1536],
                      epilogue=upmul_epi,
                      res_srcs=[(sg_dram[:, half * 1536:half * 1536 + 1536], BF16)],
                      src_mark=m_p8a)
        m_p8 = stmarks(k)
        es8w.close()

        es9w = ExitStack()
        wd_sb = k.sb("wd_sb", [128, 24 * D], BF16, es9w)
        wint9 = k.sb("wint9", [128, 2 * D], I32, es9w)
        guard_weights(k)
        load_dequant(k, wdT, wd_sb, 24, D, wint9)
        m_wd = k.v.mark()
        t.wait(m_wd)

        def x1n1_epi(k2, m_mm, psum_ap, out_ap, aux):
            v, s, d = k2.v, k2.s, k2.d
            it = aux["tile"]
            v.wait(m_mm)
            v.tensor_tensor(out=out_ap, in0=psum_ap, in1=aux["res"][0], op=ALU.add)
            v.tensor_tensor(out=out_ap, in0=out_ap, in1=aux["res"][1], op=ALU.add)
            if ln9["m_n1dma"][it % 2] is not None:
                v.wait(ln9["m_n1dma"][it % 2])
                s.wait(ln9["m_n1dma"][it % 2])
            emit_ln_core(k2, out_ap, D, ln9["sm"], ln9["scr"][:], ln9["rstd"][:])
            v.tensor_scalar(out=ln9["n1b"][it % 2][:], in0=out_ap,
                            scalar1=ln9["sm"][:, 0:1], scalar2=ln9["rstd"][:],
                            op0=ALU.subtract, op1=ALU.mult)
            m_v = k2.v.mark()
            d.wait(m_v)
            d.start(o_n1[it * 128:(it + 1) * 128, :], ln9["n1b"][it % 2][:],
                    sem=k2.st2_sems[it % 2])
            ln9["m_n1dma"][it % 2] = k2.d.mark(k2.st2_sems[it % 2])
            return m_v

        pass_gemm(k, gu_dram, T, FFN0, D, wd_sb, out=o_x1, out_dtype=F32,
                  epilogue=x1n1_epi,
                  res_srcs=[(brA_dram, F32), (x0, F32)], src_mark=m_p8)
        m_p9 = stmarks(k)
        es9w.close()

        gate_sb = k.sb("gate_sbuf", [128, 8 * 65], F32)
        guard_weights(k)
        load_cast(k, gatew, gate_sb, 8, 65)
        m_gw = gd.mark()
        t.wait(m_gw)
        with ExitStack() as es11:
            nin = [k.sb("p11n", [128, D], F32, es11) for _ in range(2)]
            nT = k.sb("p11nT", [128, D], F32, es11)
            n1f11 = k.sb("p11n1f", [128, D], F32, es11)
            scr11 = k.sb("p11scr", [128, D], F32, es11)
            sm11b = k.sb("p11smb", [128, 8], F32, es11)
            rstd11 = k.sb("p11rstd", [128, 1], F32, es11)
            probs = k.sb("p11probs", [128, 64], F32, es11)
            pm8 = k.sb("p11pm8", [128, 8], F32, es11)
            idx8s = [k.sb("p11idx", [128, 8], F32, es11) for _ in range(2)]
            w8s = [k.sb("p11w8", [128, 8], F32, es11) for _ in range(2)]
            used = k.sb("p11used", [128, 64], F32, es11)
            cand = k.sb("p11cand", [128, 64], F32, es11)
            eqm = k.sb("p11eqm", [128, 64], F32, es11)
            ohe = k.sb("p11ohe", [128, 8 * 64], F32, es11)
            ohr = k.sb("p11ohr", [128, 64], F32, es11)
            counts_sb = k.sb("p11cnt", [1, 64], F32, es11)
            sm11 = k.sb("p11sm", [128, 4], F32, es11)
            onec = k.sb("p11onec", [128, 1], F32, es11)
            p_tp11 = k.ps(es11, "p11ptp", [128, 1024], F32)
            p_gate = k.ps(es11, "p11pg", [128, 65], F32)
            sgt = [k.sb("p11sgt", [128, 1], F32, es11) for _ in range(2)]
            p_cnt = k.ps(es11, "p11pc", [1, 64], F32)
            v.memset(onec[:], 1.0)

            entry_st11 = stmarks(k)

            def load11(i):
                d.wait(entry_st11)
                d.wait(m_p9)
                sl = k.ld_sems[i % 2]
                d.start(nin[i % 2][:], o_x1[i * 128:(i + 1) * 128, :], sem=sl)
                return d.mark(sl)

            mk11 = [load11(0)]
            if ntiles > 1:
                mk11.append(load11(1))
            m_xT = None
            m_gmm_prev = None
            m_soft_prev = None
            m_odma = [None, None]
            m_oh_prev = None
            identf = k.sb("p11identf", [128, 128], F32, es11)
            v.wait(k.m_ident)
            v.tensor_copy(out=identf[:], in_=k.ident[:])
            m_idf11 = k.v.mark()
            for it in range(ntiles):
                v.wait(mk11[it])
                if m_xT is not None:
                    v.wait(m_xT)
                emit_ln_core(k, nin[it % 2][:], D, sm11b, scr11[:], rstd11[:])
                v.tensor_scalar(out=n1f11[:], in0=nin[it % 2][:],
                                scalar1=sm11b[:, 0:1], scalar2=rstd11[:],
                                op0=ALU.subtract, op1=ALU.mult)
                m_n1f = v.mark()
                t.wait(m_n1f)
                if m_xT is not None:
                    t.wait(m_xT)
                for c in range(8):
                    t.transpose(p_tp11[:, c * 128:(c + 1) * 128],
                                n1f11[:, c * 128:(c + 1) * 128], identf[:])
                m_tp = t.mark()
                v.wait(m_tp)
                if m_gmm_prev is not None:
                    v.wait(m_gmm_prev)
                v.tensor_copy(out=nT[:], in_=p_tp11[:])
                m_xT = v.mark()
                t.wait(m_xT)
                if m_soft_prev is not None:
                    t.wait(m_soft_prev)
                for c in range(8):
                    t.matmul(p_gate[:], nT[:, c * 128:(c + 1) * 128],
                             gate_sb[:, c * 65:(c + 1) * 65],
                             start=(c == 0), stop=(c == 7))
                m_gmm = t.mark()
                m_gmm_prev = m_gmm
                v.wait(m_gmm)
                v.tensor_reduce(out=sm11[:, 0:1], in_=p_gate[:, 0:64], axis=AX.X,
                                op=ALU.max, negate=True)
                s.wait(v.mark())
                if m_odma[it % 2] is not None:
                    s.wait(m_odma[it % 2])
                s.activation(out=probs[:], in_=p_gate[:, 0:64], func=AF.Exp,
                             bias=sm11[:, 0:1])
                s.activation(out=sgt[it % 2][:], in_=p_gate[:, 64:65], func=AF.Sigmoid)
                m_soft = s.mark()
                m_soft_prev = m_soft
                v.wait(m_soft)
                v.tensor_reduce(out=sm11[:, 1:2], in_=probs[:], axis=AX.X, op=ALU.add)
                v.reciprocal(out=sm11[:, 1:2], in_=sm11[:, 1:2])
                v.tensor_scalar(out=probs[:], in0=probs[:], scalar1=sm11[:, 1:2],
                                scalar2=None, op0=ALU.mult)
                if m_odma[it % 2] is not None:
                    v.wait(m_odma[it % 2])
                v.max(out=pm8[:], in_=probs[:])
                v.memset(used[:], 0.0)
                for j in range(TOPK):
                    v.tensor_scalar(out=eqm[:], in0=probs[:], scalar1=pm8[:, j:j + 1],
                                    scalar2=None, op0=ALU.is_equal)
                    v.scalar_tensor_tensor(out=cand[:], in0=eqm[:], scalar=-65536.0,
                                           in1=iota_sb[:], op0=ALU.mult, op1=ALU.add)
                    v.tensor_scalar(out=cand[:], in0=cand[:], scalar1=65536.0,
                                    scalar2=None, op0=ALU.add)
                    v.scalar_tensor_tensor(out=cand[:], in0=used[:], scalar=65536.0,
                                           in1=cand[:], op0=ALU.mult, op1=ALU.add)
                    v.tensor_reduce(out=idx8s[it % 2][:, j:j + 1], in_=cand[:],
                                    axis=AX.X, op=ALU.min)
                    v.tensor_scalar(out=eqm[:], in0=iota_sb[:],
                                    scalar1=idx8s[it % 2][:, j:j + 1],
                                    scalar2=None, op0=ALU.is_equal)
                    v.tensor_tensor(out=used[:], in0=used[:], in1=eqm[:], op=ALU.max)
                v.tensor_reduce(out=sm11[:, 2:3], in_=pm8[:], axis=AX.X, op=ALU.add)
                v.reciprocal(out=sm11[:, 2:3], in_=sm11[:, 2:3])
                v.tensor_scalar(out=w8s[it % 2][:], in0=pm8[:], scalar1=sm11[:, 2:3],
                                scalar2=None, op0=ALU.mult)
                m_top8 = v.mark()
                d.wait(m_top8)
                d.start(o_idx8[it * 128:(it + 1) * 128, :], idx8s[it % 2][:],
                        sem=k.st_sems[it % 2])
                d.start(o_w8[it * 128:(it + 1) * 128, :], w8s[it % 2][:],
                        sem=k.st_sems[it % 2])
                d.wait(m_soft)
                d.start(o_sgate[it * 128:(it + 1) * 128, :], sgt[it % 2][:],
                        sem=k.st_sems[it % 2])
                m_odma[it % 2] = d.mark(k.st_sems[it % 2])
                if m_oh_prev is not None:
                    v.wait(m_oh_prev)
                ioh = iota_sb[:].unsqueeze(1).to_broadcast([128, 8, 64])
                idq = idx8s[it % 2][:].unsqueeze(2).to_broadcast([128, 8, 64])
                v.tensor_tensor(out=ohe[:].rearrange("p (j e) -> p j e", j=8),
                                in0=idq, in1=ioh, op=ALU.is_equal)
                v.tensor_reduce(out=ohr[:],
                                in_=ohe[:].rearrange("p (j e) -> p e j", j=8),
                                axis=AX.X, op=ALU.add)
                m_oh = v.mark()
                t.wait(m_oh)
                t.matmul(p_cnt[:], onec[:, 0:1], ohr[:], start=(it == 0),
                         stop=(it == ntiles - 1))
                m_oh_prev = t.mark()
                if it + 2 < ntiles:
                    d.wait(m_xT)
                    mk11.append(load11(it + 2))
            v.wait(m_oh_prev)
            v.tensor_copy(out=counts_sb[:], in_=p_cnt[:])
            m_cfin = v.mark()
            d.wait(m_cfin)
            d.start(o_counts, counts_sb[:])

        k.finalize()
    return k



def _bf(x):
    return np.asarray(x, dtype=bfdt)


def prep_A_inputs(core, full):
    b, half = core // 2, core % 2
    x0 = np.ascontiguousarray(full["L0_attn_q_x"][b, half * TPC:(half + 1) * TPC, :])
    eye = np.eye(128, dtype=np.float32)
    bdm = np.kron(np.eye(8, dtype=np.float32), np.ones((16, 16), np.float32))
    iota = np.broadcast_to(np.arange(64, dtype=np.float32), (128, 64)).copy()
    woH = np.ascontiguousarray(
        full["L0_attn_wo"].reshape(16, 64, D).transpose(1, 0, 2).reshape(64, 16 * D))
    m = {
        "x0": x0,
        "wq": full["L0_attn_wq"], "wk": full["L0_attn_wk"], "wv": full["L0_attn_wv"],
        "woH": woH,
        "fcw": full["L0_attn_fc_w"],
        "projw": full["L0_attn_proj_w"],
        "wgT": np.ascontiguousarray(full["L0_ffn_w_gate"].T),
        "wuT": np.ascontiguousarray(full["L0_ffn_w_up"].T),
        "wdT": np.ascontiguousarray(full["L0_ffn_w_down"].T),
        "g1b": np.broadcast_to(full["L0_attn_ln1_w"], (128, D)).copy(),
        "b1b": np.broadcast_to(full["L0_attn_ln1_b"], (128, D)).copy(),
        "g2b": np.broadcast_to(full["L0_attn_ln2_w"], (128, D)).copy(),
        "b2b": np.broadcast_to(full["L0_attn_ln2_b"], (128, D)).copy(),
        "bq_r": _bf(full["L0_attn_bq"].reshape(1, D)),
        "bk_r": _bf(full["L0_attn_bk"].reshape(1, D)),
        "bv_r": _bf(full["L0_attn_bv"].reshape(1, D)),
        "bo_r": _bf(full["L0_attn_bo"].reshape(1, D)),
        "fcb_r": _bf(full["L0_attn_fc_b"].reshape(1, MLP0)),
        "projb_r": _bf(full["L0_attn_proj_b"].reshape(1, D)),
        "gatew": np.ascontiguousarray(np.concatenate(
            [full["L1_ffn_gate_w"], full["L1_ffn_shared_gate_w"]], axis=1)),
        "identb": _bf(eye),
        "bdm": bdm,
        "iota64": iota,
    }
    return m


def prep_C_inputs(core, full, a_outs, b_outs):
    c = core
    eye = np.eye(128, dtype=np.float32)
    return {
        "dw1_all": np.concatenate([np.asarray(b_outs[r]["dw1"], np.float32)
                                   for r in range(NCORES)], axis=0),
        "dw2T_all": np.concatenate([np.asarray(b_outs[r]["dw2T"], np.float32)
                                    for r in range(NCORES)], axis=0),
        "db1_all": np.concatenate([np.asarray(b_outs[r]["db1"], np.float32)
                                   for r in range(NCORES)], axis=1),
        "db2_all": np.concatenate([np.asarray(b_outs[r]["db2"], np.float32)
                                   for r in range(NCORES)], axis=1),
        "w1_in": full["L1_attn_w1"],
        "w2T_in": np.ascontiguousarray(full["L1_attn_w2"].T),
        "b1_row": full["L1_attn_b1"].reshape(1, -1),
        "b2_row": full["L1_attn_b2"].reshape(1, -1),
        "alpha128": np.broadcast_to(np.float32(full["L1_attn_alpha"]),
                                    (128, 1)).copy(),
        "theta128": np.broadcast_to(np.float32(full["L1_attn_theta"]),
                                    (128, 1)).copy(),
        "lin1T": np.ascontiguousarray(full["L1_attn_lin1"].T),
        "lin2T": np.ascontiguousarray(full["L1_attn_lin2"].T),
        "n1own_c": np.asarray(a_outs[c]["n1"]),
        "swa_own": np.asarray(b_outs[c]["swa_full"], np.float32)[64:64 + TPC],
        "tokT_all": np.concatenate([np.asarray(b_outs[r]["tokT"], np.float32)
                                    for r in range(NCORES)], axis=0),
        "wsc_all": np.concatenate([np.asarray(b_outs[r]["wscT"], np.float32)
                                   for r in range(NCORES)], axis=0),
        "xn_all": np.concatenate([np.asarray(a_outs[r]["n1"])
                                  for r in range(NCORES)], axis=0),
        "egu_c": np.ascontiguousarray(
            full["L1_ffn_expert_gate_up_w"][8 * c:8 * c + 8].reshape(8 * D, 2 * I_EXP)),
        "edn_c": np.ascontiguousarray(
            full["L1_ffn_expert_down_w"][8 * c:8 * c + 8].reshape(8 * I_EXP, D)),
        "ebase8": (8.0 * c + np.arange(8, dtype=np.float32)).reshape(8, 1),
        "identb3": _bf(eye),
    }


def prep_D_inputs(core, a_outs, b_outs, c_outs):
    c = core
    return {
        "outtab_all": np.concatenate([np.asarray(c_outs[r]["outtab"], np.float32)
                                      for r in range(NCORES)], axis=0),
        "offs_in": np.asarray(b_outs[c]["offs"], np.float32),
        "x1_in": np.asarray(a_outs[c]["x1"], np.float32),
        "merged_in": np.asarray(c_outs[c]["merged"], np.float32),
        "sh_in": np.asarray(b_outs[c]["sh"], np.float32),
    }


def kernel(**inputs):
    full = inputs
    in_maps = []
    for c in range(NCORES):
        m = prep_A_inputs(c, full)
        b, half = c // 2, c % 2
        m["x0"] = np.ascontiguousarray(full["L0_attn_q_x"][b, half * TPC:(half + 1) * TPC])
        in_maps.append(m)
    ka = build_launch_A()
    resA = run_bass_kernel_spmd(ka.nc, in_maps, core_ids=list(range(NCORES)))
    a_outs = resA.results
    in_maps = [prep_B_inputs(c, full, a_outs) for c in range(NCORES)]
    kb = build_launch_B()
    resB = run_bass_kernel_spmd(kb.nc, in_maps, core_ids=list(range(NCORES)))
    b_outs = resB.results
    in_maps = [prep_C_inputs(c, full, a_outs, b_outs) for c in range(NCORES)]
    kc = build_launch_C()
    resC = run_bass_kernel_spmd(kc.nc, in_maps, core_ids=list(range(NCORES)))
    c_outs = resC.results
    in_maps = [prep_D_inputs(c, a_outs, b_outs, c_outs) for c in range(NCORES)]
    kd = build_launch_D()
    resD = run_bass_kernel_spmd(kd.nc, in_maps, core_ids=list(range(NCORES)))
    out = np.empty((B, S, D), dtype=np.float32)
    for c in range(NCORES):
        b, half = c // 2, c % 2
        out[b, half * TPC:(half + 1) * TPC] = np.asarray(resD.results[c]["x2"],
                                                         np.float32)
    return out


if __name__ == "__main__":
    print("module; use driver")


def pass_tokcontract(k, stat_src, Ks, mov_src, N, out, T, src_mark=None, group=4):
    t, v, d = k.t, k.v, k.d
    assert N == 1024
    nch = Ks // 128
    ntiles = T // 128
    es = ExitStack()
    with es:
        statb = [k.sb("tcs", [128, group * 128], BF16, es) for _ in range(2)]
        movb = [k.sb("tcm", [128, N], BF16, es) for _ in range(2)]
        osb = k.sb("tco", [128, N], F32, es)
        ps = [k.ps(es, "tcp", [128, N], F32) for _ in range(group)]
        entry_st = stmarks(k)
        m_evac = None
        for cg in range(0, nch, group):
            gn = min(group, nch - cg)

            def load_t(i):
                if src_mark is not None:
                    d.wait(src_mark)
                d.wait(entry_st)
                sl = k.ld_sems[i % 2]
                d.start(statb[i % 2][:, 0:gn * 128],
                        stat_src[i * 128:(i + 1) * 128, cg * 128:(cg + gn) * 128],
                        sem=sl)
                d.start(movb[i % 2][:], mov_src[i * 128:(i + 1) * 128, :], sem=sl)
                return d.mark(sl)

            mks = [load_t(0)]
            if ntiles > 1:
                mks.append(load_t(1))
            for it in range(ntiles):
                t.wait(mks[it])
                if it == 0 and m_evac is not None:
                    t.wait(m_evac)
                for c in range(gn):
                    st_ = statb[it % 2][:, c * 128:(c + 1) * 128]
                    t.matmul(ps[c][:, 0:512], st_, movb[it % 2][:, 0:512],
                             start=(it == 0), stop=(it == ntiles - 1))
                    t.matmul(ps[c][:, 512:1024], st_, movb[it % 2][:, 512:1024],
                             start=(it == 0), stop=(it == ntiles - 1))
                m_mm = t.mark()
                if it + 2 < ntiles:
                    d.wait(m_mm)
                    mks.append(load_t(it + 2))
            m_mmf = t.mark()
            for c in range(gn):
                v.wait(m_mmf)
                v.tensor_copy(out=osb[:], in_=ps[c][:])
                m_e = k.v.mark()
                d.wait(m_e)
                d.start(out[(cg + c) * 128:(cg + c + 1) * 128, :], osb[:],
                        sem=k.st_sems[0])
                m_evac = k.d.mark(k.st_sems[0])
                v.wait(m_evac)
    return k.d.mark(k.st_sems[0])


def pass_colsum(k, src, N, out, T, src_mark=None):
    t, v, d = k.t, k.v, k.d
    nj = N // 512
    ntiles = T // 128
    es = ExitStack()
    with es:
        movb = [k.sb("csm", [128, N], BF16, es) for _ in range(2)]
        onesc = k.sb("csone", [128, 1], BF16, es)
        osb = k.sb("cso", [1, N], F32, es)
        ps = [k.ps(es, "csp", [1, 512], F32) for _ in range(nj)]
        v.memset(onesc[:], 1.0)
        entry_st = stmarks(k)

        def load_t(i):
            if src_mark is not None:
                d.wait(src_mark)
            d.wait(entry_st)
            sl = k.ld_sems[i % 2]
            d.start(movb[i % 2][:], src[i * 128:(i + 1) * 128, :], sem=sl)
            return d.mark(sl)

        mks = [load_t(0)]
        if ntiles > 1:
            mks.append(load_t(1))
        t.wait(k.v.mark())
        for it in range(ntiles):
            t.wait(mks[it])
            for j in range(nj):
                t.matmul(ps[j][:], onesc[:, 0:1], movb[it % 2][:, j * 512:(j + 1) * 512],
                         start=(it == 0), stop=(it == ntiles - 1))
            m_mm = t.mark()
            if it + 2 < ntiles:
                d.wait(m_mm)
                mks.append(load_t(it + 2))
        m_f = t.mark()
        v.wait(m_f)
        for j in range(nj):
            v.tensor_copy(out=osb[0:1, j * 512:(j + 1) * 512], in_=ps[j][:])
        m_e = k.v.mark()
        d.wait(m_e)
        d.start(out, osb[:], sem=k.st_sems[0])
    return k.d.mark(k.st_sems[0])


def _moe_dispatch(k, To, NTO, idx8_in, w8_in, counts_all, coremask, tokbase,
                  iota64b, iota512b, lexc, o_tokT, o_wscT, o_offs):
    t, v, s, d = k.t, k.v, k.s, k.d
    es = ExitStack()
    with es:
        idxb = [k.sb("mdidx", [128, 8], F32, es) for _ in range(2)]
        w8b = [k.sb("mdw8", [128, 8], F32, es) for _ in range(2)]
        offsb = [k.sb("mdoffs", [128, 8], F32, es) for _ in range(2)]
        cnts = k.sb("mdcnt", [1, 512], F32, es)
        cmsk = k.sb("mdcm", [1, 512], F32, es)
        base = k.sb("mdbase", [1, 64], F32, es)
        onesf = k.sb("mdones", [1, 128], F32, es)
        onescol = k.sb("mdonec", [128, 1], F32, es)
        runb = k.sb("mdrun", [128, 64], F32, es)
        iota64s = k.sb("mdio64", [128, 64], F32, es)
        iota512s = k.sb("mdio512", [128, 512], F32, es)
        lexcs = k.sb("mdlexc", [128, 128], F32, es)
        tokb = k.sb("mdtokb", [128, 1], F32, es)
        tokid = k.sb("mdtokid", [128, 1], F32, es)
        OH = k.sb("mdOH", [128, 512], F32, es)
        jexc = k.sb("mdjexc", [128, 512], F32, es)
        pallj = k.sb("mdpallj", [128, 64], F32, es)
        posfull = k.sb("mdposf", [128, 512], F32, es)
        posj = k.sb("mdposj", [128, 8], F32, es)
        slot = k.sb("mdslot", [128, 8], F32, es)
        keep = k.sb("mdkeep", [128, 8], F32, es)
        w8k = k.sb("mdw8k", [128, 8], F32, es)
        valT = k.sb("mdvalT", [128, 64], F32, es)
        valW = k.sb("mdvalW", [128, 64], F32, es)
        OHS = k.sb("mdOHS", [128, 512], F32, es)
        tokTo = k.sb("mdtokTo", [64, 512], F32, es)
        wscTo = k.sb("mdwscTo", [64, 512], F32, es)
        tcnt = k.sb("mdtcnt", [1, 64], F32, es)
        p_pre = k.ps(es, "mdppre", [128, 512], F32)
        p_bc = k.ps(es, "mdpbc", [128, 64], F32)
        p_tc = k.ps(es, "mdptc", [1, 64], F32)
        p_tok = k.ps(es, "mdptok", [64, 512], F32)
        p_wsc = k.ps(es, "mdpwsc", [64, 512], F32)

        entry_md = stmarks(k)
        d.wait(entry_md)
        d.start(cnts[:], counts_all)
        d.start(cmsk[:], coremask)
        d.start(iota64s[:], iota64b)
        d.start(iota512s[:], iota512b)
        d.start(lexcs[:], lexc)
        d.start(tokb[:], tokbase)
        m_l0 = k.d.mark()
        v.wait(m_l0)
        v.memset(onesf[:], 1.0)
        v.memset(onescol[:], 1.0)
        v.tensor_tensor(out=cnts[:], in0=cnts[:], in1=cmsk[:], op=ALU.mult)
        v.tensor_reduce(out=base[:],
                        in_=cnts[:].rearrange("p (r e) -> p e r", r=8),
                        axis=AX.X, op=ALU.add)
        m_base = k.v.mark()
        t.wait(m_base)
        t.matmul(p_bc[:], onesf[0:1, :], base[0:1, :], start=True, stop=True)
        m_bc = t.mark()
        v.wait(m_bc)
        v.tensor_copy(out=runb[:], in_=p_bc[:])
        m_run = k.v.mark()

        m_scat_prev = None
        m_offs_dma = [None, None]
        for it in range(NTO):
            sl = k.ld_sems[it % 2]
            if it < 2:
                d.wait(entry_md)
            d.start(idxb[it % 2][:], idx8_in[it * 128:(it + 1) * 128, :], sem=sl)
            d.start(w8b[it % 2][:], w8_in[it * 128:(it + 1) * 128, :], sem=sl)
            m_li = d.mark(sl)
            v.wait(m_li)
            idq = idxb[it % 2][:].unsqueeze(2).to_broadcast([128, 8, 64])
            ioh = iota64s[:].unsqueeze(1).to_broadcast([128, 8, 64])
            v.tensor_tensor(out=OH[:].rearrange("p (j e) -> p j e", j=8),
                            in0=idq, in1=ioh, op=ALU.is_equal)
            m_oh = k.v.mark()
            t.wait(m_oh)
            if m_scat_prev is not None:
                t.wait(m_scat_prev)
            t.matmul(p_pre[:], lexcs[:], OH[:], start=True, stop=True)
            m_pre = t.mark()
            v.wait(m_pre)
            v.tensor_reduce(out=pallj[:],
                            in_=p_pre[:].rearrange("p (j e) -> p e j", j=8),
                            axis=AX.X, op=ALU.add)
            v.memset(jexc[:, 0:64], 0.0)
            for j in range(1, 8):
                v.tensor_tensor(out=jexc[:, j * 64:(j + 1) * 64],
                                in0=jexc[:, (j - 1) * 64:j * 64],
                                in1=OH[:, (j - 1) * 64:j * 64], op=ALU.add)
            pallj_b = pallj[:].unsqueeze(1).to_broadcast([128, 8, 64])
            runb_b = runb[:].unsqueeze(1).to_broadcast([128, 8, 64])
            v.tensor_tensor(out=posfull[:].rearrange("p (j e) -> p j e", j=8),
                            in0=jexc[:].rearrange("p (j e) -> p j e", j=8),
                            in1=pallj_b, op=ALU.add)
            v.tensor_tensor(out=posfull[:].rearrange("p (j e) -> p j e", j=8),
                            in0=posfull[:].rearrange("p (j e) -> p j e", j=8),
                            in1=runb_b, op=ALU.add)
            v.tensor_tensor(out=posfull[:], in0=posfull[:], in1=OH[:], op=ALU.mult)
            v.tensor_reduce(out=posj[:],
                            in_=posfull[:].rearrange("p (j e) -> p j e", j=8),
                            axis=AX.X, op=ALU.add)
            v.tensor_scalar(out=slot[:], in0=posj[:], scalar1=float(CAP),
                            scalar2=None, op0=ALU.min)
            v.tensor_scalar(out=keep[:], in0=posj[:], scalar1=float(CAP),
                            scalar2=None, op0=ALU.is_lt)
            if m_offs_dma[it % 2] is not None:
                v.wait(m_offs_dma[it % 2])
            v.tensor_scalar(out=offsb[it % 2][:], in0=idxb[it % 2][:],
                            scalar1=float(CAP), scalar2=None, op0=ALU.mult)
            v.tensor_tensor(out=offsb[it % 2][:], in0=offsb[it % 2][:], in1=slot[:],
                            op=ALU.add)
            v.tensor_scalar(out=w8k[:], in0=keep[:], scalar1=-1.0, scalar2=1.0,
                            op0=ALU.mult, op1=ALU.add)
            v.scalar_tensor_tensor(out=offsb[it % 2][:], in0=w8k[:],
                                   scalar=float(2 ** 30), in1=offsb[it % 2][:],
                                   op0=ALU.mult, op1=ALU.add)
            v.tensor_tensor(out=w8k[:], in0=w8b[it % 2][:], in1=keep[:], op=ALU.mult)
            v.tensor_scalar(out=tokid[:], in0=tokb[:], scalar1=float(it * 128),
                            scalar2=None, op0=ALU.add)
            m_pp = k.v.mark()
            d.wait(m_pp)
            d.start(o_offs[it * 128:(it + 1) * 128, :], offsb[it % 2][:],
                    sem=k.st_sems[it % 2])
            m_offs_dma[it % 2] = k.d.mark(k.st_sems[it % 2])
            v.tensor_reduce(out=OH[:, 0:64],
                            in_=OH[:].rearrange("p (j e) -> p e j", j=8),
                            axis=AX.X, op=ALU.add)
            m_ohr = k.v.mark()
            t.wait(m_ohr)
            t.matmul(p_tc[:], onescol[:, 0:1], OH[:, 0:64], start=True, stop=True)
            m_tc = t.mark()
            v.wait(m_tc)
            v.tensor_copy(out=tcnt[:], in_=p_tc[:])
            m_tcc = k.v.mark()
            t.wait(m_tcc)
            t.matmul(p_bc[:], onesf[0:1, :], tcnt[0:1, :], start=True, stop=True)
            m_bc2 = t.mark()
            v.wait(m_bc2)
            v.tensor_tensor(out=runb[:], in0=runb[:], in1=p_bc[:], op=ALU.add)
            v.tensor_tensor(out=OH[:].rearrange("p (j e) -> p j e", j=8),
                            in0=idq, in1=ioh, op=ALU.is_equal)
            m_oh2 = k.v.mark()
            for j in range(8):
                v.tensor_scalar(out=valT[:], in0=OH[:, j * 64:(j + 1) * 64],
                                scalar1=tokid[:], scalar2=None, op0=ALU.mult)
                v.tensor_scalar(out=valW[:], in0=OH[:, j * 64:(j + 1) * 64],
                                scalar1=w8k[:, j:j + 1], scalar2=None, op0=ALU.mult)
                v.tensor_scalar(out=OHS[:], in0=iota512s[:], scalar1=slot[:, j:j + 1],
                                scalar2=None, op0=ALU.is_equal)
                m_val = k.v.mark()
                t.wait(m_val)
                first = (it == 0 and j == 0)
                last = (it == NTO - 1 and j == 7)
                t.matmul(p_tok[:], valT[:], OHS[:], start=first, stop=last)
                t.matmul(p_wsc[:], valW[:], OHS[:], start=first, stop=last)
                m_scat_prev = t.mark()
                if j < 7:
                    v.wait(m_scat_prev)
        v.wait(m_scat_prev)
        v.tensor_copy(out=tokTo[:], in_=p_tok[:])
        v.tensor_copy(out=wscTo[:], in_=p_wsc[:])
        m_fin = k.v.mark()
        d.wait(m_fin)
        d.start(o_tokT, tokTo[:], sem=k.st_sems[0])
        d.start(o_wscT, wscTo[:], sem=k.st_sems[0])
    return k.d.mark(k.st_sems[0])




def build_launch_B(To=TPC):
    import os as _os
    BSTOP = int(_os.environ.get("BSTOP", "99"))
    k = K()
    TQ = 64 + To
    NTQ = (TQ + 127) // 128
    TQP = NTQ * 128
    KV = 128 * (NTQ - 1) + 640
    NKV = KV // 128
    NTO = To // 128

    n1own = k.din("n1own", [To, D], BF16)
    n1ctx = k.din("n1ctx", [512, D], BF16)
    memw = k.din("memw", [64, D])
    w1T = k.din("w1T", [D, HID1]); b1_r = k.din("b1_r", [1, HID1], BF16)
    w2T = k.din("w2T", [HID1, D]); b2_r = k.din("b2_r", [1, D], BF16)
    w2as = k.din("w2as", [D, HID1])
    wqT = k.din("wqT", [D, D]); wkT = k.din("wkT", [D, D])
    wvT = k.din("wvT", [D, D]); woT = k.din("woT", [D, D])
    sguw = k.din("sguw", [D, 2 * I_SH]); sdw = k.din("sdw", [I_SH, D])
    sgate_in = k.din("sgate_in", [To, 1])
    masks = k.din("masks", [NTQ * 128, 704], BF16)
    rowmask = k.din("rowmask", [TQP, 1])
    counts_all = k.din("counts_all", [1, 8 * 64])
    idx8_in = k.din("idx8_in", [To, TOPK])
    w8_in = k.din("w8_in", [To, TOPK])
    coremask = k.din("coremask", [1, 8 * 64])
    tokbase = k.din("tokbase", [128, 1])
    identb2 = k.din("identb2", [128, 128], BF16)
    iota64b = k.din("iota64b", [128, 64])
    iota512b = k.din("iota512b", [128, 512])
    lexc = k.din("lexc", [128, 128])

    o_swa = k.dout("swa_full", [TQP, D])
    o_dw1 = k.dout("dw1", [HID1, D])
    o_dw2T = k.dout("dw2T", [HID1, D])
    o_db1 = k.dout("db1", [1, HID1])
    o_db2 = k.dout("db2", [1, D])
    o_tokT = k.dout("tokT", [64, CAP])
    o_wscT = k.dout("wscT", [64, CAP])
    o_offs = k.dout("offs", [To, TOPK])
    o_sh = k.dout("sh", [To, D])

    nq_dram = k.dint("nq_dram", [TQP, D], BF16)
    kv_dram = k.dint("kv_dram", [KV, D], BF16)
    qbT_dram = k.dint("qbT_dram", [D, TQP], BF16)
    kbT_dram = k.dint("kbT_dram", [D, KV], BF16)
    vb_dram = k.dint("vb_dram", [KV, D], BF16)
    swah_dram = k.dint("swah_dram", [TQP, D], BF16)
    h1_dram = k.dint("h1_dram", [TQP, HID1], BF16)
    a1_dram = k.dint("a1_dram", [TQP, HID1], BF16)
    dpred_dram = k.dint("dpred_dram", [TQP, D], BF16)
    dh1_dram = k.dint("dh1_dram", [TQP, HID1], BF16)
    sgA_dram = k.dint("sgA_dram", [To, I_SH], BF16)
    shgu_dram = k.dint("shgu_dram", [To, I_SH], BF16)

    with k.es:
        k.build_seqs()
        t, v, s, g, d, gd = k.t, k.v, k.s, k.g, k.d, k.gd
        k.init_common(identb2)

        zb = k.sb("zb", [128, D], BF16)
        membf = k.sb("membf", [64, D], BF16)
        v.memset(zb[:], 0.0)
        m_z = k.v.mark()
        gd.start(membf[:], memw)
        m_mem = k.gd.mark()
        gd.wait(m_mem)
        d.wait(m_z)
        d.wait(m_mem)
        d.start(nq_dram[0:64, :], membf[:])
        d.start(kv_dram[0:64, :], membf[:])
        d.start(nq_dram[64:64 + To, :], n1own)
        d.start(kv_dram[64:576, :], n1ctx)
        d.start(kv_dram[576:576 + To, :], n1own)
        d.start(nq_dram[64 + To:TQP, :], zb[0:TQP - 64 - To, :])
        d.start(kv_dram[576 + To:KV, :], zb[0:KV - 576 - To, :])
        m_b0 = k.d.mark()

        es1w = ExitStack()
        wq_sb = k.sb("bwq", [128, 8 * D], BF16, es1w)
        wk_sb = k.sb("bwk", [128, 8 * D], BF16, es1w)
        wv_sb = k.sb("bwv", [128, 8 * D], BF16, es1w)
        guard_weights(k)
        load_cast(k, wqT, wq_sb, 8, D)
        load_cast(k, wkT, wk_sb, 8, D)
        load_cast(k, wvT, wv_sb, 8, D)
        m_w1 = gd.mark()
        t.wait(m_w1)
        pass_gemm(k, nq_dram, TQP, D, D, wq_sb, out=qbT_dram, src_mark=m_b0,
                  store_transposed=True)
        pass_gemm(k, kv_dram, KV, D, D, wk_sb, out=kbT_dram, src_mark=m_b0,
                  store_transposed=True)
        pass_gemm(k, kv_dram, KV, D, D, wv_sb, out=vb_dram, src_mark=m_b0)
        m_b1 = stmarks(k)
        es1w.close()

        if BSTOP < 1:
            k.finalize()
            return k
        with ExitStack() as esw:
            kfm = k.sb("kfm", [128, 2 * KV], BF16, esw)
            vh = k.sb("vh", [128, NKV * 256], BF16, esw)
            msk_all = k.sb("mskall", [128, NTQ * 704], BF16, esw)
            qfm = [k.sb("qfm", [128, 256], BF16, esw) for _ in range(2)]
            Pb = [k.sb("Pb", [128, 768], BF16, esw) for _ in range(2)]
            PT = [k.sb("PT", [128, 768], BF16, esw) for _ in range(2)]
            rs_ = [k.sb("rs_", [128, 2], F32, esw) for _ in range(2)]
            avout = [k.sb("avout", [128, 256], BF16, esw) for _ in range(2)]
            p_log = [k.ps(esw, "p_log", [128, 1024], F32) for _ in range(2)]
            p_av = [k.ps(esw, "p_av", [128, 256], F32) for _ in range(2)]

            gd.wait(m_b1)
            for i in range(NTQ):
                gd.start(msk_all[:, i * 704:(i + 1) * 704],
                         masks[i * 128:(i + 1) * 128, :], sem=k.gld_sems[0])
            m_msk = gd.mark(k.gld_sems[0])
            gd.wait(m_msk)
            v.wait(m_msk)
            v.memset(Pb[0][:, 704:768], 0.0)
            v.memset(Pb[1][:, 704:768], 0.0)
            m_pbz = k.v.mark()
            t.wait(m_pbz)

            sd = k.sd
            m_lq, m_lgt, m_vm, m_exp = {}, {}, {}, {}
            m_ptd, m_avmm, m_avev, m_avd = {}, {}, {}, {}
            m_kfmT = {}
            m_vh = {}
            for h in range(H1):
                hc = h * 256
                hit0 = h * NTQ
                if h == 0:
                    gd.wait(m_b1)
                else:
                    gd.wait(m_lgt[hit0 - 1])
                    gd.wait(m_avmm[hit0 - 1])
                for cc in range(2):
                    gd.start(kfm[:, cc * KV:(cc + 1) * KV],
                             kbT_dram[hc + cc * 128:hc + (cc + 1) * 128, :],
                             sem=k.gld_sems[h % 2])
                m_kfmT[h] = gd.mark(k.gld_sems[h % 2])
                gd.wait(m_kfmT[h])
                gd.start(vh[:].rearrange("p (c f) -> p c f", c=NKV),
                         vb_dram[:, hc:hc + 256].rearrange("(c p) f -> p c f", c=NKV),
                         sem=k.gld_sems[h % 2])
                m_vh[h] = gd.mark(k.gld_sems[h % 2])
                gd.wait(m_vh[h])

                if h == 0:
                    d.wait(m_b1)
                if hit0 - 2 >= 0:
                    d.wait(m_lgt[hit0 - 2])
                d.start(qfm[hit0 % 2][:].rearrange("p (c f) -> p c f", c=2),
                        qbT_dram[hc:hc + 256, 0:128]
                        .rearrange("(c p) f -> p c f", c=2),
                        sem=k.ld_sems[hit0 % 2])
                m_lq[hit0] = d.mark(k.ld_sems[hit0 % 2])

                for step in range(NTQ + 1):
                    i = step
                    j = step - 1
                    it = hit0 + i
                    jt = hit0 + j
                    if i < NTQ:
                        t.wait(m_lq[it])
                        if i < 2:
                            t.wait(m_kfmT[h])
                        if it - 2 >= 0:
                            t.wait(m_exp[it - 2])
                        for cc in range(2):
                            st_ = qfm[it % 2][:, cc * 128:(cc + 1) * 128]
                            base = cc * KV + i * 128
                            t.matmul(p_log[it % 2][:, 0:512], st_,
                                     kfm[:, base:base + 512],
                                     start=(cc == 0), stop=(cc == 1))
                            t.matmul(p_log[it % 2][:, 512:640], st_,
                                     kfm[:, base + 512:base + 640],
                                     start=(cc == 0), stop=False)
                            t.matmul(p_log[it % 2][:, 640:704], st_,
                                     kfm[:, cc * KV:cc * KV + 64],
                                     start=False, stop=(cc == 1))
                        m_lgt[it] = t.mark()
                        v.wait(m_lgt[it])
                        if it - 2 >= 0:
                            v.wait(m_exp[it - 2])
                        v.tensor_tensor(out=p_log[it % 2][:, 0:704],
                                        in0=p_log[it % 2][:, 0:704],
                                        in1=msk_all[:, i * 704:(i + 1) * 704],
                                        op=ALU.add)
                        v.tensor_reduce(out=rs_[it % 2][:, 0:1],
                                        in_=p_log[it % 2][:, 0:704], axis=AX.X,
                                        op=ALU.max, negate=True)
                        v.tensor_scalar(out=rs_[it % 2][:, 0:1],
                                        in0=rs_[it % 2][:, 0:1],
                                        scalar1=1.0 / 16.0, scalar2=None,
                                        op0=ALU.mult)
                        m_vm[it] = k.v.mark()
                    if j >= 0:
                        k.s.wait(m_exp[jt])
                        if jt - 2 >= 0:
                            k.s.wait(m_avmm[jt - 2])
                        if jt == 0:
                            k.s.wait(m_pbz)
                        sd.start(PT[jt % 2][:].rearrange("p (c f) -> p c f", c=6),
                                 Pb[jt % 2][:].rearrange("p (c f) -> p c f", c=6),
                                 transpose=True, sem=k.sd_sems[jt % 2])
                        m_ptd[jt] = sd.mark(k.sd_sems[jt % 2])
                    if i < NTQ:
                        s.wait(m_vm[it])
                        if it - 2 >= 0:
                            s.wait(m_ptd[it - 2])
                            s.wait(m_avev[it - 2])
                        s.activation(out=Pb[it % 2][:, 0:704],
                                     in_=p_log[it % 2][:, 0:704], func=AF.Exp,
                                     bias=rs_[it % 2][:, 0:1], scale=1.0 / 16.0,
                                     accum_out=rs_[it % 2][:, 1:2])
                        m_exp[it] = k.s.mark()
                        if i + 1 < NTQ:
                            d.wait(m_lgt[it - 1] if it - 1 >= 0 else m_b1)
                            d.start(qfm[(it + 1) % 2][:]
                                    .rearrange("p (c f) -> p c f", c=2),
                                    qbT_dram[hc:hc + 256,
                                             (i + 1) * 128:(i + 2) * 128]
                                    .rearrange("(c p) f -> p c f", c=2),
                                    sem=k.ld_sems[(it + 1) % 2])
                            m_lq[it + 1] = d.mark(k.ld_sems[(it + 1) % 2])
                    if j >= 0:
                        v.wait(m_exp[jt])
                        v.reciprocal(out=rs_[jt % 2][:, 1:2], in_=rs_[jt % 2][:, 1:2])
                        t.wait(m_ptd[jt])
                        if j < 2:
                            t.wait(m_vh[h])
                        if jt - 2 >= 0:
                            t.wait(m_avev[jt - 2])
                        for c5 in range(5):
                            t.matmul(p_av[jt % 2][:],
                                     PT[jt % 2][:, c5 * 128:(c5 + 1) * 128],
                                     vh[:, (j + c5) * 256:(j + c5 + 1) * 256],
                                     start=(c5 == 0), stop=False)
                        t.matmul(p_av[jt % 2][:], PT[jt % 2][0:64, 640:768],
                                 vh[0:64, 0:256], start=False, stop=True)
                        m_avmm[jt] = t.mark()
                        v.wait(m_avmm[jt])
                        if jt - 2 >= 0:
                            v.wait(m_avd[jt - 2])
                        v.tensor_scalar(out=avout[jt % 2][:], in0=p_av[jt % 2][:],
                                        scalar1=rs_[jt % 2][:, 1:2], scalar2=None,
                                        op0=ALU.mult)
                        m_avev[jt] = k.v.mark()
                        d.wait(m_avev[jt])
                        d.start(swah_dram[j * 128:(j + 1) * 128, hc:hc + 256],
                                avout[jt % 2][:], sem=k.st_sems[jt % 2])
                        m_avd[jt] = d.mark(k.st_sems[jt % 2])
        m_b2 = stmarks(k)

        if BSTOP < 2:
            k.finalize()
            return k
        es3w = ExitStack()
        wo_sb = k.sb("bwo", [128, 8 * D], BF16, es3w)
        guard_weights(k)
        load_cast(k, woT, wo_sb, 8, D)
        t.wait(gd.mark())
        pass_gemm(k, swah_dram, TQP, D, D, wo_sb, out=o_swa, out_dtype=F32,
                  src_mark=m_b2)
        m_b3 = stmarks(k)
        es3w.close()

        if BSTOP < 3:
            k.finalize()
            return k
        es4w = ExitStack()
        w1_sb = k.sb("bw1", [128, 8 * HID1], BF16, es4w)
        bias1 = k.sb("bb1", [1, HID1], BF16, es4w)
        guard_weights(k)
        load_cast(k, w1T, w1_sb, 8, HID1)
        gd.start(bias1[:], b1_r)
        t.wait(gd.mark())
        h1keep = [k.sb("h1keep", [128, 2048], BF16, es4w) for _ in range(2)]
        h1dma = {"m": [None, None]}

        def mk_h1_epi(half):
            def epi(k2, m_mm, psum_ap, out_ap, aux):
                it = aux["tile"]
                v, s, d = k2.v, k2.s, k2.d
                v.wait(m_mm)
                if h1dma["m"][it % 2] is not None:
                    v.wait(h1dma["m"][it % 2])
                v.tensor_copy(out=h1keep[it % 2][:], in_=psum_ap)
                m_c = k2.v.mark()
                d.wait(m_c)
                d.start(h1_dram[it * 128:(it + 1) * 128,
                                half * 2048:half * 2048 + 2048],
                        h1keep[it % 2][:], sem=k2.st2_sems[it % 2])
                h1dma["m"][it % 2] = k2.d.mark(k2.st2_sems[it % 2])
                s.wait(m_mm)
                s.wait(m_c)
                s.activation(out=out_ap, in_=psum_ap, func=AF.Silu)
                return k2.s.mark()
            return epi

        for half in range(2):
            pass_gemm(k, nq_dram, TQP, D, 2048, w1_sb, wcol0=half * 2048,
                      wstride=HID1, bias_sb=bias1, bias_col0=half * 2048,
                      out=a1_dram[:, half * 2048:half * 2048 + 2048],
                      epilogue=mk_h1_epi(half), src_mark=m_b0)
        m_b4 = stmarks(k)
        es4w.close()

        if BSTOP < 4:
            k.finalize()
            return k
        es5w = ExitStack()
        w2_sb = k.sb("bw2", [128, 32 * D], BF16, es5w)
        bias2 = k.sb("bb2", [1, D], BF16, es5w)
        guard_weights(k)
        load_cast(k, w2T, w2_sb, 32, D)
        gd.start(bias2[:], b2_r)
        t.wait(gd.mark())

        def dpred_epi(k2, m_mm, psum_ap, out_ap, aux):
            v = k2.v
            v.wait(m_mm)
            v.tensor_tensor(out=out_ap, in0=psum_ap, in1=aux["res"][0], op=ALU.subtract)
            v.tensor_scalar(out=out_ap, in0=out_ap, scalar1=aux["res"][1],
                            scalar2=None, op0=ALU.mult)
            return k2.v.mark()

        pass_gemm(k, a1_dram, TQP, HID1, D, w2_sb, bias_sb=bias2, out=dpred_dram,
                  epilogue=dpred_epi,
                  res_srcs=[(o_swa, F32), (rowmask, F32)],
                  src_mark=m_b4, wait_marks=(m_b3,))
        m_b5 = stmarks(k)
        es5w.close()

        if BSTOP < 5:
            k.finalize()
            return k
        es6w = ExitStack()
        w2a_sb = k.sb("bw2a", [128, 8 * HID1], BF16, es6w)
        guard_weights(k)
        load_cast(k, w2as, w2a_sb, 8, HID1)
        t.wait(gd.mark())
        sg6 = k.sb("sg6", [128, 2048], F32, es6w)
        t6 = k.sb("t6", [128, 2048], F32, es6w)

        def dh1_epi(k2, m_mm, psum_ap, out_ap, aux):
            v, s = k2.v, k2.s
            h1t = aux["res"][0]
            s.wait(m_mm)
            s.wait(aux["m_in"])
            s.activation(out=sg6[:], in_=h1t, func=AF.Sigmoid)
            m_sg = k2.s.mark()
            v.wait(m_sg)
            v.tensor_scalar(out=t6[:], in0=sg6[:], scalar1=-1.0, scalar2=1.0,
                            op0=ALU.mult, op1=ALU.add)
            v.tensor_tensor(out=t6[:], in0=t6[:], in1=h1t, op=ALU.mult)
            v.tensor_scalar(out=t6[:], in0=t6[:], scalar1=1.0, scalar2=None,
                            op0=ALU.add)
            v.tensor_tensor(out=t6[:], in0=t6[:], in1=sg6[:], op=ALU.mult)
            v.wait(m_mm)
            v.tensor_tensor(out=out_ap, in0=t6[:], in1=psum_ap, op=ALU.mult)
            return k2.v.mark()

        for half in range(2):
            pass_gemm(k, dpred_dram, TQP, D, 2048, w2a_sb, wcol0=half * 2048,
                      wstride=HID1,
                      out=dh1_dram[:, half * 2048:half * 2048 + 2048],
                      epilogue=dh1_epi,
                      res_srcs=[(h1_dram[:, half * 2048:half * 2048 + 2048], BF16)],
                      src_mark=m_b5)
        m_b6 = stmarks(k)
        es6w.close()

        if BSTOP < 6:
            k.finalize()
            return k
        pass_tokcontract(k, a1_dram, HID1, dpred_dram, D, o_dw2T, TQP,
                         src_mark=m_b6, group=4)
        pass_tokcontract(k, dh1_dram, HID1, nq_dram, D, o_dw1, TQP,
                         src_mark=m_b6, group=4)
        pass_colsum(k, dh1_dram, HID1, o_db1, TQP, src_mark=m_b6)
        pass_colsum(k, dpred_dram, D, o_db2, TQP, src_mark=m_b6)

        if BSTOP < 7:
            k.finalize()
            return k
        _moe_dispatch(k, To, NTO, idx8_in, w8_in, counts_all, coremask, tokbase,
                      iota64b, iota512b, lexc, o_tokT, o_wscT, o_offs)

        if BSTOP < 8:
            k.finalize()
            return k
        es11w = ExitStack()
        sgu_sb = k.sb("bsgu", [128, 8 * 2 * I_SH], BF16, es11w)
        guard_weights(k)
        load_cast(k, sguw, sgu_sb, 8, 2 * I_SH)
        t.wait(gd.mark())

        def silu_epi2(k2, m_mm, psum_ap, out_ap, aux):
            k2.s.wait(m_mm)
            k2.s.activation(out=out_ap, in_=psum_ap, func=AF.Silu)
            return k2.s.mark()

        for half in range(2):
            pass_gemm(k, n1own, To, D, 1024, sgu_sb, wcol0=half * 1024,
                      wstride=2 * I_SH,
                      out=sgA_dram[:, half * 1024:half * 1024 + 1024],
                      epilogue=silu_epi2, src_mark=None)
        m_sga = stmarks(k)

        def upmul_epi2(k2, m_mm, psum_ap, out_ap, aux):
            k2.v.wait(m_mm)
            k2.v.tensor_tensor(out=out_ap, in0=psum_ap, in1=aux["res"][0], op=ALU.mult)
            return k2.v.mark()

        for half in range(2):
            pass_gemm(k, n1own, To, D, 1024, sgu_sb, wcol0=I_SH + half * 1024,
                      wstride=2 * I_SH,
                      out=shgu_dram[:, half * 1024:half * 1024 + 1024],
                      epilogue=upmul_epi2,
                      res_srcs=[(sgA_dram[:, half * 1024:half * 1024 + 1024], BF16)],
                      src_mark=m_sga)
        m_shgu = stmarks(k)
        es11w.close()

        es12w = ExitStack()
        sdw_sb = k.sb("bsdw", [128, 16 * D], BF16, es12w)
        guard_weights(k)
        load_cast(k, sdw, sdw_sb, 16, D)
        t.wait(gd.mark())

        def shfinal_epi(k2, m_mm, psum_ap, out_ap, aux):
            k2.v.wait(m_mm)
            k2.v.tensor_scalar(out=out_ap, in0=psum_ap, scalar1=aux["res"][0],
                               scalar2=None, op0=ALU.mult)
            return k2.v.mark()

        pass_gemm(k, shgu_dram, To, I_SH, D, sdw_sb, out=o_sh, out_dtype=F32,
                  epilogue=shfinal_epi, res_srcs=[(sgate_in, F32)],
                  src_mark=m_shgu)
        es12w.close()

        k.finalize()
    return k


def build_masks(NTQ, parity, To):
    NEG = -1e30
    TQ = 64 + To
    M = np.full((NTQ * 128, 704), NEG, dtype=np.float32)
    for r in range(NTQ * 128):
        tau = r // 128
        p = r % 128
        if r >= TQ or (parity == 1 and r < 64):
            M[r, 0] = 0.0
            continue
        if r < 64:
            for j in range(0, r + 1):
                M[r, 640 + j] = 0.0
            continue
        sown = r - 64
        if parity == 0:
            lo = 576 + max(0, sown - 511)
            hi = 576 + sown
            for kvr in range(lo, hi + 1):
                c = kvr - 128 * tau
                if 0 <= c < 640:
                    M[r, c] = 0.0
            jlo = max(0, sown - 447)
            if jlo <= 63:
                M[r, 640 + jlo:640 + 64] = 0.0
        else:
            for c in range(p + 1, p + 513):
                if 0 <= c < 640:
                    M[r, c] = 0.0
    return M


def prep_B_inputs(core, full, a_outs):
    To = a_outs[core]["n1"].shape[0]
    TQ = 64 + To
    NTQ = (TQ + 127) // 128
    TQP = NTQ * 128
    parity = core % 2
    n1own = np.ascontiguousarray(a_outs[core]["n1"])
    if parity == 1:
        n1ctx = np.ascontiguousarray(a_outs[core - 1]["n1"][-512:])
    else:
        n1ctx = np.zeros((512, D), dtype=bfdt)
    rowmask = np.zeros((TQP, 1), np.float32)
    rowmask[64:64 + To] = 1.0
    if parity == 0:
        rowmask[0:64] = 1.0
    counts_all = np.zeros((1, 8 * 64), np.float32)
    for c in range(len(a_outs)):
        counts_all[0, c * 64:(c + 1) * 64] = np.asarray(a_outs[c]["counts"]).reshape(64)
    cm = np.zeros((1, 8 * 64), np.float32)
    cm[0, :core * 64] = 1.0
    eye = np.eye(128, dtype=np.float32)
    lexc = np.tril(np.ones((128, 128), np.float32), -1).T
    m = {
        "n1own": n1own, "n1ctx": n1ctx, "memw": full["L1_attn_mem"],
        "w1T": np.ascontiguousarray(full["L1_attn_w1"].T),
        "b1_r": _bf(full["L1_attn_b1"].reshape(1, -1)),
        "w2T": np.ascontiguousarray(full["L1_attn_w2"].T),
        "b2_r": _bf(full["L1_attn_b2"].reshape(1, -1)),
        "w2as": full["L1_attn_w2"],
        "wqT": np.ascontiguousarray(full["L1_attn_wq"].T),
        "wkT": np.ascontiguousarray(full["L1_attn_wk"].T),
        "wvT": np.ascontiguousarray(full["L1_attn_wv"].T),
        "woT": np.ascontiguousarray(full["L1_attn_wo"].T),
        "sguw": full["L1_ffn_shared_gate_up_w"],
        "sdw": full["L1_ffn_shared_down_w"],
        "sgate_in": np.asarray(a_outs[core]["sgate"], np.float32),
        "masks": build_masks(NTQ, parity, To).astype(bfdt),
        "rowmask": rowmask,
        "counts_all": counts_all.astype(np.float32),
        "idx8_in": np.asarray(a_outs[core]["idx8"], np.float32),
        "w8_in": np.asarray(a_outs[core]["w8"], np.float32),
        "coremask": cm,
        "tokbase": (core * To + np.arange(128, dtype=np.float32)).reshape(128, 1),
        "identb2": _bf(eye),
        "iota64b": np.broadcast_to(np.arange(64, dtype=np.float32), (128, 64)).copy(),
        "iota512b": np.broadcast_to(np.arange(512, dtype=np.float32), (128, 512)).copy(),
        "lexc": np.ascontiguousarray(lexc),
    }
    return m



def build_launch_C(To=TPC, ncores=NCORES):
    k = K()
    NTO = To // 128
    TE = ncores * To

    dw1_all = k.din("dw1_all", [ncores * HID1, D])
    dw2T_all = k.din("dw2T_all", [ncores * HID1, D])
    db1_all = k.din("db1_all", [1, ncores * HID1])
    db2_all = k.din("db2_all", [1, ncores * D])
    w1_in = k.din("w1_in", [HID1, D])
    w2T_in = k.din("w2T_in", [HID1, D])
    b1_row = k.din("b1_row", [1, HID1])
    b2_row = k.din("b2_row", [1, D])
    alpha128 = k.din("alpha128", [128, 1])
    theta128 = k.din("theta128", [128, 1])
    lin1T = k.din("lin1T", [D, D])
    lin2T = k.din("lin2T", [D, D])
    n1own_c = k.din("n1own_c", [To, D], BF16)
    swa_own = k.din("swa_own", [To, D])
    tokT_all = k.din("tokT_all", [ncores * 64, CAP])
    wsc_all = k.din("wsc_all", [ncores * 64, CAP])
    xn_all = k.din("xn_all", [TE, D], BF16)
    egu_c = k.din("egu_c", [8 * D, 2 * I_EXP])
    edn_c = k.din("edn_c", [8 * I_EXP, D])
    ebase8 = k.din("ebase8", [8, 1])
    identb3 = k.din("identb3", [128, 128], BF16)

    o_merged = k.dout("merged", [To, D])
    o_outtab = k.dout("outtab", [8 * CAP, D])

    w1p_dram = k.dint("w1p_dram", [HID1, D], BF16)
    w1Tp_dram = k.dint("w1Tp_dram", [D, HID1], BF16)
    w2Tp_dram = k.dint("w2Tp_dram", [HID1, D], BF16)
    a2_dram = k.dint("a2_dram", [To, HID1], BF16)
    omem_dram = k.dint("omem_dram", [To, D], BF16)
    g1_dram = k.dint("g1_dram", [To, D], BF16)
    toksum_dram = k.dint("toksum_dram", [64, CAP])
    wscsum_dram = k.dint("wscsum_dram", [64, CAP])
    mytok_dram = k.dint("mytok_dram", [8, CAP])
    mywsc_dram = k.dint("mywsc_dram", [8, CAP])
    buf_dram = k.dint("buf_dram", [CAP, D], BF16)
    act_dram = k.dint("act_dram", [CAP, 2 * I_EXP], BF16)

    with k.es:
        k.build_seqs()
        t, v, s, g, d, gd = k.t, k.v, k.s, k.g, k.d, k.gd
        k.init_common(identb3)

        al = k.sb("al", [128, 1], F32)
        th = k.sb("th", [128, 1], F32)
        oma = k.sb("oma", [128, 1], F32)
        nth = k.sb("nth", [128, 1], F32)
        d.start(al[:], alpha128)
        d.start(th[:], theta128)
        m_at = k.d.mark()
        v.wait(m_at)
        v.tensor_scalar(out=oma[:], in0=al[:], scalar1=-1.0, scalar2=1.0,
                        op0=ALU.mult, op1=ALU.add)
        v.tensor_scalar(out=nth[:], in0=th[:], scalar1=-1.0, scalar2=None,
                        op0=ALU.mult)
        m_consts = k.v.mark()

        def upd_pass(w_src, dparts_src, out_bf_dram, also_transpose_to=None):
            nch = HID1 // 128
            es1 = ExitStack()
            with es1:
                gb = [[k.sb("updg", [128, D], F32, es1) for _ in range(ncores + 1)]
                      for _ in range(2)]
                ob = [k.sb("updo", [128, D], BF16, es1) for _ in range(2)]
                tpb = [k.sb("updtp", [128, D], BF16, es1) for _ in range(2)]
                entry = stmarks(k)
                m_done = [None, None]
                m_st = [None, None]
                m_tp = [None, None]
                m_st2 = [None, None]
                for i in range(nch):
                    p = i % 2
                    sl = k.ld_sems[p]
                    if i < 2:
                        d.wait(entry)
                    if m_done[p] is not None:
                        d.wait(m_done[p])
                    d.start(gb[p][0][:], w_src[i * 128:(i + 1) * 128, :], sem=sl)
                    for r in range(ncores):
                        d.start(gb[p][r + 1][:],
                                dparts_src[r * HID1 + i * 128:r * HID1 + (i + 1) * 128, :],
                                sem=sl)
                    m_l = d.mark(sl)
                    v.wait(m_l)
                    v.wait(m_consts)
                    if m_st[p] is not None:
                        v.wait(m_st[p])
                    if m_tp[p] is not None:
                        v.wait(m_tp[p])
                    acc = gb[p][1]
                    for r in range(2, ncores + 1):
                        v.tensor_tensor(out=acc[:], in0=acc[:], in1=gb[p][r][:],
                                        op=ALU.add)
                    v.tensor_scalar(out=gb[p][0][:], in0=gb[p][0][:], scalar1=oma[:],
                                    scalar2=None, op0=ALU.mult)
                    v.scalar_tensor_tensor(out=ob[p][:], in0=acc[:], scalar=nth[:],
                                           in1=gb[p][0][:], op0=ALU.mult, op1=ALU.add)
                    m_u = k.v.mark()
                    m_done[p] = m_u
                    d.wait(m_u)
                    d.start(out_bf_dram[i * 128:(i + 1) * 128, :], ob[p][:],
                            sem=k.st_sems[p])
                    m_st[p] = d.mark(k.st_sems[p])
                    if also_transpose_to is not None:
                        if m_st2[p] is not None:
                            d.wait(m_st2[p])
                        d.start(tpb[p][:].rearrange("p (c f) -> p c f", c=8),
                                ob[p][:].rearrange("p (c f) -> p c f", c=8),
                                transpose=True, sem=k.tpd_sems[p])
                        m_tp[p] = d.mark(k.tpd_sems[p])
                        d.wait(m_tp[p])
                        d.start(also_transpose_to[:, i * 128:(i + 1) * 128]
                                .rearrange("(c p) f -> p c f", c=8),
                                tpb[p][:].rearrange("p (c f) -> p c f", c=8),
                                sem=k.st2_sems[p])
                        m_st2[p] = d.mark(k.st2_sems[p])
            return stmarks(k)

        upd_pass(w1_in, dw1_all, w1p_dram, also_transpose_to=w1Tp_dram)
        m_c1a = stmarks(k)
        upd_pass(w2T_in, dw2T_all, w2Tp_dram)
        m_c1 = stmarks(k)

        b1p = k.sb("b1p", [1, HID1], BF16)
        b2p = k.sb("b2p", [1, D], BF16)
        br = k.sb("br", [1, HID1], F32)
        bacc = k.sb("bacc", [1, HID1], F32)
        bld = k.sb("bld", [1, HID1], F32)

        def bias_upd(b_row_src, dball_src, width, out_bp):
            d.wait(k.v.mark())
            d.start(br[0:1, 0:width], b_row_src)
            d.start(bacc[0:1, 0:width], dball_src[0:1, 0:width])
            m_lb = k.d.mark()
            v.wait(m_lb)
            for r in range(1, ncores):
                d.start(bld[0:1, 0:width],
                        dball_src[0:1, r * width:(r + 1) * width],
                        sem=k.ld_sems[0])
                m_lr = d.mark(k.ld_sems[0])
                v.wait(m_lr)
                v.tensor_tensor(out=bacc[0:1, 0:width], in0=bacc[0:1, 0:width],
                                in1=bld[0:1, 0:width], op=ALU.add)
                m_a = k.v.mark()
                d.wait(m_a)
            v.wait(m_consts)
            v.tensor_scalar(out=br[0:1, 0:width], in0=br[0:1, 0:width],
                            scalar1=oma[0:1, :], scalar2=None, op0=ALU.mult)
            v.scalar_tensor_tensor(out=out_bp, in0=bacc[0:1, 0:width],
                                   scalar=nth[0:1, :], in1=br[0:1, 0:width],
                                   op0=ALU.mult, op1=ALU.add)
            return k.v.mark()

        m_b1p = bias_upd(b1_row, db1_all, HID1, b1p[:])
        m_b2p = bias_upd(b2_row, db2_all, D, b2p[:])

        es2 = ExitStack()
        w1p_sb = k.sb("cw1p", [128, 8 * HID1], BF16, es2)
        guard_weights(k)
        gd.wait(m_c1a)
        for i in range(8):
            gd.start(w1p_sb[:, i * HID1:(i + 1) * HID1],
                     w1Tp_dram[i * 128:(i + 1) * 128, :])
        m_w1p = gd.mark()
        t.wait(m_w1p)
        t.wait(m_b1p)

        def silu_epi3(k2, m_mm, psum_ap, out_ap, aux):
            k2.s.wait(m_mm)
            k2.s.activation(out=out_ap, in_=psum_ap, func=AF.Silu)
            return k2.s.mark()

        for half in range(2):
            pass_gemm(k, n1own_c, To, D, 2048, w1p_sb, wcol0=half * 2048,
                      wstride=HID1, bias_sb=b1p, bias_col0=half * 2048,
                      out=a2_dram[:, half * 2048:half * 2048 + 2048],
                      epilogue=silu_epi3)
        m_a2 = stmarks(k)
        es2.close()

        es3 = ExitStack()
        w2p_sb = k.sb("cw2p", [128, 32 * D], BF16, es3)
        guard_weights(k)
        gd.wait(m_c1)
        for i in range(32):
            gd.start(w2p_sb[:, i * D:(i + 1) * D], w2Tp_dram[i * 128:(i + 1) * 128, :])
        m_w2p = gd.mark()
        t.wait(m_w2p)
        t.wait(m_b2p)
        pass_gemm(k, a2_dram, To, HID1, D, w2p_sb, bias_sb=b2p, out=omem_dram,
                  src_mark=m_a2)
        m_omem = stmarks(k)
        es3.close()

        es4 = ExitStack()
        l1_sb = k.sb("cl1", [128, 8 * D], BF16, es4)
        guard_weights(k)
        load_cast(k, lin1T, l1_sb, 8, D)
        t.wait(gd.mark())

        def sig_epi(k2, m_mm, psum_ap, out_ap, aux):
            k2.s.wait(m_mm)
            k2.s.activation(out=out_ap, in_=psum_ap, func=AF.Sigmoid)
            return k2.s.mark()

        pass_gemm(k, swa_own, To, D, D, l1_sb, out=g1_dram, epilogue=sig_epi,
                  src_dtype=F32)
        m_g1 = stmarks(k)
        es4.close()

        es5 = ExitStack()
        l2_sb = k.sb("cl2", [128, 8 * D], BF16, es5)
        guard_weights(k)
        load_cast(k, lin2T, l2_sb, 8, D)
        t.wait(gd.mark())
        g2t = k.sb("cg2t", [128, D], F32, es5)
        mt = k.sb("cmt", [128, D], F32, es5)
        cf1 = k.sb("ccf1", [128, D], F32, es5)
        cf2 = k.sb("ccf2", [128, D], F32, es5)

        prev_vm = [None]

        def merge_epi(k2, m_mm, psum_ap, out_ap, aux):
            v, s = k2.v, k2.s
            s.wait(m_mm)
            if prev_vm[0] is not None:
                s.wait(prev_vm[0])
            s.activation(out=g2t[:], in_=psum_ap, func=AF.Sigmoid)
            m_sg = k2.s.mark()
            v.wait(m_sg)
            v.wait(aux["m_in"])
            v.tensor_copy(out=cf1[:], in_=aux["res"][0])
            v.tensor_copy(out=cf2[:], in_=aux["res"][2])
            v.tensor_tensor(out=mt[:], in0=g2t[:], in1=cf2[:], op=ALU.mult)
            v.tensor_tensor(out=out_ap, in0=cf1[:], in1=aux["res"][1], op=ALU.mult)
            v.tensor_tensor(out=out_ap, in0=out_ap, in1=mt[:], op=ALU.add)
            prev_vm[0] = k2.v.mark()
            return prev_vm[0]

        pass_gemm(k, omem_dram, To, D, D, l2_sb, out=o_merged, out_dtype=F32,
                  epilogue=merge_epi,
                  res_srcs=[(g1_dram, BF16), (swa_own, F32), (omem_dram, BF16)],
                  src_mark=m_omem, wait_marks=(m_g1,))
        es5.close()

        es6 = ExitStack()
        tok8 = k.sb("ctok8", [64, CAP], F32, es6)
        tokt = k.sb("ctokt", [64, CAP], F32, es6)
        wsc8 = k.sb("cwsc8", [64, CAP], F32, es6)
        myt = k.sb("cmyt", [8, CAP], F32, es6)
        myw = k.sb("cmyw", [8, CAP], F32, es6)
        eb8 = k.sb("ceb8", [8, 1], F32, es6)
        eb8i = k.sb("ceb8i", [8, 1], I32, es6)
        entry6 = stmarks(k)
        d.wait(entry6)
        d.start(eb8[:], ebase8)
        d.start(tok8[:], tokT_all[0:64, :])
        d.start(wsc8[:], wsc_all[0:64, :])
        m_l6 = k.d.mark()
        v.wait(m_l6)
        v.tensor_copy(out=eb8i[:], in_=eb8[:])
        d.wait(k.v.mark())
        for r in range(1, ncores):
            d.start(tokt[:], tokT_all[r * 64:(r + 1) * 64, :], sem=k.ld_sems[1])
            m_lr = d.mark(k.ld_sems[1])
            v.wait(m_lr)
            v.tensor_tensor(out=tok8[:], in0=tok8[:], in1=tokt[:], op=ALU.add)
            m_a = k.v.mark()
            d.wait(m_a)
            d.start(tokt[:], wsc_all[r * 64:(r + 1) * 64, :], sem=k.ld_sems[1])
            m_lw = d.mark(k.ld_sems[1])
            v.wait(m_lw)
            v.tensor_tensor(out=wsc8[:], in0=wsc8[:], in1=tokt[:], op=ALU.add)
            m_a2_ = k.v.mark()
            d.wait(m_a2_)
        m_sum = k.v.mark()
        d.wait(m_sum)
        d.start(toksum_dram[:, :], tok8[:])
        d.start(wscsum_dram[:, :], wsc8[:])
        m_tsd = k.d.mark()
        gd.wait(m_tsd)
        gd.wait(m_sum)
        gd.indirect(myt[:, :], None, toksum_dram,
                    IndirectOffsetOnAxis(ap=eb8i[:, 0:1], axis=0))
        gd.indirect(myw[:, :], None, wscsum_dram,
                    IndirectOffsetOnAxis(ap=eb8i[:, 0:1], axis=0))
        m_my = k.gd.mark()
        gd.wait(m_my)
        d.wait(m_my)
        d.start(mytok_dram[:, :], myt[:])
        d.start(mywsc_dram[:, :], myw[:])
        m_myd = k.d.mark()

        idxc = k.sb("cidxc", [128, 1], F32, es6)
        idxi = k.sb("cidxi", [128, 1], I32, es6)
        bufb = k.sb("cbufb", [128, D], BF16, es6)
        egu_sb = k.sb("cegu", [128, 8 * 2 * I_EXP], BF16, es6)
        edn_sb = k.sb("cedn", [128, 4 * D], BF16, es6)
        sc6 = k.sb("csc6", [128, I_EXP], BF16, es6)
        prev_e = [None]
        for e in range(8):
            guard_weights(k)
            load_cast(k, egu_c[e * D:(e + 1) * D, :], egu_sb, 8, 2 * I_EXP)
            load_cast(k, edn_c[e * I_EXP:(e + 1) * I_EXP, :], edn_sb, 4, D)
            m_we = gd.mark()
            gd.wait(m_we)
            m_ci_prev = None
            for q in range(4):
                d.wait(m_myd)
                d.wait(stmarks(k))
                if m_ci_prev is not None:
                    d.wait(m_ci_prev)
                d.start(idxc[:], mytok_dram[e, q * 128:(q + 1) * 128].rearrange(
                    "(p one) -> p one", one=1))
                m_lq = k.d.mark()
                v.wait(m_lq)
                v.tensor_copy(out=idxi[:], in_=idxc[:])
                m_ci = k.v.mark()
                m_ci_prev = m_ci
                gd.wait(m_ci)
                gd.indirect(bufb[:, :], None, xn_all,
                            IndirectOffsetOnAxis(ap=idxi[:, 0:1], axis=0))
                m_gq = k.gd.mark()
                gd.wait(m_gq)
                d.wait(m_gq)
                d.start(buf_dram[q * 128:(q + 1) * 128, :], bufb[:],
                        sem=k.st_sems[q % 2])
                gd.wait(d.mark(k.st_sems[q % 2]))
            m_bufd = stmarks(k)
            t.wait(m_we)

            def act_epi(k2, m_mm, psum_ap, out_ap, aux):
                v, s = k2.v, k2.s
                s.wait(m_mm)
                if prev_e[0] is not None:
                    s.wait(prev_e[0])
                s.activation(out=sc6[:], in_=psum_ap[:, 0:I_EXP], func=AF.Silu)
                m_si = k2.s.mark()
                v.wait(m_si)
                v.tensor_tensor(out=out_ap[:, 0:I_EXP], in0=sc6[:],
                                in1=psum_ap[:, I_EXP:2 * I_EXP], op=ALU.mult)
                v.memset(out_ap[:, I_EXP:2 * I_EXP], 0.0)
                prev_e[0] = k2.v.mark()
                return prev_e[0]

            pass_gemm(k, buf_dram, CAP, D, 2 * I_EXP, egu_sb, out=act_dram,
                      epilogue=act_epi, src_mark=m_bufd)
            m_act = stmarks(k)

            def wsc_epi(k2, m_mm, psum_ap, out_ap, aux):
                v = k2.v
                v.wait(m_mm)
                v.tensor_scalar(out=out_ap, in0=psum_ap, scalar1=aux["res"][0],
                                scalar2=None, op0=ALU.mult)
                return k2.v.mark()

            pass_gemm(k, act_dram[:, 0:I_EXP], CAP, I_EXP, D, edn_sb,
                      out=o_outtab[e * CAP:(e + 1) * CAP, :], out_dtype=F32,
                      epilogue=wsc_epi,
                      res_srcs=[(mywsc_dram[e, :].rearrange("(s one) -> s one", one=1),
                                 F32)],
                      src_mark=m_act, wait_marks=(m_myd,))
        es6.close()

        k.finalize()
    return k



def build_launch_D(To=TPC, ncores=NCORES):
    k = K()
    NTO = To // 128
    outtab_all = k.din("outtab_all", [ncores * 8 * CAP, D])
    offs_in = k.din("offs_in", [To, TOPK])
    x1_in = k.din("x1_in", [To, D])
    merged_in = k.din("merged_in", [To, D])
    sh_in = k.din("sh_in", [To, D])
    o_x2 = k.dout("x2", [To, D])

    with k.es:
        k.build_seqs()
        t, v, s, g, d, gd = k.t, k.v, k.s, k.g, k.d, k.gd
        offb = [k.sb("doff", [128, TOPK], F32) for _ in range(2)]
        offi = [k.sb("doffi", [128, TOPK], I32) for _ in range(2)]
        ysb = [k.sb("dy", [128, D], F32) for _ in range(2)]
        g8 = [[k.sb("dg8", [128, D], F32) for _ in range(TOPK)] for _ in range(2)]
        x1b = [k.sb("dx1", [128, D], F32) for _ in range(2)]
        mgb = [k.sb("dmg", [128, D], F32) for _ in range(2)]
        shb = [k.sb("dsh", [128, D], F32) for _ in range(2)]
        xo = [k.sb("dxo", [128, D], F32) for _ in range(2)]
        kmask8 = [k.sb("dkm", [128, TOPK], F32) for _ in range(2)]
        offc = k.sb("doffc", [128, TOPK], F32)
        m_xo_dma = [None, None]
        m_add_last = [None, None]
        m_g = [None, None]
        for it in range(NTO):
            p = it % 2
            sl = k.ld_sems[p]
            if m_add_last[p] is not None:
                d.wait(m_add_last[p])
            d.start(offb[p][:], offs_in[it * 128:(it + 1) * 128, :], sem=sl)
            d.start(x1b[p][:], x1_in[it * 128:(it + 1) * 128, :], sem=sl)
            d.start(mgb[p][:], merged_in[it * 128:(it + 1) * 128, :], sem=sl)
            d.start(shb[p][:], sh_in[it * 128:(it + 1) * 128, :], sem=sl)
            m_l = d.mark(sl)
            v.wait(m_l)
            v.tensor_scalar(out=offc[:], in0=offb[p][:],
                            scalar1=float(ncores * 8 * CAP - 1), scalar2=None,
                            op0=ALU.min)
            v.tensor_copy(out=offi[p][:], in_=offc[:])
            v.tensor_scalar(out=kmask8[p][:], in0=offb[p][:],
                            scalar1=float(2 ** 29), scalar2=None, op0=ALU.is_lt)
            m_ci = k.v.mark()
            gd.wait(m_ci)
            if m_add_last[p] is not None:
                gd.wait(m_add_last[p])
            for j in range(TOPK):
                gd.indirect(g8[p][j][:, :], None, outtab_all,
                            IndirectOffsetOnAxis(ap=offi[p][:, j:j + 1], axis=0),
                            sem=k.gld_sems[p])
            m_g[p] = gd.mark(k.gld_sems[p])
            gd.wait(m_g[p])
            v.wait(m_g[p])
            v.tensor_scalar(out=ysb[p][:], in0=g8[p][0][:],
                            scalar1=kmask8[p][:, 0:1], scalar2=None, op0=ALU.mult)
            for j in range(1, TOPK):
                v.scalar_tensor_tensor(out=ysb[p][:], in0=g8[p][j][:],
                                       scalar=kmask8[p][:, j:j + 1],
                                       in1=ysb[p][:], op0=ALU.mult, op1=ALU.add)
            if m_xo_dma[p] is not None:
                v.wait(m_xo_dma[p])
            v.tensor_tensor(out=xo[p][:], in0=x1b[p][:], in1=mgb[p][:], op=ALU.add)
            v.tensor_tensor(out=xo[p][:], in0=xo[p][:], in1=shb[p][:], op=ALU.add)
            v.tensor_tensor(out=xo[p][:], in0=xo[p][:], in1=ysb[p][:], op=ALU.add)
            m_add_last[p] = k.v.mark()
            d.wait(m_add_last[p])
            d.start(o_x2[it * 128:(it + 1) * 128, :], xo[p][:],
                    sem=k.st_sems[p])
            m_xo_dma[p] = d.mark(k.st_sems[p])
        k.finalize()
    return k


# version 18
# speedup vs baseline: 1.1663x; 1.0658x over previous
import numpy as np
from contextlib import ExitStack

import concourse.bass as bass
import concourse.mybir as mybir
from concourse.bass import IndirectOffsetOnAxis
from concourse.bass_utils import run_bass_kernel_spmd
import ml_dtypes

F32 = mybir.dt.float32
BF16 = mybir.dt.bfloat16
I32 = mybir.dt.int32
AF = mybir.ActivationFunctionType
ALU = mybir.AluOpType
AX = mybir.AxisListType

NCORES = 8
B, S, D = 4, 8192, 1024
TPC = S // 2
H0, HD0 = 16, 64
MLP0 = 4096
FFN0 = 3072
H1, HD1 = 4, 256
WIN = 512
NPM = 64
HID1 = 4096
E_EXP, TOPK, CAP = 64, 8, 512
I_EXP, I_SH = 512, 2048
EPS = 1e-5

bfdt = ml_dtypes.bfloat16



class Seq:

    def __init__(self, name, sem, ops, chain=True):
        self.name = name
        self.sem = sem
        self.n = 0
        self.chain = chain
        self.ops = ops

    def wait(self, mark):
        if isinstance(mark, list):
            for m in mark:
                self.wait(m)
            return
        sem, val = mark
        if val > 0:
            self.ops.append(lambda e, s=sem, v=val: e.wait_ge(s, v))

    def mark(self):
        return (self.sem, self.n)

    def __getattr__(self, name):
        if name.startswith("_"):
            raise AttributeError(name)

        def wrapper(*a, **kw):
            need_wait = self.chain and self.n > 0
            nprev = self.n
            sem = self.sem

            def run(e):
                if need_wait:
                    e.wait_ge(sem, nprev)
                getattr(e, name)(*a, **kw).then_inc(sem, 1)

            self.ops.append(run)
            self.n += 1

        return wrapper


class Dma:

    def __init__(self, name, sem, ops):
        self.name = name
        self.sem = sem
        self.counts = {id(sem): 0}
        self.sems = {id(sem): sem}
        self.ops = ops

    def wait(self, mark):
        if isinstance(mark, list):
            for m in mark:
                self.wait(m)
            return
        sem, val = mark
        if val > 0:
            self.ops.append(lambda e, s=sem, v=val: e.wait_ge(s, v))

    def mark(self, sem=None):
        sem = sem if sem is not None else self.sem
        return (sem, 16 * self.counts.get(id(sem), 0))

    def _bump(self, sem):
        self.sems[id(sem)] = sem
        self.counts[id(sem)] = self.counts.get(id(sem), 0) + 1

    def start(self, out, in_, sem=None, **kw):
        sem = sem if sem is not None else self.sem

        def run(e):
            e.dma_start(out=out, in_=in_, **kw).then_inc(sem, 16)

        self.ops.append(run)
        self._bump(sem)

    def indirect(self, out, out_offset, in_, in_offset, sem=None, **kw):
        sem = sem if sem is not None else self.sem

        def run(e):
            e.indirect_dma_start(out=out, out_offset=out_offset, in_=in_,
                                 in_offset=in_offset, **kw).then_inc(sem, 16)

        self.ops.append(run)
        self._bump(sem)

    def final_wait(self):
        for sid, sem in self.sems.items():
            n = self.counts[sid]
            if n > 0:
                self.ops.append(lambda e, s=sem, v=16 * n: e.wait_ge(s, v))


class K:
    def __init__(self):
        self.nc = bass.Bass("TRN2", target_bir_lowering=False, debug=False)
        self.es = ExitStack()
        self._uid = 0
        self.ident = None

    def build_seqs(self):
        nc, es = self.nc, self.es
        sem = lambda n: es.enter_context(nc.semaphore(n))
        gp_ops = []
        s_ops = []
        self.t = Seq("t", sem("sem_t"), [], chain=False)
        self.v = Seq("v", sem("sem_v"), [])
        self.s = Seq("s", sem("sem_s"), s_ops)
        self.g = Seq("g", sem("sem_g"), gp_ops)
        self.d = Dma("d", sem("sem_d"), [])
        self.gd = Dma("gd", sem("sem_gd"), gp_ops)
        self.sd = Dma("sd", sem("sem_sd"), s_ops)
        self.ld_sems = [sem("sem_ld0"), sem("sem_ld1"), sem("sem_ld2")]
        self.ldr_sems = [sem("sem_ldr0"), sem("sem_ldr1"), sem("sem_ldr2")]
        self.tpd_sems = [sem("sem_tpd0"), sem("sem_tpd1")]
        self.id_sem = sem("sem_id")
        self.st_sems = [sem("sem_st0"), sem("sem_st1")]
        self.gld_sems = [sem("sem_gld0"), sem("sem_gld1")]
        self.st2_sems = [sem("sem_st2a"), sem("sem_st2b")]
        self.sd_sems = [sem("sem_sda"), sem("sem_sdb")]

    def finalize(self):
        self.d.final_wait()
        self.gd.final_wait()
        self.sd.final_wait()
        with self.nc.Block() as block:
            block.tensor(lambda e: [f(e) for f in self.t.ops])
            block.vector(lambda e: [f(e) for f in self.v.ops])
            block.scalar(lambda e: [f(e) for f in self.s.ops])
            block.sync(lambda e: [f(e) for f in self.d.ops])
            block.gpsimd(lambda e: [f(e) for f in self.g.ops])

    def uname(self, base):
        self._uid += 1
        return f"{base}_{self._uid}"

    def sb(self, name, shape, dtype, scope=None):
        return (scope or self.es).enter_context(
            self.nc.sbuf_tensor(self.uname(name), shape, dtype))

    def ps(self, scope, name, shape, dtype=F32):
        return scope.enter_context(self.nc.psum_tensor(self.uname(name), shape, dtype))

    def din(self, name, shape, dtype=F32):
        return self.nc.dram_tensor(name, shape, dtype, kind="ExternalInput").ap()

    def dout(self, name, shape, dtype=F32):
        return self.nc.dram_tensor(name, shape, dtype, kind="ExternalOutput").ap()

    def dint(self, name, shape, dtype=F32):
        return self.nc.dram_tensor(name, shape, dtype).ap()

    def init_common(self, ident_in):
        self.ident = self.sb("identsb", [128, 128], BF16)
        self.d.start(self.ident[:], ident_in, sem=self.id_sem)
        self.m_ident = self.d.mark(self.id_sem)
        self.t.wait(self.m_ident)
        self.ones_r = self.sb("ones_r", [1, 128], BF16)
        self.v.memset(self.ones_r[:], 1.0)



def load_cast(k, w_hbm, sb, nchunk, width, col0=0):
    for i in range(nchunk):
        k.gd.start(sb[:, col0 + i * width:col0 + (i + 1) * width],
                   w_hbm[i * 128:(i + 1) * 128, :])
    return k.gd.mark()


def stmarks(k):
    ms = [k.d.mark(k.st_sems[0]), k.d.mark(k.st_sems[1])]
    if hasattr(k, "st2_sems"):
        ms += [k.d.mark(k.st2_sems[0]), k.d.mark(k.st2_sems[1])]
    return ms


def guard_weights(k):
    k.gd.wait(k.v.mark())
    k.gd.wait(k.s.mark())
    k.gd.wait(k.t.mark())
    k.gd.wait(stmarks(k))


def load_dequant(k, w_hbm, sb, nchunk, width, wint, scale=1.0 / 127.0):
    for i0 in range(0, nchunk, 2):
        nn = min(2, nchunk - i0)
        for j in range(nn):
            k.gd.start(wint[:, j * width:(j + 1) * width],
                       w_hbm[(i0 + j) * 128:(i0 + j + 1) * 128, :])
        m_l = k.gd.mark()
        k.v.wait(m_l)
        for j in range(nn):
            k.v.tensor_scalar(out=sb[:, (i0 + j) * width:(i0 + j + 1) * width],
                              in0=wint[:, j * width:(j + 1) * width],
                              scalar1=scale, scalar2=None, op0=ALU.mult)
        k.gd.wait(k.v.mark())
    return k.v.mark()


def pass_gemm(k, src, T, Kd, N, wsb, wcol0=0, wstride=None, bias_sb=None,
              bias_col0=0, out=None, out_dtype=BF16, epilogue=None, res_srcs=(),
              src_mark=None, wait_marks=(), src_dtype=BF16,
              store_transposed=False):
    t, v, s, d = k.t, k.v, k.s, k.d
    nk = Kd // 128
    nj = N // 512
    wstride = wstride if wstride is not None else N
    assert N % 512 == 0 and 1 <= nj <= 4
    dbl = (nj <= 2)
    ntiles = T // 128
    es = ExitStack()
    with es:
        xin = [k.sb("gx", [128, Kd], src_dtype, es) for _ in range(3)]
        xT = [k.sb("gxT", [128, Kd], BF16, es) for _ in range(2)]
        osb = [k.sb("gout", [128, N], out_dtype, es) for _ in range(2)]
        tbo = ([k.sb("gtb", [128, N], out_dtype, es) for _ in range(2)]
               if store_transposed else None)
        resb = [[k.sb("gres", [128, rs.shape[1]], dt_, es) for _ in range(3)]
                for (rs, dt_) in res_srcs]
        p_tp = [k.ps(es, "gp_tp", [128, 1024], src_dtype) for _ in range(2)]
        p_out = [k.ps(es, "gp_out", [128, N], F32) for _ in range(2 if dbl else 1)]
        identx = k.ident[:]
        m_idf = None
        if src_dtype == F32:
            identf_g = k.sb("gidf", [128, 128], F32, es)
            k.v.wait(k.m_ident)
            k.v.tensor_copy(out=identf_g[:], in_=k.ident[:])
            m_idf = k.v.mark()
            identx = identf_g[:]

        entry_st = stmarks(k)

        def load_x(i):
            if src_mark is not None:
                d.wait(src_mark)
            for m in wait_marks:
                d.wait(m)
            d.wait(entry_st)
            sl = k.ld_sems[i % 3]
            d.start(xin[i % 3][:], src[i * 128:(i + 1) * 128, :], sem=sl)
            return d.mark(sl)

        def load_res(i):
            if not res_srcs:
                return None
            d.wait(entry_st)
            sl = k.ldr_sems[i % 3]
            for ri, (rs, _) in enumerate(res_srcs):
                d.start(resb[ri][i % 3][:], rs[i * 128:(i + 1) * 128, :], sem=sl)
            return d.mark(sl)

        marks_x = {}
        marks_r = {}
        for i in range(min(3, ntiles)):
            marks_x[i] = load_x(i)
            marks_r[i] = load_res(i)

        m_mm = {}
        m_xT = {}
        m_ep = {}
        m_out_dma = [entry_st, entry_st]

        last_round = (nk + 7) // 8 - 1

        def emit_transposes(i):
            if src_dtype == BF16:
                d.wait(marks_x[i])
                if i - 2 >= 0:
                    d.wait(m_mm[i - 2])
                sl = k.tpd_sems[i % 2]
                d.start(xT[i % 2][:].rearrange("p (c f) -> p c f", c=nk),
                        xin[i % 3][:].rearrange("p (c f) -> p c f", c=nk),
                        transpose=True, sem=sl)
                m_xT[(i, last_round)] = d.mark(sl)
                return
            t.wait(marks_x[i])
            if m_idf is not None:
                t.wait(m_idf)
            for r in range((nk + 7) // 8):
                if (i - 2, r) in m_xT:
                    t.wait(m_xT[(i - 2, r)])
                if (i, r - 1) in m_xT:
                    t.wait(m_xT[(i, r - 1)])
                c0, c1 = r * 8, min(r * 8 + 8, nk)
                for c in range(c0, c1):
                    t.transpose(p_tp[i % 2][:, (c - c0) * 128:(c - c0 + 1) * 128],
                                xin[i % 3][:, c * 128:(c + 1) * 128], identx)
                m_tpr = t.mark()
                v.wait(m_tpr)
                if i - 2 >= 0:
                    v.wait(m_mm[i - 2])
                v.tensor_copy(out=xT[i % 2][:, c0 * 128:c1 * 128],
                              in_=p_tp[i % 2][:, 0:(c1 - c0) * 128])
                m_xT[(i, r)] = v.mark()

        emit_transposes(0)
        if ntiles > 1:
            emit_transposes(1)

        for it in range(ntiles):
            if it + 3 < ntiles:
                d.wait(m_xT[(it, last_round)])
                marks_x[it + 3] = load_x(it + 3)
            pb = p_out[it % 2] if dbl else p_out[0]
            t.wait(m_xT[(it, last_round)])
            gate = m_ep.get(it - 2 if dbl else it - 1)
            if gate is not None:
                t.wait(gate)
            for c in range(nk):
                st = xT[it % 2][:, c * 128:(c + 1) * 128]
                for j in range(nj):
                    t.matmul(pb[:, j * 512:(j + 1) * 512], st,
                             wsb[:, wcol0 + c * wstride + j * 512:
                                 wcol0 + c * wstride + (j + 1) * 512],
                             start=(c == 0),
                             stop=(bias_sb is None and c == nk - 1))
            if bias_sb is not None:
                for j in range(nj):
                    t.matmul(pb[:, j * 512:(j + 1) * 512], k.ones_r[0:1, :],
                             bias_sb[0:1, bias_col0 + j * 512:bias_col0 + (j + 1) * 512],
                             start=False, stop=True)
            m_mm[it] = t.mark()
            if it + 2 < ntiles:
                emit_transposes(it + 2)
            aux = {"res": [rb[it % 3][:] for rb in resb], "tile": it,
                   "m_in": [marks_x[it]] + ([marks_r[it]] if res_srcs else [])}
            if res_srcs:
                v.wait(marks_r[it])
            v.wait(m_out_dma[it % 2])
            s.wait(m_out_dma[it % 2])
            if epilogue is None:
                v.wait(m_mm[it])
                if resb:
                    v.tensor_tensor(out=osb[it % 2][:], in0=pb[:],
                                    in1=aux["res"][0], op=ALU.add)
                    for rb in aux["res"][1:]:
                        v.tensor_tensor(out=osb[it % 2][:], in0=osb[it % 2][:],
                                        in1=rb, op=ALU.add)
                else:
                    v.tensor_copy(out=osb[it % 2][:], in_=pb[:])
                m_ep[it] = k.v.mark()
            else:
                m_ep[it] = epilogue(k, m_mm[it], pb[:], osb[it % 2][:], aux)
            d.wait(m_ep[it])
            if store_transposed:
                nkN = N // 128
                d.wait(m_out_dma[it % 2])
                d.start(tbo[it % 2][:].rearrange("p (c f) -> p c f", c=nkN),
                        osb[it % 2][:].rearrange("p (c f) -> p c f", c=nkN),
                        transpose=True, sem=k.st2_sems[it % 2])
                m_tb = d.mark(k.st2_sems[it % 2])
                d.wait(m_tb)
                d.start(out[:, it * 128:(it + 1) * 128]
                        .rearrange("(c p) f -> p c f", c=nkN),
                        tbo[it % 2][:].rearrange("p (c f) -> p c f", c=nkN),
                        sem=k.st_sems[it % 2])
            else:
                d.start(out[it * 128:(it + 1) * 128, :], osb[it % 2][:],
                        sem=k.st_sems[it % 2])
            m_out_dma[it % 2] = d.mark(k.st_sems[it % 2])
            if it + 3 < ntiles:
                marks_r[it + 3] = load_res(it + 3)
    return stmarks(k)


def emit_ln_core(k, x_ap, width, sm, scr_ap, rstd_ap, want_var=False):
    v, s = k.v, k.s
    v.tensor_reduce(out=sm[:, 0:1], in_=x_ap, axis=AX.X, op=ALU.add)
    v.tensor_scalar(out=sm[:, 0:1], in0=sm[:, 0:1], scalar1=1.0 / width,
                    scalar2=None, op0=ALU.mult)
    s.wait(k.v.mark())
    s.activation(out=scr_ap, in_=x_ap, func=AF.Square, accum_out=sm[:, 1:2])
    v.wait(k.s.mark())
    v.tensor_scalar(out=sm[:, 2:3], in0=sm[:, 1:2], scalar1=1.0 / width,
                    scalar2=None, op0=ALU.mult)
    v.tensor_tensor(out=sm[:, 3:4], in0=sm[:, 0:1], in1=sm[:, 0:1], op=ALU.mult)
    v.tensor_tensor(out=sm[:, 2:3], in0=sm[:, 2:3], in1=sm[:, 3:4], op=ALU.subtract)
    v.tensor_scalar(out=sm[:, 6:7], in0=sm[:, 2:3], scalar1=EPS, scalar2=None,
                    op0=ALU.add)
    s.wait(k.v.mark())
    s.activation(out=rstd_ap, in_=sm[:, 6:7], func=AF.Sqrt)
    v.wait(k.s.mark())
    v.reciprocal(out=rstd_ap, in_=rstd_ap)
    return k.v.mark()



def build_launch_A(T=TPC):
    k = K()
    ntiles = T // 128

    x0 = k.din("x0", [T, D])
    wq = k.din("wq", [D, D]); wk = k.din("wk", [D, D]); wv = k.din("wv", [D, D])
    woH = k.din("woH", [64, 16 * D])
    fcw = k.din("fcw", [D, MLP0]); projw = k.din("projw", [MLP0, D])
    wgT = k.din("wgT", [D, FFN0], I32); wuT = k.din("wuT", [D, FFN0], I32)
    wdT = k.din("wdT", [FFN0, D], I32)
    g1b = k.din("g1b", [128, D]); b1b = k.din("b1b", [128, D])
    g2b = k.din("g2b", [128, D]); b2b = k.din("b2b", [128, D])
    bq_r = k.din("bq_r", [1, D], BF16); bk_r = k.din("bk_r", [1, D], BF16)
    bv_r = k.din("bv_r", [1, D], BF16); bo_r = k.din("bo_r", [1, D], BF16)
    fcb_r = k.din("fcb_r", [1, MLP0], BF16); projb_r = k.din("projb_r", [1, D], BF16)
    gatew = k.din("gatew", [D, 65])
    identb = k.din("identb", [128, 128], BF16)
    bdm = k.din("bdm", [128, 128])
    iota64 = k.din("iota64", [128, 64])

    o_x1 = k.dout("x1", [T, D])
    o_n1 = k.dout("n1", [T, D], BF16)
    o_idx8 = k.dout("idx8", [T, TOPK])
    o_w8 = k.dout("w8", [T, TOPK])
    o_counts = k.dout("counts", [1, 64])
    o_sgate = k.dout("sgate", [T, 1])

    n0f_dram = k.dint("n0f_dram", [T, D])
    n0b_dram = k.dint("n0b_dram", [T, D], BF16)
    xn2_dram = k.dint("xn2_dram", [T, D], BF16)
    fq_dram = k.dint("fq_dram", [T, D], BF16)
    fk_dram = k.dint("fk_dram", [T, D], BF16)
    v_dram = k.dint("v_dram", [T, D], BF16)
    ea_dram = k.dint("ea_dram", [ntiles * 64, 2048], BF16)
    a_dram = k.dint("a_dram", [T, D])
    t2_dram = k.dint("t2_dram", [T, D], BF16)
    gg_dram = k.dint("gg_dram", [T, MLP0], BF16)
    brA_dram = k.dint("brA_dram", [T, D])
    gu_dram = k.dint("gu_dram", [T, FFN0], BF16)
    sg_dram = k.dint("sg_dram", [T, FFN0], BF16)

    with k.es:
        k.build_seqs()
        t, v, s, g, d, gd = k.t, k.v, k.s, k.g, k.d, k.gd
        k.init_common(identb)

        bd_sb = k.sb("bd", [128, 128], F32)
        iota_sb = k.sb("iota64s", [128, 64], F32)
        g1_sb = k.sb("g1s", [128, D], F32); b1_sb = k.sb("b1s", [128, D], F32)
        g2_sb = k.sb("g2s", [128, D], F32); b2_sb = k.sb("b2s", [128, D], F32)
        bias_q = k.sb("biasq", [1, D], BF16); bias_k = k.sb("biask", [1, D], BF16)
        bias_v = k.sb("biasv", [1, D], BF16); bias_o = k.sb("biaso", [1, D], BF16)
        bias_fc = k.sb("biasfc", [1, MLP0], BF16); bias_pj = k.sb("biaspj", [1, D], BF16)
        for dst, srcin in ((bd_sb, bdm), (iota_sb, iota64), (g1_sb, g1b), (b1_sb, b1b),
                           (g2_sb, g2b), (b2_sb, b2b), (bias_q, bq_r), (bias_k, bk_r),
                           (bias_v, bv_r), (bias_o, bo_r), (bias_fc, fcb_r),
                           (bias_pj, projb_r)):
            d.start(dst[:], srcin)
        m_const = d.mark()
        v.wait(m_const)
        eluscr = k.sb("eluscr", [128, D], F32)
        ln9 = {}
        ln9["scr"] = k.sb("p9scr", [128, D], F32)
        ln9["sm"] = k.sb("p9sm", [128, 8], F32)
        ln9["rstd"] = k.sb("p9rstd", [128, 1], F32)
        ln9["n1b"] = [k.sb("p9n1b", [128, D], BF16) for _ in range(2)]
        ln9["m_n1dma"] = [None, None]

        with ExitStack() as es1:
            xin = [k.sb("p1x", [128, D], F32, es1) for _ in range(2)]
            n0f = k.sb("p1n0f", [128, D], F32, es1)
            n0b = k.sb("p1n0b", [128, D], BF16, es1)
            xn2 = k.sb("p1xn2", [128, D], BF16, es1)
            scr = k.sb("p1scr", [128, D], F32, es1)
            sm = k.sb("p1sm", [128, 8], F32, es1)
            rstd = k.sb("p1rstd", [128, 2], F32, es1)
            entry_st1 = stmarks(k)
            d.wait(entry_st1)
            d.start(xin[0][:], x0[0:128, :], sem=k.ld_sems[0])
            mk = [d.mark(k.ld_sems[0])]
            if ntiles > 1:
                d.start(xin[1][:], x0[128:256, :], sem=k.ld_sems[1])
                mk.append(d.mark(k.ld_sems[1]))
            m_st = [None, None]
            for it in range(ntiles):
                xs = xin[it % 2]
                v.wait(mk[it])
                if it >= 1 and m_st[(it - 1) % 2] is not None:
                    v.wait(m_st[(it - 1) % 2])
                    s.wait(m_st[(it - 1) % 2])
                emit_ln_core(k, xs[:], D, sm, scr[:], rstd[:, 0:1])
                v.tensor_scalar(out=n0f[:], in0=xs[:], scalar1=sm[:, 0:1],
                                scalar2=rstd[:, 0:1], op0=ALU.subtract, op1=ALU.mult)
                v.tensor_copy(out=n0b[:], in_=n0f[:])
                v.tensor_scalar(out=sm[:, 4:5], in0=sm[:, 2:3], scalar1=EPS,
                                scalar2=None, op0=ALU.add)
                v.reciprocal(out=sm[:, 5:6], in_=sm[:, 4:5])
                v.tensor_tensor(out=sm[:, 5:6], in0=sm[:, 2:3], in1=sm[:, 5:6],
                                op=ALU.mult)
                v.tensor_scalar(out=sm[:, 5:6], in0=sm[:, 5:6], scalar1=EPS,
                                scalar2=None, op0=ALU.add)
                s.wait(v.mark())
                s.activation(out=rstd[:, 1:2], in_=sm[:, 5:6], func=AF.Sqrt)
                v.wait(s.mark())
                v.reciprocal(out=rstd[:, 1:2], in_=rstd[:, 1:2])
                v.tensor_scalar(out=scr[:], in0=n0f[:], scalar1=rstd[:, 1:2],
                                scalar2=None, op0=ALU.mult)
                v.tensor_tensor(out=scr[:], in0=scr[:], in1=g1_sb[:], op=ALU.mult)
                v.tensor_tensor(out=xn2[:], in0=scr[:], in1=b1_sb[:], op=ALU.add)
                m_v = v.mark()
                d.wait(m_v)
                ss = k.st_sems[it % 2]
                d.start(n0f_dram[it * 128:(it + 1) * 128, :], n0f[:], sem=ss)
                d.start(n0b_dram[it * 128:(it + 1) * 128, :], n0b[:], sem=ss)
                d.start(xn2_dram[it * 128:(it + 1) * 128, :], xn2[:], sem=ss)
                m_st[it % 2] = d.mark(ss)
                if it + 2 < ntiles:
                    d.wait(m_v)
                    d.start(xin[it % 2][:], x0[(it + 2) * 128:(it + 3) * 128, :],
                            sem=k.ld_sems[it % 2])
                    mk.append(d.mark(k.ld_sems[it % 2]))
            m_p1 = [d.mark(k.st_sems[0]), d.mark(k.st_sems[1])]

        es2w = ExitStack()
        wq_sb = k.sb("wq_sb", [128, 8 * D], BF16, es2w)
        wk_sb = k.sb("wk_sb", [128, 8 * D], BF16, es2w)
        wv_sb = k.sb("wv_sb", [128, 8 * D], BF16, es2w)
        guard_weights(k)
        load_cast(k, wq, wq_sb, 8, D)
        load_cast(k, wk, wk_sb, 8, D)
        load_cast(k, wv, wv_sb, 8, D)
        m_wqkv = gd.mark()
        t.wait(m_wqkv)
        t.wait(m_const)

        def mk_elu_epi():
            scr1 = eluscr

            def epi(k2, m_mm, psum_ap, out_ap, aux):
                v, s = k2.v, k2.s
                v.wait(m_mm)
                v.tensor_scalar(out=scr1[:], in0=psum_ap, scalar1=0.0, scalar2=None,
                                op0=ALU.min)
                s.wait(v.mark())
                s.activation(out=out_ap, in_=scr1[:], func=AF.Exp)
                v.wait(s.mark())
                v.tensor_scalar(out=scr1[:], in0=psum_ap, scalar1=0.0, scalar2=None,
                                op0=ALU.max)
                v.tensor_tensor(out=out_ap, in0=out_ap, in1=scr1[:], op=ALU.add)
                return k2.v.mark()

            return epi

        pass_gemm(k, xn2_dram, T, D, D, wq_sb, bias_sb=bias_q, out=fq_dram,
                  epilogue=mk_elu_epi(), src_mark=m_p1)
        pass_gemm(k, xn2_dram, T, D, D, wk_sb, bias_sb=bias_k, out=fk_dram,
                  epilogue=mk_elu_epi(), src_mark=m_p1)
        m_p2v = pass_gemm(k, xn2_dram, T, D, D, wv_sb, bias_sb=bias_v, out=v_dram,
                          src_mark=m_p1, out_dtype=BF16)
        m_p2 = stmarks(k)
        es2w.close()

        with ExitStack() as es3:
            fqi = [k.sb("p3fq", [128, D], BF16, es3) for _ in range(2)]
            fki = [k.sb("p3fk", [128, D], BF16, es3) for _ in range(2)]
            vvi = [k.sb("p3v", [128, D], BF16, es3) for _ in range(2)]
            FK = k.sb("p3FK", [64, 16 * 128], BF16, es3)
            FQ = k.sb("p3FQ", [64, 16 * 128], BF16, es3)
            VE = k.sb("p3VE", [64, 16 * 128], BF16, es3)
            Mmask = [k.sb("p3Mm", [128, 128], BF16, es3) for _ in range(2)]
            V8o = [k.sb("p3V8", [128, 72], BF16, es3) for _ in range(2)]
            r8 = k.sb("p3r8", [128, 1], F32, es3)
            attn = k.sb("p3attn", [128, 16 * 64], BF16, es3)
            EA = k.sb("p3EA", [64, 2048], BF16, es3)
            p_hT = k.ps(es3, "p3phT", [64, 2048], BF16)
            p_M = [k.ps(es3, "p3pM", [128, 128], F32) for _ in range(2)]
            p_v8 = [k.ps(es3, "p3pv8", [128, 64], BF16) for _ in range(2)]
            p_qkv = [k.ps(es3, "p3pqkv", [128, 72], F32) for _ in range(2)]
            p_eT = p_hT

            v.memset(V8o[0][:], 0.0)
            v.memset(V8o[1][:], 0.0)
            v.memset(V8o[0][:, 64:65], 1.0)
            v.memset(V8o[1][:, 64:65], 1.0)

            entry_st3 = stmarks(k)

            def load3(i):
                d.wait(m_p2)
                d.wait(entry_st3)
                sl = k.ld_sems[i % 2]
                d.start(fqi[i % 2][:], fq_dram[i * 128:(i + 1) * 128, :], sem=sl)
                d.start(fki[i % 2][:], fk_dram[i * 128:(i + 1) * 128, :], sem=sl)
                d.start(vvi[i % 2][:], v_dram[i * 128:(i + 1) * 128, :], sem=sl)
                return d.mark(sl)

            mk3 = [load3(0)]
            if ntiles > 1:
                mk3.append(load3(1))

            m_head_evac = None
            m_mask = [None, None]
            m_v8e = [None, None]
            m_qke = [None, None]
            m_ea_dma = None
            m_eT_prev = None
            for it in range(ntiles):
                t.wait(mk3[it])
                if m_ea_dma is not None:
                    t.wait(m_ea)
                for (srcb, dstb) in ((fki[it % 2], FK), (fqi[it % 2], FQ),
                                     (vvi[it % 2], VE)):
                    if m_head_evac is not None:
                        t.wait(m_head_evac)
                    for h in range(16):
                        t.transpose(p_hT[:, h * 128:(h + 1) * 128],
                                    srcb[:, h * 64:(h + 1) * 64], k.ident[:])
                    m_hT = t.mark()
                    v.wait(m_hT)
                    src_ap = p_hT[:].rearrange("p (h tk) -> p tk h", h=16)
                    dst_ap = dstb[:].rearrange("p (tk h) -> p tk h", h=16)
                    if it > 0:
                        v.wait(m_groups_done)
                    v.tensor_copy(out=dst_ap, in_=src_ap)
                    m_head_evac = v.mark()

                for gi in range(17):
                    pp = gi % 2
                    if gi < 16:
                        if m_mask[pp] is not None:
                            t.wait(m_mask[pp])
                        t.wait(m_head_evac)
                        t.matmul(p_M[pp][:], FK[:, gi * 128:(gi + 1) * 128],
                                 FQ[:, gi * 128:(gi + 1) * 128], start=True, stop=True)
                        if m_v8e[pp] is not None:
                            t.wait(m_v8e[pp])
                        t.transpose(p_v8[pp][:], VE[:, gi * 128:(gi + 1) * 128],
                                    k.ident[0:64, 0:64])
                        m_pe = t.mark()
                        v.wait(m_pe)
                        v.tensor_tensor(out=Mmask[pp][:], in0=p_M[pp][:], in1=bd_sb[:],
                                        op=ALU.mult)
                        m_mask[pp] = v.mark()
                        v.tensor_copy(out=V8o[pp][:, 0:64], in_=p_v8[pp][:])
                        m_v8e[pp] = v.mark()
                    if gi >= 1:
                        gj = gi - 1
                        qq = gj % 2
                        t.wait(m_mask[qq])
                        t.wait(m_v8e[qq])
                        if m_qke[qq] is not None:
                            t.wait(m_qke[qq])
                        t.matmul(p_qkv[qq][:], Mmask[qq][:], V8o[qq][:],
                                 start=True, stop=True)
                        m_qk = t.mark()
                        v.wait(m_qk)
                        if gj == 0 and m_eT_prev is not None:
                            v.wait(m_eT_prev)
                        v.tensor_scalar(out=r8[:], in0=p_qkv[qq][:, 64:65],
                                        scalar1=1e-6, scalar2=None, op0=ALU.add)
                        v.reciprocal(out=r8[:], in_=r8[:])
                        v.tensor_scalar(out=attn[:, gj * 64:(gj + 1) * 64],
                                        in0=p_qkv[qq][:, 0:64], scalar1=r8[:],
                                        scalar2=None, op0=ALU.mult)
                        m_qke[qq] = v.mark()
                m_groups_done = k.v.mark()

                t.wait(m_groups_done)
                if m_ea_dma is not None:
                    t.wait(m_ea)
                for gi2 in range(16):
                    t.transpose(p_eT[:, gi2 * 128:(gi2 + 1) * 128],
                                attn[:, gi2 * 64:(gi2 + 1) * 64], k.ident[:])
                m_eT = t.mark()
                m_eT_prev = m_eT
                v.wait(m_eT)
                if m_ea_dma is not None:
                    v.wait(m_ea_dma)
                src_ap = p_eT[:].rearrange("p (gg tk h) -> p h gg tk", gg=16, tk=8)
                dst_ap = EA[:].rearrange("p (h gg tk) -> p h gg tk", h=16, gg=16)
                v.tensor_copy(out=dst_ap, in_=src_ap)
                m_ea = v.mark()
                d.wait(m_ea)
                d.start(ea_dram[it * 64:(it + 1) * 64, :], EA[:], sem=k.st_sems[0])
                m_ea_dma = d.mark(k.st_sems[0])
                if it + 2 < ntiles:
                    t_ok = k.t.mark()
                    d.wait(t_ok)
                    mk3.append(load3(it + 2))
            m_p3 = k.d.mark(k.st_sems[0])

        es4w = ExitStack()
        woH_sb = k.sb("woH_sb", [64, 16 * D], BF16, es4w)
        guard_weights(k)
        gd.start(woH_sb[:], woH)
        m_wo = gd.mark()
        with ExitStack() as es4:
            eain = [k.sb("p4ea", [64, 2048], BF16, es4) for _ in range(2)]
            resb = [k.sb("p4res", [128, D], F32, es4) for _ in range(2)]
            osb = [k.sb("p4o", [128, D], F32, es4) for _ in range(2)]
            p_out = k.ps(es4, "p4pout", [128, D], F32)

            entry_st4 = stmarks(k)

            def load4(i):
                d.wait(entry_st4)
                d.wait(m_p1)
                sl = k.ld_sems[i % 2]
                d.start(eain[i % 2][:], ea_dram[i * 64:(i + 1) * 64, :], sem=sl)
                d.start(resb[i % 2][:], n0f_dram[i * 128:(i + 1) * 128, :], sem=sl)
                return d.mark(sl)

            mk4 = [load4(0)]
            if ntiles > 1:
                mk4.append(load4(1))
            t.wait(m_wo)
            m_ep = None
            m_odma = [None, None]
            for it in range(ntiles):
                t.wait(mk4[it])
                if m_ep is not None:
                    t.wait(m_ep)
                for h in range(16):
                    st = eain[it % 2][:, h * 128:(h + 1) * 128]
                    t.matmul(p_out[:, 0:512], st, woH_sb[:, h * D:h * D + 512],
                             start=(h == 0), stop=False)
                    t.matmul(p_out[:, 512:1024], st, woH_sb[:, h * D + 512:(h + 1) * D],
                             start=(h == 0), stop=False)
                t.matmul(p_out[:, 0:512], k.ones_r[0:1, :], bias_o[0:1, 0:512],
                         start=False, stop=True)
                t.matmul(p_out[:, 512:1024], k.ones_r[0:1, :], bias_o[0:1, 512:1024],
                         start=False, stop=True)
                m_mm = t.mark()
                v.wait(m_mm)
                if m_odma[it % 2] is not None:
                    v.wait(m_odma[it % 2])
                v.tensor_tensor(out=osb[it % 2][:], in0=p_out[:], in1=resb[it % 2][:],
                                op=ALU.add)
                m_ep = v.mark()
                d.wait(m_ep)
                d.start(a_dram[it * 128:(it + 1) * 128, :], osb[it % 2][:],
                        sem=k.st_sems[it % 2])
                m_odma[it % 2] = d.mark(k.st_sems[it % 2])
                if it + 2 < ntiles:
                    d.wait(m_mm)
                    d.wait(m_ep)
                    mk4.append(load4(it + 2))
            m_p4 = [k.d.mark(k.st_sems[0]), k.d.mark(k.st_sems[1])]
        es4w.close()

        with ExitStack() as es5:
            ain = [k.sb("p5a", [128, D], F32, es5) for _ in range(2)]
            t2 = k.sb("p5t2", [128, D], BF16, es5)
            scr = k.sb("p5scr", [128, D], F32, es5)
            sm = k.sb("p5sm", [128, 8], F32, es5)
            rstd = k.sb("p5rstd", [128, 1], F32, es5)
            entry_st5 = stmarks(k)
            d.wait(entry_st5)
            d.start(ain[0][:], a_dram[0:128, :], sem=k.ld_sems[0])
            mk5 = [d.mark(k.ld_sems[0])]
            if ntiles > 1:
                d.start(ain[1][:], a_dram[128:256, :], sem=k.ld_sems[1])
                mk5.append(d.mark(k.ld_sems[1]))
            m_st = [None, None]
            for it in range(ntiles):
                xs = ain[it % 2]
                v.wait(mk5[it])
                if it >= 1 and m_st[(it - 1) % 2] is not None:
                    v.wait(m_st[(it - 1) % 2])
                    s.wait(m_st[(it - 1) % 2])
                emit_ln_core(k, xs[:], D, sm, scr[:], rstd[:])
                v.tensor_scalar(out=scr[:], in0=xs[:], scalar1=sm[:, 0:1],
                                scalar2=rstd[:], op0=ALU.subtract, op1=ALU.mult)
                v.tensor_tensor(out=scr[:], in0=scr[:], in1=g2_sb[:], op=ALU.mult)
                v.tensor_tensor(out=t2[:], in0=scr[:], in1=b2_sb[:], op=ALU.add)
                m_v = v.mark()
                d.wait(m_v)
                d.start(t2_dram[it * 128:(it + 1) * 128, :], t2[:],
                        sem=k.st_sems[it % 2])
                m_st[it % 2] = d.mark(k.st_sems[it % 2])
                if it + 2 < ntiles:
                    d.wait(m_v)
                    d.start(ain[it % 2][:], a_dram[(it + 2) * 128:(it + 3) * 128, :],
                            sem=k.ld_sems[it % 2])
                    mk5.append(d.mark(k.ld_sems[it % 2]))
            m_p5 = [d.mark(k.st_sems[0]), d.mark(k.st_sems[1])]

        es6w = ExitStack()
        fc_sb = k.sb("fc_sb", [128, 8 * MLP0], BF16, es6w)
        guard_weights(k)
        load_cast(k, fcw, fc_sb, 8, MLP0)
        m_wfc = gd.mark()
        t.wait(m_wfc)

        def gelu_epi(k2, m_mm, psum_ap, out_ap, aux):
            k2.s.wait(m_mm)
            k2.s.activation(out=out_ap, in_=psum_ap, func=AF.Gelu)
            return k2.s.mark()

        for half in range(2):
            pass_gemm(k, t2_dram, T, D, 2048, fc_sb, wcol0=half * 2048, wstride=MLP0,
                      bias_sb=bias_fc, bias_col0=half * 2048,
                      out=gg_dram[:, half * 2048:half * 2048 + 2048],
                      epilogue=gelu_epi, src_mark=m_p5)
        m_p6 = stmarks(k)
        es6w.close()

        es7w = ExitStack()
        pj_sb = k.sb("pj_sb", [128, 32 * D], BF16, es7w)
        guard_weights(k)
        load_cast(k, projw, pj_sb, 32, D)
        m_wpj = gd.mark()
        t.wait(m_wpj)
        pass_gemm(k, gg_dram, T, MLP0, D, pj_sb, bias_sb=bias_pj, out=brA_dram,
                  out_dtype=F32, res_srcs=[(a_dram, F32)], src_mark=m_p6)
        m_p7 = stmarks(k)
        es7w.close()

        es8w = ExitStack()
        wg_sb = k.sb("wg_sb", [128, 8 * FFN0], BF16, es8w)
        wu_sb = k.sb("wu_sb", [128, 8 * FFN0], BF16, es8w)
        wint = k.sb("wint", [128, 2 * FFN0], I32, es8w)
        guard_weights(k)
        load_dequant(k, wgT, wg_sb, 8, FFN0, wint)
        m_wg = k.v.mark()
        load_dequant(k, wuT, wu_sb, 8, FFN0, wint)
        m_wu = k.v.mark()
        t.wait(m_wg)
        t.wait(m_wu)

        def silu_epi(k2, m_mm, psum_ap, out_ap, aux):
            k2.s.wait(m_mm)
            k2.s.activation(out=out_ap, in_=psum_ap, func=AF.Silu)
            return k2.s.mark()

        for half in range(2):
            pass_gemm(k, n0b_dram, T, D, 1536, wg_sb, wcol0=half * 1536, wstride=FFN0,
                      out=sg_dram[:, half * 1536:half * 1536 + 1536],
                      epilogue=silu_epi, src_mark=m_p1)
        m_p8a = stmarks(k)

        def upmul_epi(k2, m_mm, psum_ap, out_ap, aux):
            k2.v.wait(m_mm)
            k2.v.tensor_tensor(out=out_ap, in0=psum_ap, in1=aux["res"][0], op=ALU.mult)
            return k2.v.mark()

        for half in range(2):
            pass_gemm(k, n0b_dram, T, D, 1536, wu_sb, wcol0=half * 1536, wstride=FFN0,
                      out=gu_dram[:, half * 1536:half * 1536 + 1536],
                      epilogue=upmul_epi,
                      res_srcs=[(sg_dram[:, half * 1536:half * 1536 + 1536], BF16)],
                      src_mark=m_p8a)
        m_p8 = stmarks(k)
        es8w.close()

        es9w = ExitStack()
        wd_sb = k.sb("wd_sb", [128, 24 * D], BF16, es9w)
        wint9 = k.sb("wint9", [128, 2 * D], I32, es9w)
        guard_weights(k)
        load_dequant(k, wdT, wd_sb, 24, D, wint9)
        m_wd = k.v.mark()
        t.wait(m_wd)

        def x1n1_epi(k2, m_mm, psum_ap, out_ap, aux):
            v, s, d = k2.v, k2.s, k2.d
            it = aux["tile"]
            v.wait(m_mm)
            v.tensor_tensor(out=out_ap, in0=psum_ap, in1=aux["res"][0], op=ALU.add)
            v.tensor_tensor(out=out_ap, in0=out_ap, in1=aux["res"][1], op=ALU.add)
            if ln9["m_n1dma"][it % 2] is not None:
                v.wait(ln9["m_n1dma"][it % 2])
                s.wait(ln9["m_n1dma"][it % 2])
            emit_ln_core(k2, out_ap, D, ln9["sm"], ln9["scr"][:], ln9["rstd"][:])
            v.tensor_scalar(out=ln9["n1b"][it % 2][:], in0=out_ap,
                            scalar1=ln9["sm"][:, 0:1], scalar2=ln9["rstd"][:],
                            op0=ALU.subtract, op1=ALU.mult)
            m_v = k2.v.mark()
            d.wait(m_v)
            d.start(o_n1[it * 128:(it + 1) * 128, :], ln9["n1b"][it % 2][:],
                    sem=k2.st2_sems[it % 2])
            ln9["m_n1dma"][it % 2] = k2.d.mark(k2.st2_sems[it % 2])
            return m_v

        pass_gemm(k, gu_dram, T, FFN0, D, wd_sb, out=o_x1, out_dtype=F32,
                  epilogue=x1n1_epi,
                  res_srcs=[(brA_dram, F32), (x0, F32)], src_mark=m_p8)
        m_p9 = stmarks(k)
        es9w.close()

        gate_sb = k.sb("gate_sbuf", [128, 8 * 65], F32)
        guard_weights(k)
        load_cast(k, gatew, gate_sb, 8, 65)
        m_gw = gd.mark()
        t.wait(m_gw)
        with ExitStack() as es11:
            nin = [k.sb("p11n", [128, D], F32, es11) for _ in range(2)]
            nT = k.sb("p11nT", [128, D], F32, es11)
            n1f11 = k.sb("p11n1f", [128, D], F32, es11)
            scr11 = k.sb("p11scr", [128, D], F32, es11)
            sm11b = k.sb("p11smb", [128, 8], F32, es11)
            rstd11 = k.sb("p11rstd", [128, 1], F32, es11)
            probs = k.sb("p11probs", [128, 64], F32, es11)
            pm8 = k.sb("p11pm8", [128, 8], F32, es11)
            idx8s = [k.sb("p11idx", [128, 8], F32, es11) for _ in range(2)]
            w8s = [k.sb("p11w8", [128, 8], F32, es11) for _ in range(2)]
            used = k.sb("p11used", [128, 64], F32, es11)
            cand = k.sb("p11cand", [128, 64], F32, es11)
            eqm = k.sb("p11eqm", [128, 64], F32, es11)
            ohe = k.sb("p11ohe", [128, 8 * 64], F32, es11)
            ohr = k.sb("p11ohr", [128, 64], F32, es11)
            counts_sb = k.sb("p11cnt", [1, 64], F32, es11)
            sm11 = k.sb("p11sm", [128, 4], F32, es11)
            onec = k.sb("p11onec", [128, 1], F32, es11)
            p_tp11 = k.ps(es11, "p11ptp", [128, 1024], F32)
            p_gate = k.ps(es11, "p11pg", [128, 65], F32)
            sgt = [k.sb("p11sgt", [128, 1], F32, es11) for _ in range(2)]
            p_cnt = k.ps(es11, "p11pc", [1, 64], F32)
            v.memset(onec[:], 1.0)

            entry_st11 = stmarks(k)

            def load11(i):
                d.wait(entry_st11)
                d.wait(m_p9)
                sl = k.ld_sems[i % 2]
                d.start(nin[i % 2][:], o_x1[i * 128:(i + 1) * 128, :], sem=sl)
                return d.mark(sl)

            mk11 = [load11(0)]
            if ntiles > 1:
                mk11.append(load11(1))
            m_xT = None
            m_gmm_prev = None
            m_soft_prev = None
            m_odma = [None, None]
            m_oh_prev = None
            identf = k.sb("p11identf", [128, 128], F32, es11)
            v.wait(k.m_ident)
            v.tensor_copy(out=identf[:], in_=k.ident[:])
            m_idf11 = k.v.mark()
            for it in range(ntiles):
                v.wait(mk11[it])
                if m_xT is not None:
                    v.wait(m_xT)
                emit_ln_core(k, nin[it % 2][:], D, sm11b, scr11[:], rstd11[:])
                v.tensor_scalar(out=n1f11[:], in0=nin[it % 2][:],
                                scalar1=sm11b[:, 0:1], scalar2=rstd11[:],
                                op0=ALU.subtract, op1=ALU.mult)
                m_n1f = v.mark()
                t.wait(m_n1f)
                if m_xT is not None:
                    t.wait(m_xT)
                for c in range(8):
                    t.transpose(p_tp11[:, c * 128:(c + 1) * 128],
                                n1f11[:, c * 128:(c + 1) * 128], identf[:])
                m_tp = t.mark()
                v.wait(m_tp)
                if m_gmm_prev is not None:
                    v.wait(m_gmm_prev)
                v.tensor_copy(out=nT[:], in_=p_tp11[:])
                m_xT = v.mark()
                t.wait(m_xT)
                if m_soft_prev is not None:
                    t.wait(m_soft_prev)
                for c in range(8):
                    t.matmul(p_gate[:], nT[:, c * 128:(c + 1) * 128],
                             gate_sb[:, c * 65:(c + 1) * 65],
                             start=(c == 0), stop=(c == 7))
                m_gmm = t.mark()
                m_gmm_prev = m_gmm
                v.wait(m_gmm)
                v.tensor_reduce(out=sm11[:, 0:1], in_=p_gate[:, 0:64], axis=AX.X,
                                op=ALU.max, negate=True)
                s.wait(v.mark())
                if m_odma[it % 2] is not None:
                    s.wait(m_odma[it % 2])
                s.activation(out=probs[:], in_=p_gate[:, 0:64], func=AF.Exp,
                             bias=sm11[:, 0:1])
                s.activation(out=sgt[it % 2][:], in_=p_gate[:, 64:65], func=AF.Sigmoid)
                m_soft = s.mark()
                m_soft_prev = m_soft
                v.wait(m_soft)
                v.tensor_reduce(out=sm11[:, 1:2], in_=probs[:], axis=AX.X, op=ALU.add)
                v.reciprocal(out=sm11[:, 1:2], in_=sm11[:, 1:2])
                v.tensor_scalar(out=probs[:], in0=probs[:], scalar1=sm11[:, 1:2],
                                scalar2=None, op0=ALU.mult)
                if m_odma[it % 2] is not None:
                    v.wait(m_odma[it % 2])
                v.max(out=pm8[:], in_=probs[:])
                v.memset(used[:], 0.0)
                for j in range(TOPK):
                    v.tensor_scalar(out=eqm[:], in0=probs[:], scalar1=pm8[:, j:j + 1],
                                    scalar2=None, op0=ALU.is_equal)
                    v.scalar_tensor_tensor(out=cand[:], in0=eqm[:], scalar=-65536.0,
                                           in1=iota_sb[:], op0=ALU.mult, op1=ALU.add)
                    v.tensor_scalar(out=cand[:], in0=cand[:], scalar1=65536.0,
                                    scalar2=None, op0=ALU.add)
                    v.scalar_tensor_tensor(out=cand[:], in0=used[:], scalar=65536.0,
                                           in1=cand[:], op0=ALU.mult, op1=ALU.add)
                    v.tensor_reduce(out=idx8s[it % 2][:, j:j + 1], in_=cand[:],
                                    axis=AX.X, op=ALU.min)
                    v.tensor_scalar(out=eqm[:], in0=iota_sb[:],
                                    scalar1=idx8s[it % 2][:, j:j + 1],
                                    scalar2=None, op0=ALU.is_equal)
                    v.tensor_tensor(out=used[:], in0=used[:], in1=eqm[:], op=ALU.max)
                v.tensor_reduce(out=sm11[:, 2:3], in_=pm8[:], axis=AX.X, op=ALU.add)
                v.reciprocal(out=sm11[:, 2:3], in_=sm11[:, 2:3])
                v.tensor_scalar(out=w8s[it % 2][:], in0=pm8[:], scalar1=sm11[:, 2:3],
                                scalar2=None, op0=ALU.mult)
                m_top8 = v.mark()
                d.wait(m_top8)
                d.start(o_idx8[it * 128:(it + 1) * 128, :], idx8s[it % 2][:],
                        sem=k.st_sems[it % 2])
                d.start(o_w8[it * 128:(it + 1) * 128, :], w8s[it % 2][:],
                        sem=k.st_sems[it % 2])
                d.wait(m_soft)
                d.start(o_sgate[it * 128:(it + 1) * 128, :], sgt[it % 2][:],
                        sem=k.st_sems[it % 2])
                m_odma[it % 2] = d.mark(k.st_sems[it % 2])
                if m_oh_prev is not None:
                    v.wait(m_oh_prev)
                ioh = iota_sb[:].unsqueeze(1).to_broadcast([128, 8, 64])
                idq = idx8s[it % 2][:].unsqueeze(2).to_broadcast([128, 8, 64])
                v.tensor_tensor(out=ohe[:].rearrange("p (j e) -> p j e", j=8),
                                in0=idq, in1=ioh, op=ALU.is_equal)
                v.tensor_reduce(out=ohr[:],
                                in_=ohe[:].rearrange("p (j e) -> p e j", j=8),
                                axis=AX.X, op=ALU.add)
                m_oh = v.mark()
                t.wait(m_oh)
                t.matmul(p_cnt[:], onec[:, 0:1], ohr[:], start=(it == 0),
                         stop=(it == ntiles - 1))
                m_oh_prev = t.mark()
                if it + 2 < ntiles:
                    d.wait(m_xT)
                    mk11.append(load11(it + 2))
            v.wait(m_oh_prev)
            v.tensor_copy(out=counts_sb[:], in_=p_cnt[:])
            m_cfin = v.mark()
            d.wait(m_cfin)
            d.start(o_counts, counts_sb[:])

        k.finalize()
    return k



def _bf(x):
    return np.asarray(x, dtype=bfdt)


def prep_A_inputs(core, full):
    b, half = core // 2, core % 2
    x0 = np.ascontiguousarray(full["L0_attn_q_x"][b, half * TPC:(half + 1) * TPC, :])
    eye = np.eye(128, dtype=np.float32)
    bdm = np.kron(np.eye(8, dtype=np.float32), np.ones((16, 16), np.float32))
    iota = np.broadcast_to(np.arange(64, dtype=np.float32), (128, 64)).copy()
    woH = np.ascontiguousarray(
        full["L0_attn_wo"].reshape(16, 64, D).transpose(1, 0, 2).reshape(64, 16 * D))
    m = {
        "x0": x0,
        "wq": full["L0_attn_wq"], "wk": full["L0_attn_wk"], "wv": full["L0_attn_wv"],
        "woH": woH,
        "fcw": full["L0_attn_fc_w"],
        "projw": full["L0_attn_proj_w"],
        "wgT": np.ascontiguousarray(full["L0_ffn_w_gate"].T),
        "wuT": np.ascontiguousarray(full["L0_ffn_w_up"].T),
        "wdT": np.ascontiguousarray(full["L0_ffn_w_down"].T),
        "g1b": np.broadcast_to(full["L0_attn_ln1_w"], (128, D)).copy(),
        "b1b": np.broadcast_to(full["L0_attn_ln1_b"], (128, D)).copy(),
        "g2b": np.broadcast_to(full["L0_attn_ln2_w"], (128, D)).copy(),
        "b2b": np.broadcast_to(full["L0_attn_ln2_b"], (128, D)).copy(),
        "bq_r": _bf(full["L0_attn_bq"].reshape(1, D)),
        "bk_r": _bf(full["L0_attn_bk"].reshape(1, D)),
        "bv_r": _bf(full["L0_attn_bv"].reshape(1, D)),
        "bo_r": _bf(full["L0_attn_bo"].reshape(1, D)),
        "fcb_r": _bf(full["L0_attn_fc_b"].reshape(1, MLP0)),
        "projb_r": _bf(full["L0_attn_proj_b"].reshape(1, D)),
        "gatew": np.ascontiguousarray(np.concatenate(
            [full["L1_ffn_gate_w"], full["L1_ffn_shared_gate_w"]], axis=1)),
        "identb": _bf(eye),
        "bdm": bdm,
        "iota64": iota,
    }
    return m


def prep_C1_inputs(core, full, b_outs):
    c = core
    SL = HID1 // NCORES
    eye = np.eye(128, dtype=np.float32)
    rs = slice(c * SL, (c + 1) * SL)
    return {
        "w1_sl": np.ascontiguousarray(full["L1_attn_w1"][rs]),
        "w2_sl": np.ascontiguousarray(full["L1_attn_w2"].T[rs]),
        "dw1_sl": np.concatenate([np.asarray(b_outs[r]["dw1"], np.float32)[rs]
                                  for r in range(NCORES)], axis=0),
        "dw2_sl": np.concatenate([np.asarray(b_outs[r]["dw2T"], np.float32)[rs]
                                  for r in range(NCORES)], axis=0),
        "c1alpha": np.broadcast_to(np.float32(full["L1_attn_alpha"]),
                                   (128, 1)).copy(),
        "c1theta": np.broadcast_to(np.float32(full["L1_attn_theta"]),
                                   (128, 1)).copy(),
        "c1ident": _bf(eye),
    }


def prep_C_inputs(core, full, a_outs, b_outs, c1_outs):
    c = core
    eye = np.eye(128, dtype=np.float32)
    return {
        "w1tp_in": np.concatenate([np.asarray(c1_outs[r]["w1tp"])
                                   for r in range(NCORES)], axis=1),
        "w2p_in": np.concatenate([np.asarray(c1_outs[r]["w2p"])
                                  for r in range(NCORES)], axis=0),
        "db1_all": np.concatenate([np.asarray(b_outs[r]["db1"], np.float32)
                                   for r in range(NCORES)], axis=1),
        "db2_all": np.concatenate([np.asarray(b_outs[r]["db2"], np.float32)
                                   for r in range(NCORES)], axis=1),
        "b1_row": full["L1_attn_b1"].reshape(1, -1),
        "b2_row": full["L1_attn_b2"].reshape(1, -1),
        "alpha128": np.broadcast_to(np.float32(full["L1_attn_alpha"]),
                                    (128, 1)).copy(),
        "theta128": np.broadcast_to(np.float32(full["L1_attn_theta"]),
                                    (128, 1)).copy(),
        "lin1T": np.ascontiguousarray(full["L1_attn_lin1"].T),
        "lin2T": np.ascontiguousarray(full["L1_attn_lin2"].T),
        "n1own_c": np.asarray(a_outs[c]["n1"]),
        "swa_own": np.asarray(b_outs[c]["swa_full"], np.float32)[64:64 + TPC],
        "tokT_all": np.concatenate([np.asarray(b_outs[r]["tokT"], np.float32)
                                    for r in range(NCORES)], axis=0),
        "wsc_all": np.concatenate([np.asarray(b_outs[r]["wscT"], np.float32)
                                   for r in range(NCORES)], axis=0),
        "xn_all": np.concatenate([np.asarray(a_outs[r]["n1"])
                                  for r in range(NCORES)], axis=0),
        "egu_c": np.ascontiguousarray(
            full["L1_ffn_expert_gate_up_w"][8 * c:8 * c + 8].reshape(8 * D, 2 * I_EXP)),
        "edn_c": np.ascontiguousarray(
            full["L1_ffn_expert_down_w"][8 * c:8 * c + 8].reshape(8 * I_EXP, D)),
        "ebase8": (8.0 * c + np.arange(8, dtype=np.float32)).reshape(8, 1),
        "identb3": _bf(eye),
    }


def prep_D_inputs(core, a_outs, b_outs, c_outs):
    c = core
    return {
        "outtab_all": np.concatenate([np.asarray(c_outs[r]["outtab"], np.float32)
                                      for r in range(NCORES)], axis=0),
        "offs_in": np.asarray(b_outs[c]["offs"], np.float32),
        "x1_in": np.asarray(a_outs[c]["x1"], np.float32),
        "merged_in": np.asarray(c_outs[c]["merged"], np.float32),
        "sh_in": np.asarray(b_outs[c]["sh"], np.float32),
    }


def kernel(**inputs):
    full = inputs
    in_maps = []
    for c in range(NCORES):
        m = prep_A_inputs(c, full)
        b, half = c // 2, c % 2
        m["x0"] = np.ascontiguousarray(full["L0_attn_q_x"][b, half * TPC:(half + 1) * TPC])
        in_maps.append(m)
    ka = build_launch_A()
    resA = run_bass_kernel_spmd(ka.nc, in_maps, core_ids=list(range(NCORES)))
    a_outs = resA.results
    in_maps = [prep_B_inputs(c, full, a_outs) for c in range(NCORES)]
    kb = build_launch_B()
    resB = run_bass_kernel_spmd(kb.nc, in_maps, core_ids=list(range(NCORES)))
    b_outs = resB.results
    in_maps = [prep_C1_inputs(c, full, b_outs) for c in range(NCORES)]
    kc1 = build_launch_C1()
    resC1 = run_bass_kernel_spmd(kc1.nc, in_maps, core_ids=list(range(NCORES)))
    c1_outs = resC1.results
    in_maps = [prep_C_inputs(c, full, a_outs, b_outs, c1_outs) for c in range(NCORES)]
    kc = build_launch_C()
    resC = run_bass_kernel_spmd(kc.nc, in_maps, core_ids=list(range(NCORES)))
    c_outs = resC.results
    in_maps = [prep_D_inputs(c, a_outs, b_outs, c_outs) for c in range(NCORES)]
    kd = build_launch_D()
    resD = run_bass_kernel_spmd(kd.nc, in_maps, core_ids=list(range(NCORES)))
    out = np.empty((B, S, D), dtype=np.float32)
    for c in range(NCORES):
        b, half = c // 2, c % 2
        out[b, half * TPC:(half + 1) * TPC] = np.asarray(resD.results[c]["x2"],
                                                         np.float32)
    return out


if __name__ == "__main__":
    print("module; use driver")


def pass_tokcontract(k, stat_src, Ks, mov_src, N, out, T, src_mark=None, group=4):
    t, v, d = k.t, k.v, k.d
    assert N == 1024
    nch = Ks // 128
    ntiles = T // 128
    es = ExitStack()
    with es:
        statb = [k.sb("tcs", [128, group * 128], BF16, es) for _ in range(2)]
        movb = [k.sb("tcm", [128, N], BF16, es) for _ in range(2)]
        osb = k.sb("tco", [128, N], F32, es)
        ps = [k.ps(es, "tcp", [128, N], F32) for _ in range(group)]
        entry_st = stmarks(k)
        m_evac = None
        for cg in range(0, nch, group):
            gn = min(group, nch - cg)

            def load_t(i):
                if src_mark is not None:
                    d.wait(src_mark)
                d.wait(entry_st)
                sl = k.ld_sems[i % 2]
                d.start(statb[i % 2][:, 0:gn * 128],
                        stat_src[i * 128:(i + 1) * 128, cg * 128:(cg + gn) * 128],
                        sem=sl)
                d.start(movb[i % 2][:], mov_src[i * 128:(i + 1) * 128, :], sem=sl)
                return d.mark(sl)

            mks = [load_t(0)]
            if ntiles > 1:
                mks.append(load_t(1))
            for it in range(ntiles):
                t.wait(mks[it])
                if it == 0 and m_evac is not None:
                    t.wait(m_evac)
                for c in range(gn):
                    st_ = statb[it % 2][:, c * 128:(c + 1) * 128]
                    t.matmul(ps[c][:, 0:512], st_, movb[it % 2][:, 0:512],
                             start=(it == 0), stop=(it == ntiles - 1))
                    t.matmul(ps[c][:, 512:1024], st_, movb[it % 2][:, 512:1024],
                             start=(it == 0), stop=(it == ntiles - 1))
                m_mm = t.mark()
                if it + 2 < ntiles:
                    d.wait(m_mm)
                    mks.append(load_t(it + 2))
            m_mmf = t.mark()
            for c in range(gn):
                v.wait(m_mmf)
                v.tensor_copy(out=osb[:], in_=ps[c][:])
                m_e = k.v.mark()
                d.wait(m_e)
                d.start(out[(cg + c) * 128:(cg + c + 1) * 128, :], osb[:],
                        sem=k.st_sems[0])
                m_evac = k.d.mark(k.st_sems[0])
                v.wait(m_evac)
    return k.d.mark(k.st_sems[0])


def pass_colsum(k, src, N, out, T, src_mark=None):
    t, v, d = k.t, k.v, k.d
    nj = N // 512
    ntiles = T // 128
    es = ExitStack()
    with es:
        movb = [k.sb("csm", [128, N], BF16, es) for _ in range(2)]
        onesc = k.sb("csone", [128, 1], BF16, es)
        osb = k.sb("cso", [1, N], F32, es)
        ps = [k.ps(es, "csp", [1, 512], F32) for _ in range(nj)]
        v.memset(onesc[:], 1.0)
        entry_st = stmarks(k)

        def load_t(i):
            if src_mark is not None:
                d.wait(src_mark)
            d.wait(entry_st)
            sl = k.ld_sems[i % 2]
            d.start(movb[i % 2][:], src[i * 128:(i + 1) * 128, :], sem=sl)
            return d.mark(sl)

        mks = [load_t(0)]
        if ntiles > 1:
            mks.append(load_t(1))
        t.wait(k.v.mark())
        for it in range(ntiles):
            t.wait(mks[it])
            for j in range(nj):
                t.matmul(ps[j][:], onesc[:, 0:1], movb[it % 2][:, j * 512:(j + 1) * 512],
                         start=(it == 0), stop=(it == ntiles - 1))
            m_mm = t.mark()
            if it + 2 < ntiles:
                d.wait(m_mm)
                mks.append(load_t(it + 2))
        m_f = t.mark()
        v.wait(m_f)
        for j in range(nj):
            v.tensor_copy(out=osb[0:1, j * 512:(j + 1) * 512], in_=ps[j][:])
        m_e = k.v.mark()
        d.wait(m_e)
        d.start(out, osb[:], sem=k.st_sems[0])
    return k.d.mark(k.st_sems[0])


def _moe_dispatch(k, To, NTO, idx8_in, w8_in, counts_all, coremask, tokbase,
                  iota64b, iota512b, lexc, o_tokT, o_wscT, o_offs):
    t, v, s, d = k.t, k.v, k.s, k.d
    es = ExitStack()
    with es:
        idxb = [k.sb("mdidx", [128, 8], F32, es) for _ in range(2)]
        w8b = [k.sb("mdw8", [128, 8], F32, es) for _ in range(2)]
        offsb = [k.sb("mdoffs", [128, 8], F32, es) for _ in range(2)]
        cnts = k.sb("mdcnt", [1, 512], F32, es)
        cmsk = k.sb("mdcm", [1, 512], F32, es)
        base = k.sb("mdbase", [1, 64], F32, es)
        onesf = k.sb("mdones", [1, 128], F32, es)
        onescol = k.sb("mdonec", [128, 1], F32, es)
        runb = k.sb("mdrun", [128, 64], F32, es)
        iota64s = k.sb("mdio64", [128, 64], F32, es)
        iota512s = k.sb("mdio512", [128, 512], F32, es)
        lexcs = k.sb("mdlexc", [128, 128], F32, es)
        tokb = k.sb("mdtokb", [128, 1], F32, es)
        tokid = k.sb("mdtokid", [128, 1], F32, es)
        OH = k.sb("mdOH", [128, 512], F32, es)
        jexc = k.sb("mdjexc", [128, 512], F32, es)
        pallj = k.sb("mdpallj", [128, 64], F32, es)
        posfull = k.sb("mdposf", [128, 512], F32, es)
        posj = k.sb("mdposj", [128, 8], F32, es)
        slot = k.sb("mdslot", [128, 8], F32, es)
        keep = k.sb("mdkeep", [128, 8], F32, es)
        w8k = k.sb("mdw8k", [128, 8], F32, es)
        valT = k.sb("mdvalT", [128, 64], F32, es)
        valW = k.sb("mdvalW", [128, 64], F32, es)
        OHS = k.sb("mdOHS", [128, 512], F32, es)
        tokTo = k.sb("mdtokTo", [64, 512], F32, es)
        wscTo = k.sb("mdwscTo", [64, 512], F32, es)
        tcnt = k.sb("mdtcnt", [1, 64], F32, es)
        p_pre = k.ps(es, "mdppre", [128, 512], F32)
        p_bc = k.ps(es, "mdpbc", [128, 64], F32)
        p_tc = k.ps(es, "mdptc", [1, 64], F32)
        p_tok = k.ps(es, "mdptok", [64, 512], F32)
        p_wsc = k.ps(es, "mdpwsc", [64, 512], F32)

        entry_md = stmarks(k)
        d.wait(entry_md)
        d.start(cnts[:], counts_all)
        d.start(cmsk[:], coremask)
        d.start(iota64s[:], iota64b)
        d.start(iota512s[:], iota512b)
        d.start(lexcs[:], lexc)
        d.start(tokb[:], tokbase)
        m_l0 = k.d.mark()
        v.wait(m_l0)
        v.memset(onesf[:], 1.0)
        v.memset(onescol[:], 1.0)
        v.tensor_tensor(out=cnts[:], in0=cnts[:], in1=cmsk[:], op=ALU.mult)
        v.tensor_reduce(out=base[:],
                        in_=cnts[:].rearrange("p (r e) -> p e r", r=8),
                        axis=AX.X, op=ALU.add)
        m_base = k.v.mark()
        t.wait(m_base)
        t.matmul(p_bc[:], onesf[0:1, :], base[0:1, :], start=True, stop=True)
        m_bc = t.mark()
        v.wait(m_bc)
        v.tensor_copy(out=runb[:], in_=p_bc[:])
        m_run = k.v.mark()

        m_scat_prev = None
        m_offs_dma = [None, None]
        for it in range(NTO):
            sl = k.ld_sems[it % 2]
            if it < 2:
                d.wait(entry_md)
            d.start(idxb[it % 2][:], idx8_in[it * 128:(it + 1) * 128, :], sem=sl)
            d.start(w8b[it % 2][:], w8_in[it * 128:(it + 1) * 128, :], sem=sl)
            m_li = d.mark(sl)
            v.wait(m_li)
            idq = idxb[it % 2][:].unsqueeze(2).to_broadcast([128, 8, 64])
            ioh = iota64s[:].unsqueeze(1).to_broadcast([128, 8, 64])
            v.tensor_tensor(out=OH[:].rearrange("p (j e) -> p j e", j=8),
                            in0=idq, in1=ioh, op=ALU.is_equal)
            m_oh = k.v.mark()
            t.wait(m_oh)
            if m_scat_prev is not None:
                t.wait(m_scat_prev)
            t.matmul(p_pre[:], lexcs[:], OH[:], start=True, stop=True)
            m_pre = t.mark()
            v.wait(m_pre)
            v.tensor_reduce(out=pallj[:],
                            in_=p_pre[:].rearrange("p (j e) -> p e j", j=8),
                            axis=AX.X, op=ALU.add)
            v.memset(jexc[:, 0:64], 0.0)
            for j in range(1, 8):
                v.tensor_tensor(out=jexc[:, j * 64:(j + 1) * 64],
                                in0=jexc[:, (j - 1) * 64:j * 64],
                                in1=OH[:, (j - 1) * 64:j * 64], op=ALU.add)
            pallj_b = pallj[:].unsqueeze(1).to_broadcast([128, 8, 64])
            runb_b = runb[:].unsqueeze(1).to_broadcast([128, 8, 64])
            v.tensor_tensor(out=posfull[:].rearrange("p (j e) -> p j e", j=8),
                            in0=jexc[:].rearrange("p (j e) -> p j e", j=8),
                            in1=pallj_b, op=ALU.add)
            v.tensor_tensor(out=posfull[:].rearrange("p (j e) -> p j e", j=8),
                            in0=posfull[:].rearrange("p (j e) -> p j e", j=8),
                            in1=runb_b, op=ALU.add)
            v.tensor_tensor(out=posfull[:], in0=posfull[:], in1=OH[:], op=ALU.mult)
            v.tensor_reduce(out=posj[:],
                            in_=posfull[:].rearrange("p (j e) -> p j e", j=8),
                            axis=AX.X, op=ALU.add)
            v.tensor_scalar(out=slot[:], in0=posj[:], scalar1=float(CAP),
                            scalar2=None, op0=ALU.min)
            v.tensor_scalar(out=keep[:], in0=posj[:], scalar1=float(CAP),
                            scalar2=None, op0=ALU.is_lt)
            if m_offs_dma[it % 2] is not None:
                v.wait(m_offs_dma[it % 2])
            v.tensor_scalar(out=offsb[it % 2][:], in0=idxb[it % 2][:],
                            scalar1=float(CAP), scalar2=None, op0=ALU.mult)
            v.tensor_tensor(out=offsb[it % 2][:], in0=offsb[it % 2][:], in1=slot[:],
                            op=ALU.add)
            v.tensor_scalar(out=w8k[:], in0=keep[:], scalar1=-1.0, scalar2=1.0,
                            op0=ALU.mult, op1=ALU.add)
            v.scalar_tensor_tensor(out=offsb[it % 2][:], in0=w8k[:],
                                   scalar=float(2 ** 30), in1=offsb[it % 2][:],
                                   op0=ALU.mult, op1=ALU.add)
            v.tensor_tensor(out=w8k[:], in0=w8b[it % 2][:], in1=keep[:], op=ALU.mult)
            v.tensor_scalar(out=tokid[:], in0=tokb[:], scalar1=float(it * 128),
                            scalar2=None, op0=ALU.add)
            m_pp = k.v.mark()
            d.wait(m_pp)
            d.start(o_offs[it * 128:(it + 1) * 128, :], offsb[it % 2][:],
                    sem=k.st_sems[it % 2])
            m_offs_dma[it % 2] = k.d.mark(k.st_sems[it % 2])
            v.tensor_reduce(out=OH[:, 0:64],
                            in_=OH[:].rearrange("p (j e) -> p e j", j=8),
                            axis=AX.X, op=ALU.add)
            m_ohr = k.v.mark()
            t.wait(m_ohr)
            t.matmul(p_tc[:], onescol[:, 0:1], OH[:, 0:64], start=True, stop=True)
            m_tc = t.mark()
            v.wait(m_tc)
            v.tensor_copy(out=tcnt[:], in_=p_tc[:])
            m_tcc = k.v.mark()
            t.wait(m_tcc)
            t.matmul(p_bc[:], onesf[0:1, :], tcnt[0:1, :], start=True, stop=True)
            m_bc2 = t.mark()
            v.wait(m_bc2)
            v.tensor_tensor(out=runb[:], in0=runb[:], in1=p_bc[:], op=ALU.add)
            v.tensor_tensor(out=OH[:].rearrange("p (j e) -> p j e", j=8),
                            in0=idq, in1=ioh, op=ALU.is_equal)
            m_oh2 = k.v.mark()
            for j in range(8):
                v.tensor_scalar(out=valT[:], in0=OH[:, j * 64:(j + 1) * 64],
                                scalar1=tokid[:], scalar2=None, op0=ALU.mult)
                v.tensor_scalar(out=valW[:], in0=OH[:, j * 64:(j + 1) * 64],
                                scalar1=w8k[:, j:j + 1], scalar2=None, op0=ALU.mult)
                v.tensor_scalar(out=OHS[:], in0=iota512s[:], scalar1=slot[:, j:j + 1],
                                scalar2=None, op0=ALU.is_equal)
                m_val = k.v.mark()
                t.wait(m_val)
                first = (it == 0 and j == 0)
                last = (it == NTO - 1 and j == 7)
                t.matmul(p_tok[:], valT[:], OHS[:], start=first, stop=last)
                t.matmul(p_wsc[:], valW[:], OHS[:], start=first, stop=last)
                m_scat_prev = t.mark()
                if j < 7:
                    v.wait(m_scat_prev)
        v.wait(m_scat_prev)
        v.tensor_copy(out=tokTo[:], in_=p_tok[:])
        v.tensor_copy(out=wscTo[:], in_=p_wsc[:])
        m_fin = k.v.mark()
        d.wait(m_fin)
        d.start(o_tokT, tokTo[:], sem=k.st_sems[0])
        d.start(o_wscT, wscTo[:], sem=k.st_sems[0])
    return k.d.mark(k.st_sems[0])




def build_launch_B(To=TPC):
    import os as _os
    BSTOP = int(_os.environ.get("BSTOP", "99"))
    k = K()
    TQ = 64 + To
    NTQ = (TQ + 127) // 128
    TQP = NTQ * 128
    KV = 128 * (NTQ - 1) + 640
    NKV = KV // 128
    NTO = To // 128

    n1own = k.din("n1own", [To, D], BF16)
    n1ctx = k.din("n1ctx", [512, D], BF16)
    memw = k.din("memw", [64, D])
    w1T = k.din("w1T", [D, HID1]); b1_r = k.din("b1_r", [1, HID1], BF16)
    w2T = k.din("w2T", [HID1, D]); b2_r = k.din("b2_r", [1, D], BF16)
    w2as = k.din("w2as", [D, HID1])
    wqT = k.din("wqT", [D, D]); wkT = k.din("wkT", [D, D])
    wvT = k.din("wvT", [D, D]); woT = k.din("woT", [D, D])
    sguw = k.din("sguw", [D, 2 * I_SH]); sdw = k.din("sdw", [I_SH, D])
    sgate_in = k.din("sgate_in", [To, 1])
    masks = k.din("masks", [NTQ * 128, 704], BF16)
    rowmask = k.din("rowmask", [TQP, 1])
    counts_all = k.din("counts_all", [1, 8 * 64])
    idx8_in = k.din("idx8_in", [To, TOPK])
    w8_in = k.din("w8_in", [To, TOPK])
    coremask = k.din("coremask", [1, 8 * 64])
    tokbase = k.din("tokbase", [128, 1])
    identb2 = k.din("identb2", [128, 128], BF16)
    iota64b = k.din("iota64b", [128, 64])
    iota512b = k.din("iota512b", [128, 512])
    lexc = k.din("lexc", [128, 128])

    o_swa = k.dout("swa_full", [TQP, D])
    o_dw1 = k.dout("dw1", [HID1, D])
    o_dw2T = k.dout("dw2T", [HID1, D])
    o_db1 = k.dout("db1", [1, HID1])
    o_db2 = k.dout("db2", [1, D])
    o_tokT = k.dout("tokT", [64, CAP])
    o_wscT = k.dout("wscT", [64, CAP])
    o_offs = k.dout("offs", [To, TOPK])
    o_sh = k.dout("sh", [To, D])

    nq_dram = k.dint("nq_dram", [TQP, D], BF16)
    kv_dram = k.dint("kv_dram", [KV, D], BF16)
    qbT_dram = k.dint("qbT_dram", [D, TQP], BF16)
    kbT_dram = k.dint("kbT_dram", [D, KV], BF16)
    vb_dram = k.dint("vb_dram", [KV, D], BF16)
    swah_dram = k.dint("swah_dram", [TQP, D], BF16)
    h1_dram = k.dint("h1_dram", [TQP, HID1], BF16)
    a1_dram = k.dint("a1_dram", [TQP, HID1], BF16)
    dpred_dram = k.dint("dpred_dram", [TQP, D], BF16)
    dh1_dram = k.dint("dh1_dram", [TQP, HID1], BF16)
    sgA_dram = k.dint("sgA_dram", [To, I_SH], BF16)
    shgu_dram = k.dint("shgu_dram", [To, I_SH], BF16)

    with k.es:
        k.build_seqs()
        t, v, s, g, d, gd = k.t, k.v, k.s, k.g, k.d, k.gd
        k.init_common(identb2)

        zb = k.sb("zb", [128, D], BF16)
        membf = k.sb("membf", [64, D], BF16)
        v.memset(zb[:], 0.0)
        m_z = k.v.mark()
        gd.start(membf[:], memw)
        m_mem = k.gd.mark()
        gd.wait(m_mem)
        d.wait(m_z)
        d.wait(m_mem)
        d.start(nq_dram[0:64, :], membf[:])
        d.start(kv_dram[0:64, :], membf[:])
        d.start(nq_dram[64:64 + To, :], n1own)
        d.start(kv_dram[64:576, :], n1ctx)
        d.start(kv_dram[576:576 + To, :], n1own)
        d.start(nq_dram[64 + To:TQP, :], zb[0:TQP - 64 - To, :])
        d.start(kv_dram[576 + To:KV, :], zb[0:KV - 576 - To, :])
        m_b0 = k.d.mark()

        es1w = ExitStack()
        wq_sb = k.sb("bwq", [128, 8 * D], BF16, es1w)
        wk_sb = k.sb("bwk", [128, 8 * D], BF16, es1w)
        wv_sb = k.sb("bwv", [128, 8 * D], BF16, es1w)
        guard_weights(k)
        load_cast(k, wqT, wq_sb, 8, D)
        load_cast(k, wkT, wk_sb, 8, D)
        load_cast(k, wvT, wv_sb, 8, D)
        m_w1 = gd.mark()
        t.wait(m_w1)
        pass_gemm(k, nq_dram, TQP, D, D, wq_sb, out=qbT_dram, src_mark=m_b0,
                  store_transposed=True)
        pass_gemm(k, kv_dram, KV, D, D, wk_sb, out=kbT_dram, src_mark=m_b0,
                  store_transposed=True)
        pass_gemm(k, kv_dram, KV, D, D, wv_sb, out=vb_dram, src_mark=m_b0)
        m_b1 = stmarks(k)
        es1w.close()

        if BSTOP < 1:
            k.finalize()
            return k
        with ExitStack() as esw:
            kfm = k.sb("kfm", [128, 2 * KV], BF16, esw)
            vh = k.sb("vh", [128, NKV * 256], BF16, esw)
            msk_all = k.sb("mskall", [128, NTQ * 704], BF16, esw)
            qfm = [k.sb("qfm", [128, 256], BF16, esw) for _ in range(2)]
            Pb = [k.sb("Pb", [128, 768], BF16, esw) for _ in range(2)]
            PT = [k.sb("PT", [128, 768], BF16, esw) for _ in range(2)]
            rs_ = [k.sb("rs_", [128, 2], F32, esw) for _ in range(2)]
            avout = [k.sb("avout", [128, 256], BF16, esw) for _ in range(2)]
            p_log = [k.ps(esw, "p_log", [128, 1024], F32) for _ in range(2)]
            p_av = [k.ps(esw, "p_av", [128, 256], F32) for _ in range(2)]

            gd.wait(m_b1)
            for i in range(NTQ):
                gd.start(msk_all[:, i * 704:(i + 1) * 704],
                         masks[i * 128:(i + 1) * 128, :], sem=k.gld_sems[0])
            m_msk = gd.mark(k.gld_sems[0])
            gd.wait(m_msk)
            v.wait(m_msk)
            v.memset(Pb[0][:, 704:768], 0.0)
            v.memset(Pb[1][:, 704:768], 0.0)
            m_pbz = k.v.mark()
            t.wait(m_pbz)

            sd = k.sd
            m_lq, m_lgt, m_vm, m_exp = {}, {}, {}, {}
            m_ptd, m_avmm, m_avev, m_avd = {}, {}, {}, {}
            m_kfmT = {}
            m_vh = {}
            for h in range(H1):
                hc = h * 256
                hit0 = h * NTQ
                if h == 0:
                    gd.wait(m_b1)
                else:
                    gd.wait(m_lgt[hit0 - 1])
                    gd.wait(m_avmm[hit0 - 1])
                for cc in range(2):
                    gd.start(kfm[:, cc * KV:(cc + 1) * KV],
                             kbT_dram[hc + cc * 128:hc + (cc + 1) * 128, :],
                             sem=k.gld_sems[h % 2])
                m_kfmT[h] = gd.mark(k.gld_sems[h % 2])
                gd.wait(m_kfmT[h])
                gd.start(vh[:].rearrange("p (c f) -> p c f", c=NKV),
                         vb_dram[:, hc:hc + 256].rearrange("(c p) f -> p c f", c=NKV),
                         sem=k.gld_sems[h % 2])
                m_vh[h] = gd.mark(k.gld_sems[h % 2])
                gd.wait(m_vh[h])

                if h == 0:
                    d.wait(m_b1)
                if hit0 - 2 >= 0:
                    d.wait(m_lgt[hit0 - 2])
                d.start(qfm[hit0 % 2][:].rearrange("p (c f) -> p c f", c=2),
                        qbT_dram[hc:hc + 256, 0:128]
                        .rearrange("(c p) f -> p c f", c=2),
                        sem=k.ld_sems[hit0 % 2])
                m_lq[hit0] = d.mark(k.ld_sems[hit0 % 2])

                for step in range(NTQ + 1):
                    i = step
                    j = step - 1
                    it = hit0 + i
                    jt = hit0 + j
                    if i < NTQ:
                        t.wait(m_lq[it])
                        if i < 2:
                            t.wait(m_kfmT[h])
                        if it - 2 >= 0:
                            t.wait(m_exp[it - 2])
                        for cc in range(2):
                            st_ = qfm[it % 2][:, cc * 128:(cc + 1) * 128]
                            base = cc * KV + i * 128
                            t.matmul(p_log[it % 2][:, 0:512], st_,
                                     kfm[:, base:base + 512],
                                     start=(cc == 0), stop=(cc == 1))
                            t.matmul(p_log[it % 2][:, 512:640], st_,
                                     kfm[:, base + 512:base + 640],
                                     start=(cc == 0), stop=False)
                            t.matmul(p_log[it % 2][:, 640:704], st_,
                                     kfm[:, cc * KV:cc * KV + 64],
                                     start=False, stop=(cc == 1))
                        m_lgt[it] = t.mark()
                        v.wait(m_lgt[it])
                        if it - 2 >= 0:
                            v.wait(m_exp[it - 2])
                        v.tensor_tensor(out=p_log[it % 2][:, 0:704],
                                        in0=p_log[it % 2][:, 0:704],
                                        in1=msk_all[:, i * 704:(i + 1) * 704],
                                        op=ALU.add)
                        v.tensor_reduce(out=rs_[it % 2][:, 0:1],
                                        in_=p_log[it % 2][:, 0:704], axis=AX.X,
                                        op=ALU.max, negate=True)
                        v.tensor_scalar(out=rs_[it % 2][:, 0:1],
                                        in0=rs_[it % 2][:, 0:1],
                                        scalar1=1.0 / 16.0, scalar2=None,
                                        op0=ALU.mult)
                        m_vm[it] = k.v.mark()
                    if j >= 0:
                        k.s.wait(m_exp[jt])
                        if jt - 2 >= 0:
                            k.s.wait(m_avmm[jt - 2])
                        if jt == 0:
                            k.s.wait(m_pbz)
                        sd.start(PT[jt % 2][:].rearrange("p (c f) -> p c f", c=6),
                                 Pb[jt % 2][:].rearrange("p (c f) -> p c f", c=6),
                                 transpose=True, sem=k.sd_sems[jt % 2])
                        m_ptd[jt] = sd.mark(k.sd_sems[jt % 2])
                    if i < NTQ:
                        s.wait(m_vm[it])
                        if it - 2 >= 0:
                            s.wait(m_ptd[it - 2])
                            s.wait(m_avev[it - 2])
                        s.activation(out=Pb[it % 2][:, 0:704],
                                     in_=p_log[it % 2][:, 0:704], func=AF.Exp,
                                     bias=rs_[it % 2][:, 0:1], scale=1.0 / 16.0,
                                     accum_out=rs_[it % 2][:, 1:2])
                        m_exp[it] = k.s.mark()
                        if i + 1 < NTQ:
                            d.wait(m_lgt[it - 1] if it - 1 >= 0 else m_b1)
                            d.start(qfm[(it + 1) % 2][:]
                                    .rearrange("p (c f) -> p c f", c=2),
                                    qbT_dram[hc:hc + 256,
                                             (i + 1) * 128:(i + 2) * 128]
                                    .rearrange("(c p) f -> p c f", c=2),
                                    sem=k.ld_sems[(it + 1) % 2])
                            m_lq[it + 1] = d.mark(k.ld_sems[(it + 1) % 2])
                    if j >= 0:
                        v.wait(m_exp[jt])
                        v.reciprocal(out=rs_[jt % 2][:, 1:2], in_=rs_[jt % 2][:, 1:2])
                        t.wait(m_ptd[jt])
                        if j < 2:
                            t.wait(m_vh[h])
                        if jt - 2 >= 0:
                            t.wait(m_avev[jt - 2])
                        for c5 in range(5):
                            t.matmul(p_av[jt % 2][:],
                                     PT[jt % 2][:, c5 * 128:(c5 + 1) * 128],
                                     vh[:, (j + c5) * 256:(j + c5 + 1) * 256],
                                     start=(c5 == 0), stop=False)
                        t.matmul(p_av[jt % 2][:], PT[jt % 2][0:64, 640:768],
                                 vh[0:64, 0:256], start=False, stop=True)
                        m_avmm[jt] = t.mark()
                        v.wait(m_avmm[jt])
                        if jt - 2 >= 0:
                            v.wait(m_avd[jt - 2])
                        v.tensor_scalar(out=avout[jt % 2][:], in0=p_av[jt % 2][:],
                                        scalar1=rs_[jt % 2][:, 1:2], scalar2=None,
                                        op0=ALU.mult)
                        m_avev[jt] = k.v.mark()
                        d.wait(m_avev[jt])
                        d.start(swah_dram[j * 128:(j + 1) * 128, hc:hc + 256],
                                avout[jt % 2][:], sem=k.st_sems[jt % 2])
                        m_avd[jt] = d.mark(k.st_sems[jt % 2])
        m_b2 = stmarks(k)

        if BSTOP < 2:
            k.finalize()
            return k
        es3w = ExitStack()
        wo_sb = k.sb("bwo", [128, 8 * D], BF16, es3w)
        guard_weights(k)
        load_cast(k, woT, wo_sb, 8, D)
        t.wait(gd.mark())
        pass_gemm(k, swah_dram, TQP, D, D, wo_sb, out=o_swa, out_dtype=F32,
                  src_mark=m_b2)
        m_b3 = stmarks(k)
        es3w.close()

        if BSTOP < 3:
            k.finalize()
            return k
        es4w = ExitStack()
        w1_sb = k.sb("bw1", [128, 8 * HID1], BF16, es4w)
        bias1 = k.sb("bb1", [1, HID1], BF16, es4w)
        guard_weights(k)
        load_cast(k, w1T, w1_sb, 8, HID1)
        gd.start(bias1[:], b1_r)
        t.wait(gd.mark())
        h1keep = [k.sb("h1keep", [128, 2048], BF16, es4w) for _ in range(2)]
        h1dma = {"m": [None, None]}

        def mk_h1_epi(half):
            def epi(k2, m_mm, psum_ap, out_ap, aux):
                it = aux["tile"]
                v, s, d = k2.v, k2.s, k2.d
                v.wait(m_mm)
                if h1dma["m"][it % 2] is not None:
                    v.wait(h1dma["m"][it % 2])
                v.tensor_copy(out=h1keep[it % 2][:], in_=psum_ap)
                m_c = k2.v.mark()
                d.wait(m_c)
                d.start(h1_dram[it * 128:(it + 1) * 128,
                                half * 2048:half * 2048 + 2048],
                        h1keep[it % 2][:], sem=k2.st2_sems[it % 2])
                h1dma["m"][it % 2] = k2.d.mark(k2.st2_sems[it % 2])
                s.wait(m_mm)
                s.wait(m_c)
                s.activation(out=out_ap, in_=psum_ap, func=AF.Silu)
                return k2.s.mark()
            return epi

        for half in range(2):
            pass_gemm(k, nq_dram, TQP, D, 2048, w1_sb, wcol0=half * 2048,
                      wstride=HID1, bias_sb=bias1, bias_col0=half * 2048,
                      out=a1_dram[:, half * 2048:half * 2048 + 2048],
                      epilogue=mk_h1_epi(half), src_mark=m_b0)
        m_b4 = stmarks(k)
        es4w.close()

        if BSTOP < 4:
            k.finalize()
            return k
        es5w = ExitStack()
        w2_sb = k.sb("bw2", [128, 32 * D], BF16, es5w)
        bias2 = k.sb("bb2", [1, D], BF16, es5w)
        guard_weights(k)
        load_cast(k, w2T, w2_sb, 32, D)
        gd.start(bias2[:], b2_r)
        t.wait(gd.mark())

        def dpred_epi(k2, m_mm, psum_ap, out_ap, aux):
            v = k2.v
            v.wait(m_mm)
            v.tensor_tensor(out=out_ap, in0=psum_ap, in1=aux["res"][0], op=ALU.subtract)
            v.tensor_scalar(out=out_ap, in0=out_ap, scalar1=aux["res"][1],
                            scalar2=None, op0=ALU.mult)
            return k2.v.mark()

        pass_gemm(k, a1_dram, TQP, HID1, D, w2_sb, bias_sb=bias2, out=dpred_dram,
                  epilogue=dpred_epi,
                  res_srcs=[(o_swa, F32), (rowmask, F32)],
                  src_mark=m_b4, wait_marks=(m_b3,))
        m_b5 = stmarks(k)
        es5w.close()

        if BSTOP < 5:
            k.finalize()
            return k
        es6w = ExitStack()
        w2a_sb = k.sb("bw2a", [128, 8 * HID1], BF16, es6w)
        guard_weights(k)
        load_cast(k, w2as, w2a_sb, 8, HID1)
        t.wait(gd.mark())
        sg6 = k.sb("sg6", [128, 2048], F32, es6w)
        t6 = k.sb("t6", [128, 2048], F32, es6w)

        def dh1_epi(k2, m_mm, psum_ap, out_ap, aux):
            v, s = k2.v, k2.s
            h1t = aux["res"][0]
            s.wait(m_mm)
            s.wait(aux["m_in"])
            s.activation(out=sg6[:], in_=h1t, func=AF.Sigmoid)
            m_sg = k2.s.mark()
            v.wait(m_sg)
            v.tensor_scalar(out=t6[:], in0=sg6[:], scalar1=-1.0, scalar2=1.0,
                            op0=ALU.mult, op1=ALU.add)
            v.tensor_tensor(out=t6[:], in0=t6[:], in1=h1t, op=ALU.mult)
            v.tensor_scalar(out=t6[:], in0=t6[:], scalar1=1.0, scalar2=None,
                            op0=ALU.add)
            v.tensor_tensor(out=t6[:], in0=t6[:], in1=sg6[:], op=ALU.mult)
            v.wait(m_mm)
            v.tensor_tensor(out=out_ap, in0=t6[:], in1=psum_ap, op=ALU.mult)
            return k2.v.mark()

        for half in range(2):
            pass_gemm(k, dpred_dram, TQP, D, 2048, w2a_sb, wcol0=half * 2048,
                      wstride=HID1,
                      out=dh1_dram[:, half * 2048:half * 2048 + 2048],
                      epilogue=dh1_epi,
                      res_srcs=[(h1_dram[:, half * 2048:half * 2048 + 2048], BF16)],
                      src_mark=m_b5)
        m_b6 = stmarks(k)
        es6w.close()

        if BSTOP < 6:
            k.finalize()
            return k
        pass_tokcontract(k, a1_dram, HID1, dpred_dram, D, o_dw2T, TQP,
                         src_mark=m_b6, group=4)
        pass_tokcontract(k, dh1_dram, HID1, nq_dram, D, o_dw1, TQP,
                         src_mark=m_b6, group=4)
        pass_colsum(k, dh1_dram, HID1, o_db1, TQP, src_mark=m_b6)
        pass_colsum(k, dpred_dram, D, o_db2, TQP, src_mark=m_b6)

        if BSTOP < 7:
            k.finalize()
            return k
        _moe_dispatch(k, To, NTO, idx8_in, w8_in, counts_all, coremask, tokbase,
                      iota64b, iota512b, lexc, o_tokT, o_wscT, o_offs)

        if BSTOP < 8:
            k.finalize()
            return k
        es11w = ExitStack()
        sgu_sb = k.sb("bsgu", [128, 8 * 2 * I_SH], BF16, es11w)
        guard_weights(k)
        load_cast(k, sguw, sgu_sb, 8, 2 * I_SH)
        t.wait(gd.mark())

        def silu_epi2(k2, m_mm, psum_ap, out_ap, aux):
            k2.s.wait(m_mm)
            k2.s.activation(out=out_ap, in_=psum_ap, func=AF.Silu)
            return k2.s.mark()

        for half in range(2):
            pass_gemm(k, n1own, To, D, 1024, sgu_sb, wcol0=half * 1024,
                      wstride=2 * I_SH,
                      out=sgA_dram[:, half * 1024:half * 1024 + 1024],
                      epilogue=silu_epi2, src_mark=None)
        m_sga = stmarks(k)

        def upmul_epi2(k2, m_mm, psum_ap, out_ap, aux):
            k2.v.wait(m_mm)
            k2.v.tensor_tensor(out=out_ap, in0=psum_ap, in1=aux["res"][0], op=ALU.mult)
            return k2.v.mark()

        for half in range(2):
            pass_gemm(k, n1own, To, D, 1024, sgu_sb, wcol0=I_SH + half * 1024,
                      wstride=2 * I_SH,
                      out=shgu_dram[:, half * 1024:half * 1024 + 1024],
                      epilogue=upmul_epi2,
                      res_srcs=[(sgA_dram[:, half * 1024:half * 1024 + 1024], BF16)],
                      src_mark=m_sga)
        m_shgu = stmarks(k)
        es11w.close()

        es12w = ExitStack()
        sdw_sb = k.sb("bsdw", [128, 16 * D], BF16, es12w)
        guard_weights(k)
        load_cast(k, sdw, sdw_sb, 16, D)
        t.wait(gd.mark())

        def shfinal_epi(k2, m_mm, psum_ap, out_ap, aux):
            k2.v.wait(m_mm)
            k2.v.tensor_scalar(out=out_ap, in0=psum_ap, scalar1=aux["res"][0],
                               scalar2=None, op0=ALU.mult)
            return k2.v.mark()

        pass_gemm(k, shgu_dram, To, I_SH, D, sdw_sb, out=o_sh, out_dtype=F32,
                  epilogue=shfinal_epi, res_srcs=[(sgate_in, F32)],
                  src_mark=m_shgu)
        es12w.close()

        k.finalize()
    return k


def build_masks(NTQ, parity, To):
    NEG = -1e30
    TQ = 64 + To
    M = np.full((NTQ * 128, 704), NEG, dtype=np.float32)
    for r in range(NTQ * 128):
        tau = r // 128
        p = r % 128
        if r >= TQ or (parity == 1 and r < 64):
            M[r, 0] = 0.0
            continue
        if r < 64:
            for j in range(0, r + 1):
                M[r, 640 + j] = 0.0
            continue
        sown = r - 64
        if parity == 0:
            lo = 576 + max(0, sown - 511)
            hi = 576 + sown
            for kvr in range(lo, hi + 1):
                c = kvr - 128 * tau
                if 0 <= c < 640:
                    M[r, c] = 0.0
            jlo = max(0, sown - 447)
            if jlo <= 63:
                M[r, 640 + jlo:640 + 64] = 0.0
        else:
            for c in range(p + 1, p + 513):
                if 0 <= c < 640:
                    M[r, c] = 0.0
    return M


def prep_B_inputs(core, full, a_outs):
    To = a_outs[core]["n1"].shape[0]
    TQ = 64 + To
    NTQ = (TQ + 127) // 128
    TQP = NTQ * 128
    parity = core % 2
    n1own = np.ascontiguousarray(a_outs[core]["n1"])
    if parity == 1:
        n1ctx = np.ascontiguousarray(a_outs[core - 1]["n1"][-512:])
    else:
        n1ctx = np.zeros((512, D), dtype=bfdt)
    rowmask = np.zeros((TQP, 1), np.float32)
    rowmask[64:64 + To] = 1.0
    if parity == 0:
        rowmask[0:64] = 1.0
    counts_all = np.zeros((1, 8 * 64), np.float32)
    for c in range(len(a_outs)):
        counts_all[0, c * 64:(c + 1) * 64] = np.asarray(a_outs[c]["counts"]).reshape(64)
    cm = np.zeros((1, 8 * 64), np.float32)
    cm[0, :core * 64] = 1.0
    eye = np.eye(128, dtype=np.float32)
    lexc = np.tril(np.ones((128, 128), np.float32), -1).T
    m = {
        "n1own": n1own, "n1ctx": n1ctx, "memw": full["L1_attn_mem"],
        "w1T": np.ascontiguousarray(full["L1_attn_w1"].T),
        "b1_r": _bf(full["L1_attn_b1"].reshape(1, -1)),
        "w2T": np.ascontiguousarray(full["L1_attn_w2"].T),
        "b2_r": _bf(full["L1_attn_b2"].reshape(1, -1)),
        "w2as": full["L1_attn_w2"],
        "wqT": np.ascontiguousarray(full["L1_attn_wq"].T),
        "wkT": np.ascontiguousarray(full["L1_attn_wk"].T),
        "wvT": np.ascontiguousarray(full["L1_attn_wv"].T),
        "woT": np.ascontiguousarray(full["L1_attn_wo"].T),
        "sguw": full["L1_ffn_shared_gate_up_w"],
        "sdw": full["L1_ffn_shared_down_w"],
        "sgate_in": np.asarray(a_outs[core]["sgate"], np.float32),
        "masks": build_masks(NTQ, parity, To).astype(bfdt),
        "rowmask": rowmask,
        "counts_all": counts_all.astype(np.float32),
        "idx8_in": np.asarray(a_outs[core]["idx8"], np.float32),
        "w8_in": np.asarray(a_outs[core]["w8"], np.float32),
        "coremask": cm,
        "tokbase": (core * To + np.arange(128, dtype=np.float32)).reshape(128, 1),
        "identb2": _bf(eye),
        "iota64b": np.broadcast_to(np.arange(64, dtype=np.float32), (128, 64)).copy(),
        "iota512b": np.broadcast_to(np.arange(512, dtype=np.float32), (128, 512)).copy(),
        "lexc": np.ascontiguousarray(lexc),
    }
    return m




def build_launch_C1(ncores=NCORES):
    k = K()
    SL = HID1 // ncores
    nch = SL // 128
    w1_sl = k.din("w1_sl", [SL, D])
    w2_sl = k.din("w2_sl", [SL, D])
    dw1_sl = k.din("dw1_sl", [ncores * SL, D])
    dw2_sl = k.din("dw2_sl", [ncores * SL, D])
    alpha128 = k.din("c1alpha", [128, 1])
    theta128 = k.din("c1theta", [128, 1])
    identb = k.din("c1ident", [128, 128], BF16)
    o_w1tp = k.dout("w1tp", [D, SL], BF16)
    o_w2p = k.dout("w2p", [SL, D], BF16)

    with k.es:
        k.build_seqs()
        t, v, s, g, d, gd = k.t, k.v, k.s, k.g, k.d, k.gd
        k.init_common(identb)
        al = k.sb("al", [128, 1], F32)
        th = k.sb("th", [128, 1], F32)
        oma = k.sb("oma", [128, 1], F32)
        nth = k.sb("nth", [128, 1], F32)
        d.start(al[:], alpha128)
        d.start(th[:], theta128)
        m_at = k.d.mark()
        v.wait(m_at)
        v.tensor_scalar(out=oma[:], in0=al[:], scalar1=-1.0, scalar2=1.0,
                        op0=ALU.mult, op1=ALU.add)
        v.tensor_scalar(out=nth[:], in0=th[:], scalar1=-1.0, scalar2=None,
                        op0=ALU.mult)
        m_consts = k.v.mark()

        def upd_slice(w_src, dparts_src, out_bf_dram, also_transpose_to):
            es1 = ExitStack()
            with es1:
                gb = [[k.sb("updg", [128, D], F32, es1) for _ in range(ncores + 1)]
                      for _ in range(2)]
                ob = [k.sb("updo", [128, D], BF16, es1) for _ in range(2)]
                tpb = [k.sb("updtp", [128, D], BF16, es1) for _ in range(2)]
                entry = stmarks(k)
                m_done = [None, None]
                m_st = [None, None]
                m_tp = [None, None]
                m_st2 = [None, None]
                for i in range(nch):
                    p = i % 2
                    sl = k.ld_sems[p]
                    if i < 2:
                        d.wait(entry)
                    if m_done[p] is not None:
                        d.wait(m_done[p])
                    d.start(gb[p][0][:], w_src[i * 128:(i + 1) * 128, :], sem=sl)
                    for r in range(ncores):
                        d.start(gb[p][r + 1][:],
                                dparts_src[r * SL + i * 128:r * SL + (i + 1) * 128, :],
                                sem=sl)
                    m_l = d.mark(sl)
                    v.wait(m_l)
                    v.wait(m_consts)
                    if m_st[p] is not None:
                        v.wait(m_st[p])
                    if m_tp[p] is not None:
                        v.wait(m_tp[p])
                    acc = gb[p][1]
                    for r in range(2, ncores + 1):
                        v.tensor_tensor(out=acc[:], in0=acc[:], in1=gb[p][r][:],
                                        op=ALU.add)
                    v.tensor_scalar(out=gb[p][0][:], in0=gb[p][0][:], scalar1=oma[:],
                                    scalar2=None, op0=ALU.mult)
                    v.scalar_tensor_tensor(out=ob[p][:], in0=acc[:], scalar=nth[:],
                                           in1=gb[p][0][:], op0=ALU.mult, op1=ALU.add)
                    m_u = k.v.mark()
                    m_done[p] = m_u
                    d.wait(m_u)
                    if out_bf_dram is not None:
                        d.start(out_bf_dram[i * 128:(i + 1) * 128, :], ob[p][:],
                                sem=k.st_sems[p])
                        m_st[p] = d.mark(k.st_sems[p])
                    if also_transpose_to is not None:
                        if m_st2[p] is not None:
                            d.wait(m_st2[p])
                        d.start(tpb[p][:].rearrange("p (c f) -> p c f", c=8),
                                ob[p][:].rearrange("p (c f) -> p c f", c=8),
                                transpose=True, sem=k.tpd_sems[p])
                        m_tp[p] = d.mark(k.tpd_sems[p])
                        d.wait(m_tp[p])
                        d.start(also_transpose_to[:, i * 128:(i + 1) * 128]
                                .rearrange("(c p) f -> p c f", c=8),
                                tpb[p][:].rearrange("p (c f) -> p c f", c=8),
                                sem=k.st2_sems[p])
                        m_st2[p] = d.mark(k.st2_sems[p])

        upd_slice(w1_sl, dw1_sl, None, o_w1tp)
        upd_slice(w2_sl, dw2_sl, o_w2p, None)
        k.finalize()
    return k


def build_launch_C(To=TPC, ncores=NCORES):
    k = K()
    NTO = To // 128
    TE = ncores * To

    w1tp_in = k.din("w1tp_in", [D, HID1], BF16)
    w2p_in = k.din("w2p_in", [HID1, D], BF16)
    db1_all = k.din("db1_all", [1, ncores * HID1])
    db2_all = k.din("db2_all", [1, ncores * D])
    b1_row = k.din("b1_row", [1, HID1])
    b2_row = k.din("b2_row", [1, D])
    alpha128 = k.din("alpha128", [128, 1])
    theta128 = k.din("theta128", [128, 1])
    lin1T = k.din("lin1T", [D, D])
    lin2T = k.din("lin2T", [D, D])
    n1own_c = k.din("n1own_c", [To, D], BF16)
    swa_own = k.din("swa_own", [To, D])
    tokT_all = k.din("tokT_all", [ncores * 64, CAP])
    wsc_all = k.din("wsc_all", [ncores * 64, CAP])
    xn_all = k.din("xn_all", [TE, D], BF16)
    egu_c = k.din("egu_c", [8 * D, 2 * I_EXP])
    edn_c = k.din("edn_c", [8 * I_EXP, D])
    ebase8 = k.din("ebase8", [8, 1])
    identb3 = k.din("identb3", [128, 128], BF16)

    o_merged = k.dout("merged", [To, D])
    o_outtab = k.dout("outtab", [8 * CAP, D])

    a2_dram = k.dint("a2_dram", [To, HID1], BF16)
    omem_dram = k.dint("omem_dram", [To, D], BF16)
    g1_dram = k.dint("g1_dram", [To, D], BF16)
    toksum_dram = k.dint("toksum_dram", [64, CAP])
    wscsum_dram = k.dint("wscsum_dram", [64, CAP])
    mytok_dram = k.dint("mytok_dram", [8, CAP])
    mywsc_dram = k.dint("mywsc_dram", [8, CAP])
    buf_dram = k.dint("buf_dram", [CAP, D], BF16)
    act_dram = k.dint("act_dram", [CAP, 2 * I_EXP], BF16)

    with k.es:
        k.build_seqs()
        t, v, s, g, d, gd = k.t, k.v, k.s, k.g, k.d, k.gd
        k.init_common(identb3)

        al = k.sb("al", [128, 1], F32)
        th = k.sb("th", [128, 1], F32)
        oma = k.sb("oma", [128, 1], F32)
        nth = k.sb("nth", [128, 1], F32)
        d.start(al[:], alpha128)
        d.start(th[:], theta128)
        m_at = k.d.mark()
        v.wait(m_at)
        v.tensor_scalar(out=oma[:], in0=al[:], scalar1=-1.0, scalar2=1.0,
                        op0=ALU.mult, op1=ALU.add)
        v.tensor_scalar(out=nth[:], in0=th[:], scalar1=-1.0, scalar2=None,
                        op0=ALU.mult)
        m_consts = k.v.mark()

        b1p = k.sb("b1p", [1, HID1], BF16)
        b2p = k.sb("b2p", [1, D], BF16)
        br = k.sb("br", [1, HID1], F32)
        bacc = k.sb("bacc", [1, HID1], F32)
        bld = k.sb("bld", [1, HID1], F32)

        def bias_upd(b_row_src, dball_src, width, out_bp):
            d.wait(k.v.mark())
            d.start(br[0:1, 0:width], b_row_src)
            d.start(bacc[0:1, 0:width], dball_src[0:1, 0:width])
            m_lb = k.d.mark()
            v.wait(m_lb)
            for r in range(1, ncores):
                d.start(bld[0:1, 0:width],
                        dball_src[0:1, r * width:(r + 1) * width],
                        sem=k.ld_sems[0])
                m_lr = d.mark(k.ld_sems[0])
                v.wait(m_lr)
                v.tensor_tensor(out=bacc[0:1, 0:width], in0=bacc[0:1, 0:width],
                                in1=bld[0:1, 0:width], op=ALU.add)
                m_a = k.v.mark()
                d.wait(m_a)
            v.wait(m_consts)
            v.tensor_scalar(out=br[0:1, 0:width], in0=br[0:1, 0:width],
                            scalar1=oma[0:1, :], scalar2=None, op0=ALU.mult)
            v.scalar_tensor_tensor(out=out_bp, in0=bacc[0:1, 0:width],
                                   scalar=nth[0:1, :], in1=br[0:1, 0:width],
                                   op0=ALU.mult, op1=ALU.add)
            return k.v.mark()

        m_b1p = bias_upd(b1_row, db1_all, HID1, b1p[:])
        m_b2p = bias_upd(b2_row, db2_all, D, b2p[:])

        es2 = ExitStack()
        w1p_sb = k.sb("cw1p", [128, 8 * HID1], BF16, es2)
        guard_weights(k)
        for i in range(8):
            gd.start(w1p_sb[:, i * HID1:(i + 1) * HID1],
                     w1tp_in[i * 128:(i + 1) * 128, :])
        m_w1p = gd.mark()
        t.wait(m_w1p)
        t.wait(m_b1p)

        def silu_epi3(k2, m_mm, psum_ap, out_ap, aux):
            k2.s.wait(m_mm)
            k2.s.activation(out=out_ap, in_=psum_ap, func=AF.Silu)
            return k2.s.mark()

        for half in range(2):
            pass_gemm(k, n1own_c, To, D, 2048, w1p_sb, wcol0=half * 2048,
                      wstride=HID1, bias_sb=b1p, bias_col0=half * 2048,
                      out=a2_dram[:, half * 2048:half * 2048 + 2048],
                      epilogue=silu_epi3)
        m_a2 = stmarks(k)
        es2.close()

        es3 = ExitStack()
        w2p_sb = k.sb("cw2p", [128, 32 * D], BF16, es3)
        guard_weights(k)
        for i in range(32):
            gd.start(w2p_sb[:, i * D:(i + 1) * D], w2p_in[i * 128:(i + 1) * 128, :])
        m_w2p = gd.mark()
        t.wait(m_w2p)
        t.wait(m_b2p)
        pass_gemm(k, a2_dram, To, HID1, D, w2p_sb, bias_sb=b2p, out=omem_dram,
                  src_mark=m_a2)
        m_omem = stmarks(k)
        es3.close()

        es4 = ExitStack()
        l1_sb = k.sb("cl1", [128, 8 * D], BF16, es4)
        guard_weights(k)
        load_cast(k, lin1T, l1_sb, 8, D)
        t.wait(gd.mark())

        def sig_epi(k2, m_mm, psum_ap, out_ap, aux):
            k2.s.wait(m_mm)
            k2.s.activation(out=out_ap, in_=psum_ap, func=AF.Sigmoid)
            return k2.s.mark()

        pass_gemm(k, swa_own, To, D, D, l1_sb, out=g1_dram, epilogue=sig_epi,
                  src_dtype=F32)
        m_g1 = stmarks(k)
        es4.close()

        es5 = ExitStack()
        l2_sb = k.sb("cl2", [128, 8 * D], BF16, es5)
        guard_weights(k)
        load_cast(k, lin2T, l2_sb, 8, D)
        t.wait(gd.mark())
        g2t = k.sb("cg2t", [128, D], F32, es5)
        mt = k.sb("cmt", [128, D], F32, es5)
        cf1 = k.sb("ccf1", [128, D], F32, es5)
        cf2 = k.sb("ccf2", [128, D], F32, es5)

        prev_vm = [None]

        def merge_epi(k2, m_mm, psum_ap, out_ap, aux):
            v, s = k2.v, k2.s
            s.wait(m_mm)
            if prev_vm[0] is not None:
                s.wait(prev_vm[0])
            s.activation(out=g2t[:], in_=psum_ap, func=AF.Sigmoid)
            m_sg = k2.s.mark()
            v.wait(m_sg)
            v.wait(aux["m_in"])
            v.tensor_copy(out=cf1[:], in_=aux["res"][0])
            v.tensor_copy(out=cf2[:], in_=aux["res"][2])
            v.tensor_tensor(out=mt[:], in0=g2t[:], in1=cf2[:], op=ALU.mult)
            v.tensor_tensor(out=out_ap, in0=cf1[:], in1=aux["res"][1], op=ALU.mult)
            v.tensor_tensor(out=out_ap, in0=out_ap, in1=mt[:], op=ALU.add)
            prev_vm[0] = k2.v.mark()
            return prev_vm[0]

        pass_gemm(k, omem_dram, To, D, D, l2_sb, out=o_merged, out_dtype=F32,
                  epilogue=merge_epi,
                  res_srcs=[(g1_dram, BF16), (swa_own, F32), (omem_dram, BF16)],
                  src_mark=m_omem, wait_marks=(m_g1,))
        es5.close()

        es6 = ExitStack()
        tok8 = k.sb("ctok8", [64, CAP], F32, es6)
        tokt = k.sb("ctokt", [64, CAP], F32, es6)
        wsc8 = k.sb("cwsc8", [64, CAP], F32, es6)
        myt = k.sb("cmyt", [8, CAP], F32, es6)
        myw = k.sb("cmyw", [8, CAP], F32, es6)
        eb8 = k.sb("ceb8", [8, 1], F32, es6)
        eb8i = k.sb("ceb8i", [8, 1], I32, es6)
        entry6 = stmarks(k)
        d.wait(entry6)
        d.start(eb8[:], ebase8)
        d.start(tok8[:], tokT_all[0:64, :])
        d.start(wsc8[:], wsc_all[0:64, :])
        m_l6 = k.d.mark()
        v.wait(m_l6)
        v.tensor_copy(out=eb8i[:], in_=eb8[:])
        d.wait(k.v.mark())
        for r in range(1, ncores):
            d.start(tokt[:], tokT_all[r * 64:(r + 1) * 64, :], sem=k.ld_sems[1])
            m_lr = d.mark(k.ld_sems[1])
            v.wait(m_lr)
            v.tensor_tensor(out=tok8[:], in0=tok8[:], in1=tokt[:], op=ALU.add)
            m_a = k.v.mark()
            d.wait(m_a)
            d.start(tokt[:], wsc_all[r * 64:(r + 1) * 64, :], sem=k.ld_sems[1])
            m_lw = d.mark(k.ld_sems[1])
            v.wait(m_lw)
            v.tensor_tensor(out=wsc8[:], in0=wsc8[:], in1=tokt[:], op=ALU.add)
            m_a2_ = k.v.mark()
            d.wait(m_a2_)
        m_sum = k.v.mark()
        d.wait(m_sum)
        d.start(toksum_dram[:, :], tok8[:])
        d.start(wscsum_dram[:, :], wsc8[:])
        m_tsd = k.d.mark()
        gd.wait(m_tsd)
        gd.wait(m_sum)
        gd.indirect(myt[:, :], None, toksum_dram,
                    IndirectOffsetOnAxis(ap=eb8i[:, 0:1], axis=0))
        gd.indirect(myw[:, :], None, wscsum_dram,
                    IndirectOffsetOnAxis(ap=eb8i[:, 0:1], axis=0))
        m_my = k.gd.mark()
        gd.wait(m_my)
        d.wait(m_my)
        d.start(mytok_dram[:, :], myt[:])
        d.start(mywsc_dram[:, :], myw[:])
        m_myd = k.d.mark()

        idxc = k.sb("cidxc", [128, 1], F32, es6)
        idxi = k.sb("cidxi", [128, 1], I32, es6)
        bufb = k.sb("cbufb", [128, D], BF16, es6)
        egu_sb = k.sb("cegu", [128, 8 * 2 * I_EXP], BF16, es6)
        edn_sb = k.sb("cedn", [128, 4 * D], BF16, es6)
        sc6 = k.sb("csc6", [128, I_EXP], BF16, es6)
        prev_e = [None]
        for e in range(8):
            guard_weights(k)
            load_cast(k, egu_c[e * D:(e + 1) * D, :], egu_sb, 8, 2 * I_EXP)
            load_cast(k, edn_c[e * I_EXP:(e + 1) * I_EXP, :], edn_sb, 4, D)
            m_we = gd.mark()
            gd.wait(m_we)
            m_ci_prev = None
            for q in range(4):
                d.wait(m_myd)
                d.wait(stmarks(k))
                if m_ci_prev is not None:
                    d.wait(m_ci_prev)
                d.start(idxc[:], mytok_dram[e, q * 128:(q + 1) * 128].rearrange(
                    "(p one) -> p one", one=1))
                m_lq = k.d.mark()
                v.wait(m_lq)
                v.tensor_copy(out=idxi[:], in_=idxc[:])
                m_ci = k.v.mark()
                m_ci_prev = m_ci
                gd.wait(m_ci)
                gd.indirect(bufb[:, :], None, xn_all,
                            IndirectOffsetOnAxis(ap=idxi[:, 0:1], axis=0))
                m_gq = k.gd.mark()
                gd.wait(m_gq)
                d.wait(m_gq)
                d.start(buf_dram[q * 128:(q + 1) * 128, :], bufb[:],
                        sem=k.st_sems[q % 2])
                gd.wait(d.mark(k.st_sems[q % 2]))
            m_bufd = stmarks(k)
            t.wait(m_we)

            def act_epi(k2, m_mm, psum_ap, out_ap, aux):
                v, s = k2.v, k2.s
                s.wait(m_mm)
                if prev_e[0] is not None:
                    s.wait(prev_e[0])
                s.activation(out=sc6[:], in_=psum_ap[:, 0:I_EXP], func=AF.Silu)
                m_si = k2.s.mark()
                v.wait(m_si)
                v.tensor_tensor(out=out_ap[:, 0:I_EXP], in0=sc6[:],
                                in1=psum_ap[:, I_EXP:2 * I_EXP], op=ALU.mult)
                v.memset(out_ap[:, I_EXP:2 * I_EXP], 0.0)
                prev_e[0] = k2.v.mark()
                return prev_e[0]

            pass_gemm(k, buf_dram, CAP, D, 2 * I_EXP, egu_sb, out=act_dram,
                      epilogue=act_epi, src_mark=m_bufd)
            m_act = stmarks(k)

            def wsc_epi(k2, m_mm, psum_ap, out_ap, aux):
                v = k2.v
                v.wait(m_mm)
                v.tensor_scalar(out=out_ap, in0=psum_ap, scalar1=aux["res"][0],
                                scalar2=None, op0=ALU.mult)
                return k2.v.mark()

            pass_gemm(k, act_dram[:, 0:I_EXP], CAP, I_EXP, D, edn_sb,
                      out=o_outtab[e * CAP:(e + 1) * CAP, :], out_dtype=F32,
                      epilogue=wsc_epi,
                      res_srcs=[(mywsc_dram[e, :].rearrange("(s one) -> s one", one=1),
                                 F32)],
                      src_mark=m_act, wait_marks=(m_myd,))
        es6.close()

        k.finalize()
    return k



def build_launch_D(To=TPC, ncores=NCORES):
    k = K()
    NTO = To // 128
    outtab_all = k.din("outtab_all", [ncores * 8 * CAP, D])
    offs_in = k.din("offs_in", [To, TOPK])
    x1_in = k.din("x1_in", [To, D])
    merged_in = k.din("merged_in", [To, D])
    sh_in = k.din("sh_in", [To, D])
    o_x2 = k.dout("x2", [To, D])

    with k.es:
        k.build_seqs()
        t, v, s, g, d, gd = k.t, k.v, k.s, k.g, k.d, k.gd
        offb = [k.sb("doff", [128, TOPK], F32) for _ in range(2)]
        offi = [k.sb("doffi", [128, TOPK], I32) for _ in range(2)]
        ysb = [k.sb("dy", [128, D], F32) for _ in range(2)]
        g8 = [[k.sb("dg8", [128, D], F32) for _ in range(TOPK)] for _ in range(2)]
        x1b = [k.sb("dx1", [128, D], F32) for _ in range(2)]
        mgb = [k.sb("dmg", [128, D], F32) for _ in range(2)]
        shb = [k.sb("dsh", [128, D], F32) for _ in range(2)]
        xo = [k.sb("dxo", [128, D], F32) for _ in range(2)]
        kmask8 = [k.sb("dkm", [128, TOPK], F32) for _ in range(2)]
        offc = k.sb("doffc", [128, TOPK], F32)
        m_xo_dma = [None, None]
        m_add_last = [None, None]
        m_g = [None, None]
        for it in range(NTO):
            p = it % 2
            sl = k.ld_sems[p]
            if m_add_last[p] is not None:
                d.wait(m_add_last[p])
            d.start(offb[p][:], offs_in[it * 128:(it + 1) * 128, :], sem=sl)
            d.start(x1b[p][:], x1_in[it * 128:(it + 1) * 128, :], sem=sl)
            d.start(mgb[p][:], merged_in[it * 128:(it + 1) * 128, :], sem=sl)
            d.start(shb[p][:], sh_in[it * 128:(it + 1) * 128, :], sem=sl)
            m_l = d.mark(sl)
            v.wait(m_l)
            v.tensor_scalar(out=offc[:], in0=offb[p][:],
                            scalar1=float(ncores * 8 * CAP - 1), scalar2=None,
                            op0=ALU.min)
            v.tensor_copy(out=offi[p][:], in_=offc[:])
            v.tensor_scalar(out=kmask8[p][:], in0=offb[p][:],
                            scalar1=float(2 ** 29), scalar2=None, op0=ALU.is_lt)
            m_ci = k.v.mark()
            gd.wait(m_ci)
            if m_add_last[p] is not None:
                gd.wait(m_add_last[p])
            for j in range(TOPK):
                gd.indirect(g8[p][j][:, :], None, outtab_all,
                            IndirectOffsetOnAxis(ap=offi[p][:, j:j + 1], axis=0),
                            sem=k.gld_sems[p])
            m_g[p] = gd.mark(k.gld_sems[p])
            gd.wait(m_g[p])
            v.wait(m_g[p])
            v.tensor_scalar(out=ysb[p][:], in0=g8[p][0][:],
                            scalar1=kmask8[p][:, 0:1], scalar2=None, op0=ALU.mult)
            for j in range(1, TOPK):
                v.scalar_tensor_tensor(out=ysb[p][:], in0=g8[p][j][:],
                                       scalar=kmask8[p][:, j:j + 1],
                                       in1=ysb[p][:], op0=ALU.mult, op1=ALU.add)
            if m_xo_dma[p] is not None:
                v.wait(m_xo_dma[p])
            v.tensor_tensor(out=xo[p][:], in0=x1b[p][:], in1=mgb[p][:], op=ALU.add)
            v.tensor_tensor(out=xo[p][:], in0=xo[p][:], in1=shb[p][:], op=ALU.add)
            v.tensor_tensor(out=xo[p][:], in0=xo[p][:], in1=ysb[p][:], op=ALU.add)
            m_add_last[p] = k.v.mark()
            d.wait(m_add_last[p])
            d.start(o_x2[it * 128:(it + 1) * 128, :], xo[p][:],
                    sem=k.st_sems[p])
            m_xo_dma[p] = d.mark(k.st_sems[p])
        k.finalize()
    return k


# version 19
# speedup vs baseline: 1.2053x; 1.0334x over previous
import numpy as np
from contextlib import ExitStack

import concourse.bass as bass
import concourse.mybir as mybir
from concourse.bass import IndirectOffsetOnAxis
from concourse.bass_utils import run_bass_kernel_spmd
import ml_dtypes

F32 = mybir.dt.float32
BF16 = mybir.dt.bfloat16
I32 = mybir.dt.int32
AF = mybir.ActivationFunctionType
ALU = mybir.AluOpType
AX = mybir.AxisListType

NCORES = 8
B, S, D = 4, 8192, 1024
TPC = S // 2
H0, HD0 = 16, 64
MLP0 = 4096
FFN0 = 3072
H1, HD1 = 4, 256
WIN = 512
NPM = 64
HID1 = 4096
E_EXP, TOPK, CAP = 64, 8, 512
I_EXP, I_SH = 512, 2048
EPS = 1e-5

bfdt = ml_dtypes.bfloat16



class Seq:

    def __init__(self, name, sem, ops, chain=True):
        self.name = name
        self.sem = sem
        self.n = 0
        self.chain = chain
        self.ops = ops

    def wait(self, mark):
        if isinstance(mark, list):
            for m in mark:
                self.wait(m)
            return
        sem, val = mark
        if val > 0:
            self.ops.append(lambda e, s=sem, v=val: e.wait_ge(s, v))

    def mark(self):
        return (self.sem, self.n)

    def __getattr__(self, name):
        if name.startswith("_"):
            raise AttributeError(name)

        def wrapper(*a, **kw):
            need_wait = self.chain and self.n > 0
            nprev = self.n
            sem = self.sem

            def run(e):
                if need_wait:
                    e.wait_ge(sem, nprev)
                getattr(e, name)(*a, **kw).then_inc(sem, 1)

            self.ops.append(run)
            self.n += 1

        return wrapper


class Dma:

    def __init__(self, name, sem, ops):
        self.name = name
        self.sem = sem
        self.counts = {id(sem): 0}
        self.sems = {id(sem): sem}
        self.ops = ops

    def wait(self, mark):
        if isinstance(mark, list):
            for m in mark:
                self.wait(m)
            return
        sem, val = mark
        if val > 0:
            self.ops.append(lambda e, s=sem, v=val: e.wait_ge(s, v))

    def mark(self, sem=None):
        sem = sem if sem is not None else self.sem
        return (sem, 16 * self.counts.get(id(sem), 0))

    def _bump(self, sem):
        self.sems[id(sem)] = sem
        self.counts[id(sem)] = self.counts.get(id(sem), 0) + 1

    def start(self, out, in_, sem=None, **kw):
        sem = sem if sem is not None else self.sem

        def run(e):
            e.dma_start(out=out, in_=in_, **kw).then_inc(sem, 16)

        self.ops.append(run)
        self._bump(sem)

    def indirect(self, out, out_offset, in_, in_offset, sem=None, **kw):
        sem = sem if sem is not None else self.sem

        def run(e):
            e.indirect_dma_start(out=out, out_offset=out_offset, in_=in_,
                                 in_offset=in_offset, **kw).then_inc(sem, 16)

        self.ops.append(run)
        self._bump(sem)

    def final_wait(self):
        for sid, sem in self.sems.items():
            n = self.counts[sid]
            if n > 0:
                self.ops.append(lambda e, s=sem, v=16 * n: e.wait_ge(s, v))


class K:
    def __init__(self):
        self.nc = bass.Bass("TRN2", target_bir_lowering=False, debug=False)
        self.es = ExitStack()
        self._uid = 0
        self.ident = None

    def build_seqs(self):
        nc, es = self.nc, self.es
        sem = lambda n: es.enter_context(nc.semaphore(n))
        gp_ops = []
        s_ops = []
        self.t = Seq("t", sem("sem_t"), [], chain=False)
        self.v = Seq("v", sem("sem_v"), [])
        self.s = Seq("s", sem("sem_s"), s_ops)
        self.g = Seq("g", sem("sem_g"), gp_ops)
        self.d = Dma("d", sem("sem_d"), [])
        self.gd = Dma("gd", sem("sem_gd"), gp_ops)
        self.sd = Dma("sd", sem("sem_sd"), s_ops)
        self.ld_sems = [sem("sem_ld0"), sem("sem_ld1"), sem("sem_ld2")]
        self.ldr_sems = [sem("sem_ldr0"), sem("sem_ldr1"), sem("sem_ldr2")]
        self.tpd_sems = [sem("sem_tpd0"), sem("sem_tpd1")]
        self.id_sem = sem("sem_id")
        self.st_sems = [sem("sem_st0"), sem("sem_st1")]
        self.gld_sems = [sem("sem_gld0"), sem("sem_gld1")]
        self.st2_sems = [sem("sem_st2a"), sem("sem_st2b")]
        self.sd_sems = [sem("sem_sda"), sem("sem_sdb")]

    def finalize(self):
        self.d.final_wait()
        self.gd.final_wait()
        self.sd.final_wait()
        with self.nc.Block() as block:
            block.tensor(lambda e: [f(e) for f in self.t.ops])
            block.vector(lambda e: [f(e) for f in self.v.ops])
            block.scalar(lambda e: [f(e) for f in self.s.ops])
            block.sync(lambda e: [f(e) for f in self.d.ops])
            block.gpsimd(lambda e: [f(e) for f in self.g.ops])

    def uname(self, base):
        self._uid += 1
        return f"{base}_{self._uid}"

    def sb(self, name, shape, dtype, scope=None):
        return (scope or self.es).enter_context(
            self.nc.sbuf_tensor(self.uname(name), shape, dtype))

    def ps(self, scope, name, shape, dtype=F32):
        return scope.enter_context(self.nc.psum_tensor(self.uname(name), shape, dtype))

    def din(self, name, shape, dtype=F32):
        return self.nc.dram_tensor(name, shape, dtype, kind="ExternalInput").ap()

    def dout(self, name, shape, dtype=F32):
        return self.nc.dram_tensor(name, shape, dtype, kind="ExternalOutput").ap()

    def dint(self, name, shape, dtype=F32):
        return self.nc.dram_tensor(name, shape, dtype).ap()

    def init_common(self, ident_in):
        self.ident = self.sb("identsb", [128, 128], BF16)
        self.d.start(self.ident[:], ident_in, sem=self.id_sem)
        self.m_ident = self.d.mark(self.id_sem)
        self.t.wait(self.m_ident)
        self.ones_r = self.sb("ones_r", [1, 128], BF16)
        self.v.memset(self.ones_r[:], 1.0)



def load_cast(k, w_hbm, sb, nchunk, width, col0=0):
    for i in range(nchunk):
        k.gd.start(sb[:, col0 + i * width:col0 + (i + 1) * width],
                   w_hbm[i * 128:(i + 1) * 128, :])
    return k.gd.mark()


def stmarks(k):
    ms = [k.d.mark(k.st_sems[0]), k.d.mark(k.st_sems[1])]
    if hasattr(k, "st2_sems"):
        ms += [k.d.mark(k.st2_sems[0]), k.d.mark(k.st2_sems[1])]
    return ms


def guard_weights(k):
    k.gd.wait(k.v.mark())
    k.gd.wait(k.s.mark())
    k.gd.wait(k.t.mark())
    k.gd.wait(stmarks(k))


def load_dequant(k, w_hbm, sb, nchunk, width, wint, scale=1.0 / 127.0):
    for i0 in range(0, nchunk, 2):
        nn = min(2, nchunk - i0)
        for j in range(nn):
            k.gd.start(wint[:, j * width:(j + 1) * width],
                       w_hbm[(i0 + j) * 128:(i0 + j + 1) * 128, :])
        m_l = k.gd.mark()
        k.v.wait(m_l)
        for j in range(nn):
            k.v.tensor_scalar(out=sb[:, (i0 + j) * width:(i0 + j + 1) * width],
                              in0=wint[:, j * width:(j + 1) * width],
                              scalar1=scale, scalar2=None, op0=ALU.mult)
        k.gd.wait(k.v.mark())
    return k.v.mark()


def pass_gemm(k, src, T, Kd, N, wsb, wcol0=0, wstride=None, bias_sb=None,
              bias_col0=0, out=None, out_dtype=BF16, epilogue=None, res_srcs=(),
              src_mark=None, wait_marks=(), src_dtype=BF16,
              store_transposed=False):
    t, v, s, d = k.t, k.v, k.s, k.d
    nk = Kd // 128
    nj = N // 512
    wstride = wstride if wstride is not None else N
    assert N % 512 == 0 and 1 <= nj <= 4
    dbl = (nj <= 2)
    ntiles = T // 128
    es = ExitStack()
    with es:
        xin = [k.sb("gx", [128, Kd], src_dtype, es) for _ in range(3)]
        xT = [k.sb("gxT", [128, Kd], BF16, es) for _ in range(2)]
        osb = [k.sb("gout", [128, N], out_dtype, es) for _ in range(2)]
        tbo = ([k.sb("gtb", [128, N], out_dtype, es) for _ in range(2)]
               if store_transposed else None)
        resb = [[k.sb("gres", [128, rs.shape[1]], dt_, es) for _ in range(3)]
                for (rs, dt_) in res_srcs]
        p_tp = [k.ps(es, "gp_tp", [128, 1024], src_dtype) for _ in range(2)]
        p_out = [k.ps(es, "gp_out", [128, N], F32) for _ in range(2 if dbl else 1)]
        identx = k.ident[:]
        m_idf = None
        if src_dtype == F32:
            identf_g = k.sb("gidf", [128, 128], F32, es)
            k.v.wait(k.m_ident)
            k.v.tensor_copy(out=identf_g[:], in_=k.ident[:])
            m_idf = k.v.mark()
            identx = identf_g[:]

        entry_st = stmarks(k)

        def load_x(i):
            if src_mark is not None:
                d.wait(src_mark)
            for m in wait_marks:
                d.wait(m)
            d.wait(entry_st)
            sl = k.ld_sems[i % 3]
            d.start(xin[i % 3][:], src[i * 128:(i + 1) * 128, :], sem=sl)
            return d.mark(sl)

        def load_res(i):
            if not res_srcs:
                return None
            d.wait(entry_st)
            sl = k.ldr_sems[i % 3]
            for ri, (rs, _) in enumerate(res_srcs):
                d.start(resb[ri][i % 3][:], rs[i * 128:(i + 1) * 128, :], sem=sl)
            return d.mark(sl)

        marks_x = {}
        marks_r = {}
        for i in range(min(3, ntiles)):
            marks_x[i] = load_x(i)
            marks_r[i] = load_res(i)

        m_mm = {}
        m_xT = {}
        m_ep = {}
        m_out_dma = [entry_st, entry_st]

        last_round = (nk + 7) // 8 - 1

        def emit_transposes(i):
            if src_dtype == BF16:
                d.wait(marks_x[i])
                if i - 2 >= 0:
                    d.wait(m_mm[i - 2])
                sl = k.tpd_sems[i % 2]
                d.start(xT[i % 2][:].rearrange("p (c f) -> p c f", c=nk),
                        xin[i % 3][:].rearrange("p (c f) -> p c f", c=nk),
                        transpose=True, sem=sl)
                m_xT[(i, last_round)] = d.mark(sl)
                return
            t.wait(marks_x[i])
            if m_idf is not None:
                t.wait(m_idf)
            for r in range((nk + 7) // 8):
                if (i - 2, r) in m_xT:
                    t.wait(m_xT[(i - 2, r)])
                if (i, r - 1) in m_xT:
                    t.wait(m_xT[(i, r - 1)])
                c0, c1 = r * 8, min(r * 8 + 8, nk)
                for c in range(c0, c1):
                    t.transpose(p_tp[i % 2][:, (c - c0) * 128:(c - c0 + 1) * 128],
                                xin[i % 3][:, c * 128:(c + 1) * 128], identx)
                m_tpr = t.mark()
                v.wait(m_tpr)
                if i - 2 >= 0:
                    v.wait(m_mm[i - 2])
                v.tensor_copy(out=xT[i % 2][:, c0 * 128:c1 * 128],
                              in_=p_tp[i % 2][:, 0:(c1 - c0) * 128])
                m_xT[(i, r)] = v.mark()

        emit_transposes(0)
        if ntiles > 1:
            emit_transposes(1)

        for it in range(ntiles):
            if it + 3 < ntiles:
                d.wait(m_xT[(it, last_round)])
                marks_x[it + 3] = load_x(it + 3)
            pb = p_out[it % 2] if dbl else p_out[0]
            t.wait(m_xT[(it, last_round)])
            gate = m_ep.get(it - 2 if dbl else it - 1)
            if gate is not None:
                t.wait(gate)
            for c in range(nk):
                st = xT[it % 2][:, c * 128:(c + 1) * 128]
                for j in range(nj):
                    t.matmul(pb[:, j * 512:(j + 1) * 512], st,
                             wsb[:, wcol0 + c * wstride + j * 512:
                                 wcol0 + c * wstride + (j + 1) * 512],
                             start=(c == 0),
                             stop=(bias_sb is None and c == nk - 1))
            if bias_sb is not None:
                for j in range(nj):
                    t.matmul(pb[:, j * 512:(j + 1) * 512], k.ones_r[0:1, :],
                             bias_sb[0:1, bias_col0 + j * 512:bias_col0 + (j + 1) * 512],
                             start=False, stop=True)
            m_mm[it] = t.mark()
            if it + 2 < ntiles:
                emit_transposes(it + 2)
            aux = {"res": [rb[it % 3][:] for rb in resb], "tile": it,
                   "m_in": [marks_x[it]] + ([marks_r[it]] if res_srcs else [])}
            if res_srcs:
                v.wait(marks_r[it])
            v.wait(m_out_dma[it % 2])
            s.wait(m_out_dma[it % 2])
            if epilogue is None:
                v.wait(m_mm[it])
                if resb:
                    v.tensor_tensor(out=osb[it % 2][:], in0=pb[:],
                                    in1=aux["res"][0], op=ALU.add)
                    for rb in aux["res"][1:]:
                        v.tensor_tensor(out=osb[it % 2][:], in0=osb[it % 2][:],
                                        in1=rb, op=ALU.add)
                else:
                    v.tensor_copy(out=osb[it % 2][:], in_=pb[:])
                m_ep[it] = k.v.mark()
            else:
                m_ep[it] = epilogue(k, m_mm[it], pb[:], osb[it % 2][:], aux)
            d.wait(m_ep[it])
            if store_transposed:
                nkN = N // 128
                d.wait(m_out_dma[it % 2])
                d.start(tbo[it % 2][:].rearrange("p (c f) -> p c f", c=nkN),
                        osb[it % 2][:].rearrange("p (c f) -> p c f", c=nkN),
                        transpose=True, sem=k.st2_sems[it % 2])
                m_tb = d.mark(k.st2_sems[it % 2])
                d.wait(m_tb)
                d.start(out[:, it * 128:(it + 1) * 128]
                        .rearrange("(c p) f -> p c f", c=nkN),
                        tbo[it % 2][:].rearrange("p (c f) -> p c f", c=nkN),
                        sem=k.st_sems[it % 2])
            else:
                d.start(out[it * 128:(it + 1) * 128, :], osb[it % 2][:],
                        sem=k.st_sems[it % 2])
            m_out_dma[it % 2] = d.mark(k.st_sems[it % 2])
            if it + 3 < ntiles:
                marks_r[it + 3] = load_res(it + 3)
    return stmarks(k)


def emit_ln_core(k, x_ap, width, sm, scr_ap, rstd_ap, want_var=False):
    v, s = k.v, k.s
    v.tensor_reduce(out=sm[:, 0:1], in_=x_ap, axis=AX.X, op=ALU.add)
    v.tensor_scalar(out=sm[:, 0:1], in0=sm[:, 0:1], scalar1=1.0 / width,
                    scalar2=None, op0=ALU.mult)
    s.wait(k.v.mark())
    s.activation(out=scr_ap, in_=x_ap, func=AF.Square, accum_out=sm[:, 1:2])
    v.wait(k.s.mark())
    v.tensor_scalar(out=sm[:, 2:3], in0=sm[:, 1:2], scalar1=1.0 / width,
                    scalar2=None, op0=ALU.mult)
    v.tensor_tensor(out=sm[:, 3:4], in0=sm[:, 0:1], in1=sm[:, 0:1], op=ALU.mult)
    v.tensor_tensor(out=sm[:, 2:3], in0=sm[:, 2:3], in1=sm[:, 3:4], op=ALU.subtract)
    v.tensor_scalar(out=sm[:, 6:7], in0=sm[:, 2:3], scalar1=EPS, scalar2=None,
                    op0=ALU.add)
    s.wait(k.v.mark())
    s.activation(out=rstd_ap, in_=sm[:, 6:7], func=AF.Sqrt)
    v.wait(k.s.mark())
    v.reciprocal(out=rstd_ap, in_=rstd_ap)
    return k.v.mark()



def build_launch_A(T=TPC):
    k = K()
    ntiles = T // 128

    x0 = k.din("x0", [T, D])
    wq = k.din("wq", [D, D]); wk = k.din("wk", [D, D]); wv = k.din("wv", [D, D])
    woH = k.din("woH", [64, 16 * D])
    fcw = k.din("fcw", [D, MLP0]); projw = k.din("projw", [MLP0, D])
    wgT = k.din("wgT", [D, FFN0], I32); wuT = k.din("wuT", [D, FFN0], I32)
    wdT = k.din("wdT", [FFN0, D], I32)
    g1b = k.din("g1b", [128, D]); b1b = k.din("b1b", [128, D])
    g2b = k.din("g2b", [128, D]); b2b = k.din("b2b", [128, D])
    bq_r = k.din("bq_r", [1, D], BF16); bk_r = k.din("bk_r", [1, D], BF16)
    bv_r = k.din("bv_r", [1, D], BF16); bo_r = k.din("bo_r", [1, D], BF16)
    fcb_r = k.din("fcb_r", [1, MLP0], BF16); projb_r = k.din("projb_r", [1, D], BF16)
    gatew = k.din("gatew", [D, 65])
    identb = k.din("identb", [128, 128], BF16)
    bdm = k.din("bdm", [128, 128])
    iota64 = k.din("iota64", [128, 64])

    o_x1 = k.dout("x1", [T, D])
    o_n1 = k.dout("n1", [T, D], BF16)
    o_idx8 = k.dout("idx8", [T, TOPK])
    o_w8 = k.dout("w8", [T, TOPK])
    o_counts = k.dout("counts", [1, 64])
    o_sgate = k.dout("sgate", [T, 1])

    n0f_dram = k.dint("n0f_dram", [T, D])
    n0b_dram = k.dint("n0b_dram", [T, D], BF16)
    xn2_dram = k.dint("xn2_dram", [T, D], BF16)
    fq_dram = k.dint("fq_dram", [T, D], BF16)
    fk_dram = k.dint("fk_dram", [T, D], BF16)
    v_dram = k.dint("v_dram", [T, D], BF16)
    ea_dram = k.dint("ea_dram", [ntiles * 64, 2048], BF16)
    a_dram = k.dint("a_dram", [T, D])
    t2_dram = k.dint("t2_dram", [T, D], BF16)
    gg_dram = k.dint("gg_dram", [T, MLP0], BF16)
    brA_dram = k.dint("brA_dram", [T, D])
    gu_dram = k.dint("gu_dram", [T, FFN0], BF16)
    sg_dram = k.dint("sg_dram", [T, FFN0], BF16)

    with k.es:
        k.build_seqs()
        t, v, s, g, d, gd = k.t, k.v, k.s, k.g, k.d, k.gd
        k.init_common(identb)

        bd_sb = k.sb("bd", [128, 128], F32)
        iota_sb = k.sb("iota64s", [128, 64], F32)
        g1_sb = k.sb("g1s", [128, D], F32); b1_sb = k.sb("b1s", [128, D], F32)
        g2_sb = k.sb("g2s", [128, D], F32); b2_sb = k.sb("b2s", [128, D], F32)
        bias_q = k.sb("biasq", [1, D], BF16); bias_k = k.sb("biask", [1, D], BF16)
        bias_v = k.sb("biasv", [1, D], BF16); bias_o = k.sb("biaso", [1, D], BF16)
        bias_fc = k.sb("biasfc", [1, MLP0], BF16); bias_pj = k.sb("biaspj", [1, D], BF16)
        for dst, srcin in ((bd_sb, bdm), (iota_sb, iota64), (g1_sb, g1b), (b1_sb, b1b),
                           (g2_sb, g2b), (b2_sb, b2b), (bias_q, bq_r), (bias_k, bk_r),
                           (bias_v, bv_r), (bias_o, bo_r), (bias_fc, fcb_r),
                           (bias_pj, projb_r)):
            d.start(dst[:], srcin)
        m_const = d.mark()
        v.wait(m_const)
        eluscr = k.sb("eluscr", [128, D], F32)
        ln9 = {}
        ln9["scr"] = k.sb("p9scr", [128, D], F32)
        ln9["sm"] = k.sb("p9sm", [128, 8], F32)
        ln9["rstd"] = k.sb("p9rstd", [128, 1], F32)
        ln9["n1b"] = [k.sb("p9n1b", [128, D], BF16) for _ in range(2)]
        ln9["m_n1dma"] = [None, None]

        with ExitStack() as es1:
            xin = [k.sb("p1x", [128, D], F32, es1) for _ in range(2)]
            n0f = k.sb("p1n0f", [128, D], F32, es1)
            n0b = k.sb("p1n0b", [128, D], BF16, es1)
            xn2 = k.sb("p1xn2", [128, D], BF16, es1)
            scr = k.sb("p1scr", [128, D], F32, es1)
            sm = k.sb("p1sm", [128, 8], F32, es1)
            rstd = k.sb("p1rstd", [128, 2], F32, es1)
            entry_st1 = stmarks(k)
            d.wait(entry_st1)
            d.start(xin[0][:], x0[0:128, :], sem=k.ld_sems[0])
            mk = [d.mark(k.ld_sems[0])]
            if ntiles > 1:
                d.start(xin[1][:], x0[128:256, :], sem=k.ld_sems[1])
                mk.append(d.mark(k.ld_sems[1]))
            m_st = [None, None]
            for it in range(ntiles):
                xs = xin[it % 2]
                v.wait(mk[it])
                if it >= 1 and m_st[(it - 1) % 2] is not None:
                    v.wait(m_st[(it - 1) % 2])
                    s.wait(m_st[(it - 1) % 2])
                emit_ln_core(k, xs[:], D, sm, scr[:], rstd[:, 0:1])
                v.tensor_scalar(out=n0f[:], in0=xs[:], scalar1=sm[:, 0:1],
                                scalar2=rstd[:, 0:1], op0=ALU.subtract, op1=ALU.mult)
                v.tensor_copy(out=n0b[:], in_=n0f[:])
                v.tensor_scalar(out=sm[:, 4:5], in0=sm[:, 2:3], scalar1=EPS,
                                scalar2=None, op0=ALU.add)
                v.reciprocal(out=sm[:, 5:6], in_=sm[:, 4:5])
                v.tensor_tensor(out=sm[:, 5:6], in0=sm[:, 2:3], in1=sm[:, 5:6],
                                op=ALU.mult)
                v.tensor_scalar(out=sm[:, 5:6], in0=sm[:, 5:6], scalar1=EPS,
                                scalar2=None, op0=ALU.add)
                s.wait(v.mark())
                s.activation(out=rstd[:, 1:2], in_=sm[:, 5:6], func=AF.Sqrt)
                v.wait(s.mark())
                v.reciprocal(out=rstd[:, 1:2], in_=rstd[:, 1:2])
                v.tensor_scalar(out=scr[:], in0=n0f[:], scalar1=rstd[:, 1:2],
                                scalar2=None, op0=ALU.mult)
                v.tensor_tensor(out=scr[:], in0=scr[:], in1=g1_sb[:], op=ALU.mult)
                v.tensor_tensor(out=xn2[:], in0=scr[:], in1=b1_sb[:], op=ALU.add)
                m_v = v.mark()
                d.wait(m_v)
                ss = k.st_sems[it % 2]
                d.start(n0f_dram[it * 128:(it + 1) * 128, :], n0f[:], sem=ss)
                d.start(n0b_dram[it * 128:(it + 1) * 128, :], n0b[:], sem=ss)
                d.start(xn2_dram[it * 128:(it + 1) * 128, :], xn2[:], sem=ss)
                m_st[it % 2] = d.mark(ss)
                if it + 2 < ntiles:
                    d.wait(m_v)
                    d.start(xin[it % 2][:], x0[(it + 2) * 128:(it + 3) * 128, :],
                            sem=k.ld_sems[it % 2])
                    mk.append(d.mark(k.ld_sems[it % 2]))
            m_p1 = [d.mark(k.st_sems[0]), d.mark(k.st_sems[1])]

        es2w = ExitStack()
        wq_sb = k.sb("wq_sb", [128, 8 * D], BF16, es2w)
        wk_sb = k.sb("wk_sb", [128, 8 * D], BF16, es2w)
        wv_sb = k.sb("wv_sb", [128, 8 * D], BF16, es2w)
        guard_weights(k)
        load_cast(k, wq, wq_sb, 8, D)
        load_cast(k, wk, wk_sb, 8, D)
        load_cast(k, wv, wv_sb, 8, D)
        m_wqkv = gd.mark()
        t.wait(m_wqkv)
        t.wait(m_const)

        def mk_elu_epi():
            scr1 = eluscr

            def epi(k2, m_mm, psum_ap, out_ap, aux):
                v, s = k2.v, k2.s
                v.wait(m_mm)
                v.tensor_scalar(out=scr1[:], in0=psum_ap, scalar1=0.0, scalar2=None,
                                op0=ALU.min)
                s.wait(v.mark())
                s.activation(out=out_ap, in_=scr1[:], func=AF.Exp)
                v.wait(s.mark())
                v.tensor_scalar(out=scr1[:], in0=psum_ap, scalar1=0.0, scalar2=None,
                                op0=ALU.max)
                v.tensor_tensor(out=out_ap, in0=out_ap, in1=scr1[:], op=ALU.add)
                return k2.v.mark()

            return epi

        pass_gemm(k, xn2_dram, T, D, D, wq_sb, bias_sb=bias_q, out=fq_dram,
                  epilogue=mk_elu_epi(), src_mark=m_p1)
        pass_gemm(k, xn2_dram, T, D, D, wk_sb, bias_sb=bias_k, out=fk_dram,
                  epilogue=mk_elu_epi(), src_mark=m_p1)
        m_p2v = pass_gemm(k, xn2_dram, T, D, D, wv_sb, bias_sb=bias_v, out=v_dram,
                          src_mark=m_p1, out_dtype=BF16)
        m_p2 = stmarks(k)
        es2w.close()

        with ExitStack() as es3:
            fqi = [k.sb("p3fq", [128, D], BF16, es3) for _ in range(2)]
            fki = [k.sb("p3fk", [128, D], BF16, es3) for _ in range(2)]
            vvi = [k.sb("p3v", [128, D], BF16, es3) for _ in range(2)]
            FK = k.sb("p3FK", [64, 16 * 128], BF16, es3)
            FQ = k.sb("p3FQ", [64, 16 * 128], BF16, es3)
            VE = k.sb("p3VE", [64, 16 * 128], BF16, es3)
            Mmask4 = [k.sb("p3Mm", [128, 512], BF16, es3) for _ in range(2)]
            V8all = [k.sb("p3V8", [128, 16 * 64], BF16, es3) for _ in range(2)]
            qkall = k.sb("p3qk", [128, 16], F32, es3)
            attn_raw = k.sb("p3araw", [128, 16 * 64], F32, es3)
            ones_c = k.sb("p3ones", [128, 1], BF16, es3)
            attn = k.sb("p3attn", [128, 16 * 64], BF16, es3)
            EA = k.sb("p3EA", [64, 2048], BF16, es3)
            p_hT = k.ps(es3, "p3phT", [64, 2048], BF16)
            p_M4 = [k.ps(es3, "p3pM", [128, 512], F32) for _ in range(2)]
            p_qkv4 = [k.ps(es3, "p3pqkv", [128, 260], F32) for _ in range(2)]
            p_eT = p_hT

            v.memset(ones_c[:], 1.0)

            entry_st3 = stmarks(k)

            def load3(i):
                d.wait(m_p2)
                d.wait(entry_st3)
                sl = k.ld_sems[i % 2]
                d.start(fqi[i % 2][:], fq_dram[i * 128:(i + 1) * 128, :], sem=sl)
                d.start(fki[i % 2][:], fk_dram[i * 128:(i + 1) * 128, :], sem=sl)
                d.start(vvi[i % 2][:], v_dram[i * 128:(i + 1) * 128, :], sem=sl)
                return d.mark(sl)

            mk3 = [load3(0)]
            if ntiles > 1:
                mk3.append(load3(1))

            m_head_evac = None
            m_mask4 = [None, None]
            m_qke4 = [None, None]
            m_qkv_done = [None, None]
            m_v8_prev = None
            m_ea_dma = None
            m_eT_prev = None
            m_groups_done = None
            for it in range(ntiles):
                t.wait(mk3[it])
                if m_ea_dma is not None:
                    t.wait(m_ea)
                for (srcb, dstb) in ((fki[it % 2], FK), (fqi[it % 2], FQ),
                                     (vvi[it % 2], VE)):
                    if m_head_evac is not None:
                        t.wait(m_head_evac)
                    for h in range(16):
                        t.transpose(p_hT[:, h * 128:(h + 1) * 128],
                                    srcb[:, h * 64:(h + 1) * 64], k.ident[:])
                    m_hT = t.mark()
                    v.wait(m_hT)
                    src_ap = p_hT[:].rearrange("p (h tk) -> p tk h", h=16)
                    dst_ap = dstb[:].rearrange("p (tk h) -> p tk h", h=16)
                    if it > 0:
                        v.wait(m_groups_done)
                        if m_v8_prev is not None:
                            v.wait(m_v8_prev)
                    v.tensor_copy(out=dst_ap, in_=src_ap)
                    m_head_evac = v.mark()

                k.s.wait(m_head_evac)
                if m_qkv_done[it % 2] is not None:
                    k.s.wait(m_qkv_done[it % 2])
                k.sd.start(V8all[it % 2][:].rearrange("p (c f) -> p c f", c=16),
                           VE[:].rearrange("p (c f) -> p c f", c=16),
                           transpose=True, sem=k.sd_sems[it % 2])
                m_v8 = k.sd.mark(k.sd_sems[it % 2])
                m_v8_prev = m_v8
                k.s.wait(m_v8)

                for bk in range(5):
                    pp = bk % 2
                    if bk < 4:
                        t.wait(m_head_evac)
                        if m_mask4[pp] is not None:
                            t.wait(m_mask4[pp])
                        for g4 in range(4):
                            gi = bk * 4 + g4
                            t.matmul(p_M4[pp][:, g4 * 128:(g4 + 1) * 128],
                                     FK[:, gi * 128:(gi + 1) * 128],
                                     FQ[:, gi * 128:(gi + 1) * 128],
                                     start=(g4 == 0), stop=(g4 == 3))
                        m_M4 = t.mark()
                        v.wait(m_M4)
                        bdb = bd_sb[:].unsqueeze(1).to_broadcast([128, 4, 128])
                        v.tensor_tensor(
                            out=Mmask4[pp][:].rearrange("p (g f) -> p g f", g=4),
                            in0=p_M4[pp][:].rearrange("p (g f) -> p g f", g=4),
                            in1=bdb, op=ALU.mult)
                        m_mask4[pp] = k.v.mark()
                    if bk >= 1:
                        bj = bk - 1
                        qq = bj % 2
                        t.wait(m_mask4[qq])
                        t.wait(m_v8)
                        if m_qke4[qq] is not None:
                            t.wait(m_qke4[qq])
                        for g4 in range(4):
                            gj = bj * 4 + g4
                            st4 = Mmask4[qq][:, g4 * 128:(g4 + 1) * 128]
                            t.matmul(p_qkv4[qq][:, g4 * 65:g4 * 65 + 64], st4,
                                     V8all[it % 2][:, gj * 64:(gj + 1) * 64],
                                     start=(g4 == 0), stop=False)
                            t.matmul(p_qkv4[qq][:, g4 * 65 + 64:g4 * 65 + 65],
                                     st4, ones_c[:, 0:1],
                                     start=False, stop=(g4 == 3))
                        m_qk4 = t.mark()
                        if bk == 4:
                            m_qkv_done[it % 2] = m_qk4
                        v.wait(m_qk4)
                        p3v = p_qkv4[qq][:].rearrange("p (g f) -> p g f", f=65)
                        v.tensor_copy(
                            out=attn_raw[:, bj * 256:(bj + 1) * 256]
                            .rearrange("p (g f) -> p g f", g=4),
                            in_=p3v[:, :, 0:64])
                        v.tensor_copy(
                            out=qkall[:, bj * 4:(bj + 1) * 4]
                            .rearrange("p (g one) -> p g one", one=1),
                            in_=p3v[:, :, 64:65])
                        m_qke4[qq] = k.v.mark()
                v.tensor_scalar(out=qkall[:], in0=qkall[:], scalar1=1e-6,
                                scalar2=None, op0=ALU.add)
                v.reciprocal(out=qkall[:], in_=qkall[:])
                if m_eT_prev is not None:
                    v.wait(m_eT_prev)
                r8b = qkall[:].unsqueeze(2).to_broadcast([128, 16, 64])
                v.tensor_tensor(out=attn[:].rearrange("p (g f) -> p g f", g=16),
                                in0=attn_raw[:].rearrange("p (g f) -> p g f", g=16),
                                in1=r8b, op=ALU.mult)
                m_groups_done = k.v.mark()

                t.wait(m_groups_done)
                if m_ea_dma is not None:
                    t.wait(m_ea)
                for gi2 in range(16):
                    t.transpose(p_eT[:, gi2 * 128:(gi2 + 1) * 128],
                                attn[:, gi2 * 64:(gi2 + 1) * 64], k.ident[:])
                m_eT = t.mark()
                m_eT_prev = m_eT
                v.wait(m_eT)
                if m_ea_dma is not None:
                    v.wait(m_ea_dma)
                src_ap = p_eT[:].rearrange("p (gg tk h) -> p h gg tk", gg=16, tk=8)
                dst_ap = EA[:].rearrange("p (h gg tk) -> p h gg tk", h=16, gg=16)
                v.tensor_copy(out=dst_ap, in_=src_ap)
                m_ea = v.mark()
                d.wait(m_ea)
                d.start(ea_dram[it * 64:(it + 1) * 64, :], EA[:], sem=k.st_sems[0])
                m_ea_dma = d.mark(k.st_sems[0])
                if it + 2 < ntiles:
                    t_ok = k.t.mark()
                    d.wait(t_ok)
                    mk3.append(load3(it + 2))
            m_p3 = k.d.mark(k.st_sems[0])

        es4w = ExitStack()
        woH_sb = k.sb("woH_sb", [64, 16 * D], BF16, es4w)
        guard_weights(k)
        gd.start(woH_sb[:], woH)
        m_wo = gd.mark()
        with ExitStack() as es4:
            eain = [k.sb("p4ea", [64, 2048], BF16, es4) for _ in range(2)]
            resb = [k.sb("p4res", [128, D], F32, es4) for _ in range(2)]
            osb = [k.sb("p4o", [128, D], F32, es4) for _ in range(2)]
            p_out = k.ps(es4, "p4pout", [128, D], F32)

            entry_st4 = stmarks(k)

            def load4(i):
                d.wait(entry_st4)
                d.wait(m_p1)
                sl = k.ld_sems[i % 2]
                d.start(eain[i % 2][:], ea_dram[i * 64:(i + 1) * 64, :], sem=sl)
                d.start(resb[i % 2][:], n0f_dram[i * 128:(i + 1) * 128, :], sem=sl)
                return d.mark(sl)

            mk4 = [load4(0)]
            if ntiles > 1:
                mk4.append(load4(1))
            t.wait(m_wo)
            m_ep = None
            m_odma = [None, None]
            for it in range(ntiles):
                t.wait(mk4[it])
                if m_ep is not None:
                    t.wait(m_ep)
                for h in range(16):
                    st = eain[it % 2][:, h * 128:(h + 1) * 128]
                    t.matmul(p_out[:, 0:512], st, woH_sb[:, h * D:h * D + 512],
                             start=(h == 0), stop=False)
                    t.matmul(p_out[:, 512:1024], st, woH_sb[:, h * D + 512:(h + 1) * D],
                             start=(h == 0), stop=False)
                t.matmul(p_out[:, 0:512], k.ones_r[0:1, :], bias_o[0:1, 0:512],
                         start=False, stop=True)
                t.matmul(p_out[:, 512:1024], k.ones_r[0:1, :], bias_o[0:1, 512:1024],
                         start=False, stop=True)
                m_mm = t.mark()
                v.wait(m_mm)
                if m_odma[it % 2] is not None:
                    v.wait(m_odma[it % 2])
                v.tensor_tensor(out=osb[it % 2][:], in0=p_out[:], in1=resb[it % 2][:],
                                op=ALU.add)
                m_ep = v.mark()
                d.wait(m_ep)
                d.start(a_dram[it * 128:(it + 1) * 128, :], osb[it % 2][:],
                        sem=k.st_sems[it % 2])
                m_odma[it % 2] = d.mark(k.st_sems[it % 2])
                if it + 2 < ntiles:
                    d.wait(m_mm)
                    d.wait(m_ep)
                    mk4.append(load4(it + 2))
            m_p4 = [k.d.mark(k.st_sems[0]), k.d.mark(k.st_sems[1])]
        es4w.close()

        with ExitStack() as es5:
            ain = [k.sb("p5a", [128, D], F32, es5) for _ in range(2)]
            t2 = k.sb("p5t2", [128, D], BF16, es5)
            scr = k.sb("p5scr", [128, D], F32, es5)
            sm = k.sb("p5sm", [128, 8], F32, es5)
            rstd = k.sb("p5rstd", [128, 1], F32, es5)
            entry_st5 = stmarks(k)
            d.wait(entry_st5)
            d.start(ain[0][:], a_dram[0:128, :], sem=k.ld_sems[0])
            mk5 = [d.mark(k.ld_sems[0])]
            if ntiles > 1:
                d.start(ain[1][:], a_dram[128:256, :], sem=k.ld_sems[1])
                mk5.append(d.mark(k.ld_sems[1]))
            m_st = [None, None]
            for it in range(ntiles):
                xs = ain[it % 2]
                v.wait(mk5[it])
                if it >= 1 and m_st[(it - 1) % 2] is not None:
                    v.wait(m_st[(it - 1) % 2])
                    s.wait(m_st[(it - 1) % 2])
                emit_ln_core(k, xs[:], D, sm, scr[:], rstd[:])
                v.tensor_scalar(out=scr[:], in0=xs[:], scalar1=sm[:, 0:1],
                                scalar2=rstd[:], op0=ALU.subtract, op1=ALU.mult)
                v.tensor_tensor(out=scr[:], in0=scr[:], in1=g2_sb[:], op=ALU.mult)
                v.tensor_tensor(out=t2[:], in0=scr[:], in1=b2_sb[:], op=ALU.add)
                m_v = v.mark()
                d.wait(m_v)
                d.start(t2_dram[it * 128:(it + 1) * 128, :], t2[:],
                        sem=k.st_sems[it % 2])
                m_st[it % 2] = d.mark(k.st_sems[it % 2])
                if it + 2 < ntiles:
                    d.wait(m_v)
                    d.start(ain[it % 2][:], a_dram[(it + 2) * 128:(it + 3) * 128, :],
                            sem=k.ld_sems[it % 2])
                    mk5.append(d.mark(k.ld_sems[it % 2]))
            m_p5 = [d.mark(k.st_sems[0]), d.mark(k.st_sems[1])]

        es6w = ExitStack()
        fc_sb = k.sb("fc_sb", [128, 8 * MLP0], BF16, es6w)
        guard_weights(k)
        load_cast(k, fcw, fc_sb, 8, MLP0)
        m_wfc = gd.mark()
        t.wait(m_wfc)

        def gelu_epi(k2, m_mm, psum_ap, out_ap, aux):
            k2.s.wait(m_mm)
            k2.s.activation(out=out_ap, in_=psum_ap, func=AF.Gelu)
            return k2.s.mark()

        for half in range(2):
            pass_gemm(k, t2_dram, T, D, 2048, fc_sb, wcol0=half * 2048, wstride=MLP0,
                      bias_sb=bias_fc, bias_col0=half * 2048,
                      out=gg_dram[:, half * 2048:half * 2048 + 2048],
                      epilogue=gelu_epi, src_mark=m_p5)
        m_p6 = stmarks(k)
        es6w.close()

        es7w = ExitStack()
        pj_sb = k.sb("pj_sb", [128, 32 * D], BF16, es7w)
        guard_weights(k)
        load_cast(k, projw, pj_sb, 32, D)
        m_wpj = gd.mark()
        t.wait(m_wpj)
        pass_gemm(k, gg_dram, T, MLP0, D, pj_sb, bias_sb=bias_pj, out=brA_dram,
                  out_dtype=F32, res_srcs=[(a_dram, F32)], src_mark=m_p6)
        m_p7 = stmarks(k)
        es7w.close()

        es8w = ExitStack()
        wg_sb = k.sb("wg_sb", [128, 8 * FFN0], BF16, es8w)
        wu_sb = k.sb("wu_sb", [128, 8 * FFN0], BF16, es8w)
        wint = k.sb("wint", [128, 2 * FFN0], I32, es8w)
        guard_weights(k)
        load_dequant(k, wgT, wg_sb, 8, FFN0, wint)
        m_wg = k.v.mark()
        load_dequant(k, wuT, wu_sb, 8, FFN0, wint)
        m_wu = k.v.mark()
        t.wait(m_wg)
        t.wait(m_wu)

        def silu_epi(k2, m_mm, psum_ap, out_ap, aux):
            k2.s.wait(m_mm)
            k2.s.activation(out=out_ap, in_=psum_ap, func=AF.Silu)
            return k2.s.mark()

        for half in range(2):
            pass_gemm(k, n0b_dram, T, D, 1536, wg_sb, wcol0=half * 1536, wstride=FFN0,
                      out=sg_dram[:, half * 1536:half * 1536 + 1536],
                      epilogue=silu_epi, src_mark=m_p1)
        m_p8a = stmarks(k)

        def upmul_epi(k2, m_mm, psum_ap, out_ap, aux):
            k2.v.wait(m_mm)
            k2.v.tensor_tensor(out=out_ap, in0=psum_ap, in1=aux["res"][0], op=ALU.mult)
            return k2.v.mark()

        for half in range(2):
            pass_gemm(k, n0b_dram, T, D, 1536, wu_sb, wcol0=half * 1536, wstride=FFN0,
                      out=gu_dram[:, half * 1536:half * 1536 + 1536],
                      epilogue=upmul_epi,
                      res_srcs=[(sg_dram[:, half * 1536:half * 1536 + 1536], BF16)],
                      src_mark=m_p8a)
        m_p8 = stmarks(k)
        es8w.close()

        es9w = ExitStack()
        wd_sb = k.sb("wd_sb", [128, 24 * D], BF16, es9w)
        wint9 = k.sb("wint9", [128, 2 * D], I32, es9w)
        guard_weights(k)
        load_dequant(k, wdT, wd_sb, 24, D, wint9)
        m_wd = k.v.mark()
        t.wait(m_wd)

        def x1n1_epi(k2, m_mm, psum_ap, out_ap, aux):
            v, s, d = k2.v, k2.s, k2.d
            it = aux["tile"]
            v.wait(m_mm)
            v.tensor_tensor(out=out_ap, in0=psum_ap, in1=aux["res"][0], op=ALU.add)
            v.tensor_tensor(out=out_ap, in0=out_ap, in1=aux["res"][1], op=ALU.add)
            if ln9["m_n1dma"][it % 2] is not None:
                v.wait(ln9["m_n1dma"][it % 2])
                s.wait(ln9["m_n1dma"][it % 2])
            emit_ln_core(k2, out_ap, D, ln9["sm"], ln9["scr"][:], ln9["rstd"][:])
            v.tensor_scalar(out=ln9["n1b"][it % 2][:], in0=out_ap,
                            scalar1=ln9["sm"][:, 0:1], scalar2=ln9["rstd"][:],
                            op0=ALU.subtract, op1=ALU.mult)
            m_v = k2.v.mark()
            d.wait(m_v)
            d.start(o_n1[it * 128:(it + 1) * 128, :], ln9["n1b"][it % 2][:],
                    sem=k2.st2_sems[it % 2])
            ln9["m_n1dma"][it % 2] = k2.d.mark(k2.st2_sems[it % 2])
            return m_v

        pass_gemm(k, gu_dram, T, FFN0, D, wd_sb, out=o_x1, out_dtype=F32,
                  epilogue=x1n1_epi,
                  res_srcs=[(brA_dram, F32), (x0, F32)], src_mark=m_p8)
        m_p9 = stmarks(k)
        es9w.close()

        gate_sb = k.sb("gate_sbuf", [128, 8 * 65], F32)
        guard_weights(k)
        load_cast(k, gatew, gate_sb, 8, 65)
        m_gw = gd.mark()
        t.wait(m_gw)
        with ExitStack() as es11:
            nin = [k.sb("p11n", [128, D], F32, es11) for _ in range(2)]
            nT = k.sb("p11nT", [128, D], F32, es11)
            n1f11 = k.sb("p11n1f", [128, D], F32, es11)
            scr11 = k.sb("p11scr", [128, D], F32, es11)
            sm11b = k.sb("p11smb", [128, 8], F32, es11)
            rstd11 = k.sb("p11rstd", [128, 1], F32, es11)
            probs = k.sb("p11probs", [128, 64], F32, es11)
            pm8 = k.sb("p11pm8", [128, 8], F32, es11)
            idx8s = [k.sb("p11idx", [128, 8], F32, es11) for _ in range(2)]
            w8s = [k.sb("p11w8", [128, 8], F32, es11) for _ in range(2)]
            used = k.sb("p11used", [128, 64], F32, es11)
            cand = k.sb("p11cand", [128, 64], F32, es11)
            eqm = k.sb("p11eqm", [128, 64], F32, es11)
            ohe = k.sb("p11ohe", [128, 8 * 64], F32, es11)
            ohr = k.sb("p11ohr", [128, 64], F32, es11)
            counts_sb = k.sb("p11cnt", [1, 64], F32, es11)
            sm11 = k.sb("p11sm", [128, 4], F32, es11)
            onec = k.sb("p11onec", [128, 1], F32, es11)
            p_tp11 = k.ps(es11, "p11ptp", [128, 1024], F32)
            p_gate = k.ps(es11, "p11pg", [128, 65], F32)
            sgt = [k.sb("p11sgt", [128, 1], F32, es11) for _ in range(2)]
            p_cnt = k.ps(es11, "p11pc", [1, 64], F32)
            v.memset(onec[:], 1.0)

            entry_st11 = stmarks(k)

            def load11(i):
                d.wait(entry_st11)
                d.wait(m_p9)
                sl = k.ld_sems[i % 2]
                d.start(nin[i % 2][:], o_x1[i * 128:(i + 1) * 128, :], sem=sl)
                return d.mark(sl)

            mk11 = [load11(0)]
            if ntiles > 1:
                mk11.append(load11(1))
            m_xT = None
            m_gmm_prev = None
            m_soft_prev = None
            m_odma = [None, None]
            m_oh_prev = None
            identf = k.sb("p11identf", [128, 128], F32, es11)
            v.wait(k.m_ident)
            v.tensor_copy(out=identf[:], in_=k.ident[:])
            m_idf11 = k.v.mark()
            for it in range(ntiles):
                v.wait(mk11[it])
                if m_xT is not None:
                    v.wait(m_xT)
                emit_ln_core(k, nin[it % 2][:], D, sm11b, scr11[:], rstd11[:])
                v.tensor_scalar(out=n1f11[:], in0=nin[it % 2][:],
                                scalar1=sm11b[:, 0:1], scalar2=rstd11[:],
                                op0=ALU.subtract, op1=ALU.mult)
                m_n1f = v.mark()
                t.wait(m_n1f)
                if m_xT is not None:
                    t.wait(m_xT)
                for c in range(8):
                    t.transpose(p_tp11[:, c * 128:(c + 1) * 128],
                                n1f11[:, c * 128:(c + 1) * 128], identf[:])
                m_tp = t.mark()
                v.wait(m_tp)
                if m_gmm_prev is not None:
                    v.wait(m_gmm_prev)
                v.tensor_copy(out=nT[:], in_=p_tp11[:])
                m_xT = v.mark()
                t.wait(m_xT)
                if m_soft_prev is not None:
                    t.wait(m_soft_prev)
                for c in range(8):
                    t.matmul(p_gate[:], nT[:, c * 128:(c + 1) * 128],
                             gate_sb[:, c * 65:(c + 1) * 65],
                             start=(c == 0), stop=(c == 7))
                m_gmm = t.mark()
                m_gmm_prev = m_gmm
                v.wait(m_gmm)
                v.tensor_reduce(out=sm11[:, 0:1], in_=p_gate[:, 0:64], axis=AX.X,
                                op=ALU.max, negate=True)
                s.wait(v.mark())
                if m_odma[it % 2] is not None:
                    s.wait(m_odma[it % 2])
                s.activation(out=probs[:], in_=p_gate[:, 0:64], func=AF.Exp,
                             bias=sm11[:, 0:1])
                s.activation(out=sgt[it % 2][:], in_=p_gate[:, 64:65], func=AF.Sigmoid)
                m_soft = s.mark()
                m_soft_prev = m_soft
                v.wait(m_soft)
                v.tensor_reduce(out=sm11[:, 1:2], in_=probs[:], axis=AX.X, op=ALU.add)
                v.reciprocal(out=sm11[:, 1:2], in_=sm11[:, 1:2])
                v.tensor_scalar(out=probs[:], in0=probs[:], scalar1=sm11[:, 1:2],
                                scalar2=None, op0=ALU.mult)
                if m_odma[it % 2] is not None:
                    v.wait(m_odma[it % 2])
                v.max(out=pm8[:], in_=probs[:])
                v.memset(used[:], 0.0)
                for j in range(TOPK):
                    v.tensor_scalar(out=eqm[:], in0=probs[:], scalar1=pm8[:, j:j + 1],
                                    scalar2=None, op0=ALU.is_equal)
                    v.scalar_tensor_tensor(out=cand[:], in0=eqm[:], scalar=-65536.0,
                                           in1=iota_sb[:], op0=ALU.mult, op1=ALU.add)
                    v.tensor_scalar(out=cand[:], in0=cand[:], scalar1=65536.0,
                                    scalar2=None, op0=ALU.add)
                    v.scalar_tensor_tensor(out=cand[:], in0=used[:], scalar=65536.0,
                                           in1=cand[:], op0=ALU.mult, op1=ALU.add)
                    v.tensor_reduce(out=idx8s[it % 2][:, j:j + 1], in_=cand[:],
                                    axis=AX.X, op=ALU.min)
                    v.tensor_scalar(out=eqm[:], in0=iota_sb[:],
                                    scalar1=idx8s[it % 2][:, j:j + 1],
                                    scalar2=None, op0=ALU.is_equal)
                    v.tensor_tensor(out=used[:], in0=used[:], in1=eqm[:], op=ALU.max)
                v.tensor_reduce(out=sm11[:, 2:3], in_=pm8[:], axis=AX.X, op=ALU.add)
                v.reciprocal(out=sm11[:, 2:3], in_=sm11[:, 2:3])
                v.tensor_scalar(out=w8s[it % 2][:], in0=pm8[:], scalar1=sm11[:, 2:3],
                                scalar2=None, op0=ALU.mult)
                m_top8 = v.mark()
                d.wait(m_top8)
                d.start(o_idx8[it * 128:(it + 1) * 128, :], idx8s[it % 2][:],
                        sem=k.st_sems[it % 2])
                d.start(o_w8[it * 128:(it + 1) * 128, :], w8s[it % 2][:],
                        sem=k.st_sems[it % 2])
                d.wait(m_soft)
                d.start(o_sgate[it * 128:(it + 1) * 128, :], sgt[it % 2][:],
                        sem=k.st_sems[it % 2])
                m_odma[it % 2] = d.mark(k.st_sems[it % 2])
                if m_oh_prev is not None:
                    v.wait(m_oh_prev)
                ioh = iota_sb[:].unsqueeze(1).to_broadcast([128, 8, 64])
                idq = idx8s[it % 2][:].unsqueeze(2).to_broadcast([128, 8, 64])
                v.tensor_tensor(out=ohe[:].rearrange("p (j e) -> p j e", j=8),
                                in0=idq, in1=ioh, op=ALU.is_equal)
                v.tensor_reduce(out=ohr[:],
                                in_=ohe[:].rearrange("p (j e) -> p e j", j=8),
                                axis=AX.X, op=ALU.add)
                m_oh = v.mark()
                t.wait(m_oh)
                t.matmul(p_cnt[:], onec[:, 0:1], ohr[:], start=(it == 0),
                         stop=(it == ntiles - 1))
                m_oh_prev = t.mark()
                if it + 2 < ntiles:
                    d.wait(m_xT)
                    mk11.append(load11(it + 2))
            v.wait(m_oh_prev)
            v.tensor_copy(out=counts_sb[:], in_=p_cnt[:])
            m_cfin = v.mark()
            d.wait(m_cfin)
            d.start(o_counts, counts_sb[:])

        k.finalize()
    return k



def _bf(x):
    return np.asarray(x, dtype=bfdt)


def prep_A_inputs(core, full):
    b, half = core // 2, core % 2
    x0 = np.ascontiguousarray(full["L0_attn_q_x"][b, half * TPC:(half + 1) * TPC, :])
    eye = np.eye(128, dtype=np.float32)
    bdm = np.kron(np.eye(8, dtype=np.float32), np.ones((16, 16), np.float32))
    iota = np.broadcast_to(np.arange(64, dtype=np.float32), (128, 64)).copy()
    woH = np.ascontiguousarray(
        full["L0_attn_wo"].reshape(16, 64, D).transpose(1, 0, 2).reshape(64, 16 * D))
    m = {
        "x0": x0,
        "wq": full["L0_attn_wq"], "wk": full["L0_attn_wk"], "wv": full["L0_attn_wv"],
        "woH": woH,
        "fcw": full["L0_attn_fc_w"],
        "projw": full["L0_attn_proj_w"],
        "wgT": np.ascontiguousarray(full["L0_ffn_w_gate"].T),
        "wuT": np.ascontiguousarray(full["L0_ffn_w_up"].T),
        "wdT": np.ascontiguousarray(full["L0_ffn_w_down"].T),
        "g1b": np.broadcast_to(full["L0_attn_ln1_w"], (128, D)).copy(),
        "b1b": np.broadcast_to(full["L0_attn_ln1_b"], (128, D)).copy(),
        "g2b": np.broadcast_to(full["L0_attn_ln2_w"], (128, D)).copy(),
        "b2b": np.broadcast_to(full["L0_attn_ln2_b"], (128, D)).copy(),
        "bq_r": _bf(full["L0_attn_bq"].reshape(1, D)),
        "bk_r": _bf(full["L0_attn_bk"].reshape(1, D)),
        "bv_r": _bf(full["L0_attn_bv"].reshape(1, D)),
        "bo_r": _bf(full["L0_attn_bo"].reshape(1, D)),
        "fcb_r": _bf(full["L0_attn_fc_b"].reshape(1, MLP0)),
        "projb_r": _bf(full["L0_attn_proj_b"].reshape(1, D)),
        "gatew": np.ascontiguousarray(np.concatenate(
            [full["L1_ffn_gate_w"], full["L1_ffn_shared_gate_w"]], axis=1)),
        "identb": _bf(eye),
        "bdm": bdm,
        "iota64": iota,
    }
    return m


def prep_C1_inputs(core, full, b_outs):
    c = core
    SL = HID1 // NCORES
    eye = np.eye(128, dtype=np.float32)
    rs = slice(c * SL, (c + 1) * SL)
    return {
        "w1_sl": np.ascontiguousarray(full["L1_attn_w1"][rs]),
        "w2_sl": np.ascontiguousarray(full["L1_attn_w2"].T[rs]),
        "dw1_sl": np.concatenate([np.asarray(b_outs[r]["dw1"], np.float32)[rs]
                                  for r in range(NCORES)], axis=0),
        "dw2_sl": np.concatenate([np.asarray(b_outs[r]["dw2T"], np.float32)[rs]
                                  for r in range(NCORES)], axis=0),
        "c1alpha": np.broadcast_to(np.float32(full["L1_attn_alpha"]),
                                   (128, 1)).copy(),
        "c1theta": np.broadcast_to(np.float32(full["L1_attn_theta"]),
                                   (128, 1)).copy(),
        "c1ident": _bf(eye),
    }


def prep_C_inputs(core, full, a_outs, b_outs, c1_outs):
    c = core
    eye = np.eye(128, dtype=np.float32)
    return {
        "w1tp_in": np.concatenate([np.asarray(c1_outs[r]["w1tp"])
                                   for r in range(NCORES)], axis=1),
        "w2p_in": np.concatenate([np.asarray(c1_outs[r]["w2p"])
                                  for r in range(NCORES)], axis=0),
        "db1_all": np.concatenate([np.asarray(b_outs[r]["db1"], np.float32)
                                   for r in range(NCORES)], axis=1),
        "db2_all": np.concatenate([np.asarray(b_outs[r]["db2"], np.float32)
                                   for r in range(NCORES)], axis=1),
        "b1_row": full["L1_attn_b1"].reshape(1, -1),
        "b2_row": full["L1_attn_b2"].reshape(1, -1),
        "alpha128": np.broadcast_to(np.float32(full["L1_attn_alpha"]),
                                    (128, 1)).copy(),
        "theta128": np.broadcast_to(np.float32(full["L1_attn_theta"]),
                                    (128, 1)).copy(),
        "lin1T": np.ascontiguousarray(full["L1_attn_lin1"].T),
        "lin2T": np.ascontiguousarray(full["L1_attn_lin2"].T),
        "n1own_c": np.asarray(a_outs[c]["n1"]),
        "swa_own": np.asarray(b_outs[c]["swa_full"], np.float32)[64:64 + TPC],
        "tokT_all": np.concatenate([np.asarray(b_outs[r]["tokT"], np.float32)
                                    for r in range(NCORES)], axis=0),
        "wsc_all": np.concatenate([np.asarray(b_outs[r]["wscT"], np.float32)
                                   for r in range(NCORES)], axis=0),
        "xn_all": np.concatenate([np.asarray(a_outs[r]["n1"])
                                  for r in range(NCORES)], axis=0),
        "egu_c": np.ascontiguousarray(
            full["L1_ffn_expert_gate_up_w"][8 * c:8 * c + 8].reshape(8 * D, 2 * I_EXP)),
        "edn_c": np.ascontiguousarray(
            full["L1_ffn_expert_down_w"][8 * c:8 * c + 8].reshape(8 * I_EXP, D)),
        "ebase8": (8.0 * c + np.arange(8, dtype=np.float32)).reshape(8, 1),
        "identb3": _bf(eye),
    }


def prep_D_inputs(core, a_outs, b_outs, c_outs):
    c = core
    return {
        "outtab_all": np.concatenate([np.asarray(c_outs[r]["outtab"], np.float32)
                                      for r in range(NCORES)], axis=0),
        "offs_in": np.asarray(b_outs[c]["offs"], np.float32),
        "x1_in": np.asarray(a_outs[c]["x1"], np.float32),
        "merged_in": np.asarray(c_outs[c]["merged"], np.float32),
        "sh_in": np.asarray(b_outs[c]["sh"], np.float32),
    }


def kernel(**inputs):
    full = inputs
    in_maps = []
    for c in range(NCORES):
        m = prep_A_inputs(c, full)
        b, half = c // 2, c % 2
        m["x0"] = np.ascontiguousarray(full["L0_attn_q_x"][b, half * TPC:(half + 1) * TPC])
        in_maps.append(m)
    ka = build_launch_A()
    resA = run_bass_kernel_spmd(ka.nc, in_maps, core_ids=list(range(NCORES)))
    a_outs = resA.results
    in_maps = [prep_B_inputs(c, full, a_outs) for c in range(NCORES)]
    kb = build_launch_B()
    resB = run_bass_kernel_spmd(kb.nc, in_maps, core_ids=list(range(NCORES)))
    b_outs = resB.results
    in_maps = [prep_C1_inputs(c, full, b_outs) for c in range(NCORES)]
    kc1 = build_launch_C1()
    resC1 = run_bass_kernel_spmd(kc1.nc, in_maps, core_ids=list(range(NCORES)))
    c1_outs = resC1.results
    in_maps = [prep_C_inputs(c, full, a_outs, b_outs, c1_outs) for c in range(NCORES)]
    kc = build_launch_C()
    resC = run_bass_kernel_spmd(kc.nc, in_maps, core_ids=list(range(NCORES)))
    c_outs = resC.results
    in_maps = [prep_D_inputs(c, a_outs, b_outs, c_outs) for c in range(NCORES)]
    kd = build_launch_D()
    resD = run_bass_kernel_spmd(kd.nc, in_maps, core_ids=list(range(NCORES)))
    out = np.empty((B, S, D), dtype=np.float32)
    for c in range(NCORES):
        b, half = c // 2, c % 2
        out[b, half * TPC:(half + 1) * TPC] = np.asarray(resD.results[c]["x2"],
                                                         np.float32)
    return out


if __name__ == "__main__":
    print("module; use driver")


def pass_tokcontract(k, stat_src, Ks, mov_src, N, out, T, src_mark=None, group=4):
    t, v, d = k.t, k.v, k.d
    assert N == 1024
    nch = Ks // 128
    ntiles = T // 128
    es = ExitStack()
    with es:
        statb = [k.sb("tcs", [128, group * 128], BF16, es) for _ in range(2)]
        movb = [k.sb("tcm", [128, N], BF16, es) for _ in range(2)]
        osb = k.sb("tco", [128, N], F32, es)
        ps = [k.ps(es, "tcp", [128, N], F32) for _ in range(group)]
        entry_st = stmarks(k)
        m_evac = None
        for cg in range(0, nch, group):
            gn = min(group, nch - cg)

            def load_t(i):
                if src_mark is not None:
                    d.wait(src_mark)
                d.wait(entry_st)
                sl = k.ld_sems[i % 2]
                d.start(statb[i % 2][:, 0:gn * 128],
                        stat_src[i * 128:(i + 1) * 128, cg * 128:(cg + gn) * 128],
                        sem=sl)
                d.start(movb[i % 2][:], mov_src[i * 128:(i + 1) * 128, :], sem=sl)
                return d.mark(sl)

            mks = [load_t(0)]
            if ntiles > 1:
                mks.append(load_t(1))
            for it in range(ntiles):
                t.wait(mks[it])
                if it == 0 and m_evac is not None:
                    t.wait(m_evac)
                for c in range(gn):
                    st_ = statb[it % 2][:, c * 128:(c + 1) * 128]
                    t.matmul(ps[c][:, 0:512], st_, movb[it % 2][:, 0:512],
                             start=(it == 0), stop=(it == ntiles - 1))
                    t.matmul(ps[c][:, 512:1024], st_, movb[it % 2][:, 512:1024],
                             start=(it == 0), stop=(it == ntiles - 1))
                m_mm = t.mark()
                if it + 2 < ntiles:
                    d.wait(m_mm)
                    mks.append(load_t(it + 2))
            m_mmf = t.mark()
            for c in range(gn):
                v.wait(m_mmf)
                v.tensor_copy(out=osb[:], in_=ps[c][:])
                m_e = k.v.mark()
                d.wait(m_e)
                d.start(out[(cg + c) * 128:(cg + c + 1) * 128, :], osb[:],
                        sem=k.st_sems[0])
                m_evac = k.d.mark(k.st_sems[0])
                v.wait(m_evac)
    return k.d.mark(k.st_sems[0])


def pass_colsum(k, src, N, out, T, src_mark=None):
    t, v, d = k.t, k.v, k.d
    nj = N // 512
    ntiles = T // 128
    es = ExitStack()
    with es:
        movb = [k.sb("csm", [128, N], BF16, es) for _ in range(2)]
        onesc = k.sb("csone", [128, 1], BF16, es)
        osb = k.sb("cso", [1, N], F32, es)
        ps = [k.ps(es, "csp", [1, 512], F32) for _ in range(nj)]
        v.memset(onesc[:], 1.0)
        entry_st = stmarks(k)

        def load_t(i):
            if src_mark is not None:
                d.wait(src_mark)
            d.wait(entry_st)
            sl = k.ld_sems[i % 2]
            d.start(movb[i % 2][:], src[i * 128:(i + 1) * 128, :], sem=sl)
            return d.mark(sl)

        mks = [load_t(0)]
        if ntiles > 1:
            mks.append(load_t(1))
        t.wait(k.v.mark())
        for it in range(ntiles):
            t.wait(mks[it])
            for j in range(nj):
                t.matmul(ps[j][:], onesc[:, 0:1], movb[it % 2][:, j * 512:(j + 1) * 512],
                         start=(it == 0), stop=(it == ntiles - 1))
            m_mm = t.mark()
            if it + 2 < ntiles:
                d.wait(m_mm)
                mks.append(load_t(it + 2))
        m_f = t.mark()
        v.wait(m_f)
        for j in range(nj):
            v.tensor_copy(out=osb[0:1, j * 512:(j + 1) * 512], in_=ps[j][:])
        m_e = k.v.mark()
        d.wait(m_e)
        d.start(out, osb[:], sem=k.st_sems[0])
    return k.d.mark(k.st_sems[0])


def _moe_dispatch(k, To, NTO, idx8_in, w8_in, counts_all, coremask, tokbase,
                  iota64b, iota512b, lexc, o_tokT, o_wscT, o_offs):
    t, v, s, d = k.t, k.v, k.s, k.d
    es = ExitStack()
    with es:
        idxb = [k.sb("mdidx", [128, 8], F32, es) for _ in range(2)]
        w8b = [k.sb("mdw8", [128, 8], F32, es) for _ in range(2)]
        offsb = [k.sb("mdoffs", [128, 8], F32, es) for _ in range(2)]
        cnts = k.sb("mdcnt", [1, 512], F32, es)
        cmsk = k.sb("mdcm", [1, 512], F32, es)
        base = k.sb("mdbase", [1, 64], F32, es)
        onesf = k.sb("mdones", [1, 128], F32, es)
        onescol = k.sb("mdonec", [128, 1], F32, es)
        runb = k.sb("mdrun", [128, 64], F32, es)
        iota64s = k.sb("mdio64", [128, 64], F32, es)
        iota512s = k.sb("mdio512", [128, 512], F32, es)
        lexcs = k.sb("mdlexc", [128, 128], F32, es)
        tokb = k.sb("mdtokb", [128, 1], F32, es)
        tokid = k.sb("mdtokid", [128, 1], F32, es)
        OH = k.sb("mdOH", [128, 512], F32, es)
        jexc = k.sb("mdjexc", [128, 512], F32, es)
        pallj = k.sb("mdpallj", [128, 64], F32, es)
        posfull = k.sb("mdposf", [128, 512], F32, es)
        posj = k.sb("mdposj", [128, 8], F32, es)
        slot = k.sb("mdslot", [128, 8], F32, es)
        keep = k.sb("mdkeep", [128, 8], F32, es)
        w8k = k.sb("mdw8k", [128, 8], F32, es)
        valT = k.sb("mdvalT", [128, 64], F32, es)
        valW = k.sb("mdvalW", [128, 64], F32, es)
        OHS = k.sb("mdOHS", [128, 512], F32, es)
        tokTo = k.sb("mdtokTo", [64, 512], F32, es)
        wscTo = k.sb("mdwscTo", [64, 512], F32, es)
        tcnt = k.sb("mdtcnt", [1, 64], F32, es)
        p_pre = k.ps(es, "mdppre", [128, 512], F32)
        p_bc = k.ps(es, "mdpbc", [128, 64], F32)
        p_tc = k.ps(es, "mdptc", [1, 64], F32)
        p_tok = k.ps(es, "mdptok", [64, 512], F32)
        p_wsc = k.ps(es, "mdpwsc", [64, 512], F32)

        entry_md = stmarks(k)
        d.wait(entry_md)
        d.start(cnts[:], counts_all)
        d.start(cmsk[:], coremask)
        d.start(iota64s[:], iota64b)
        d.start(iota512s[:], iota512b)
        d.start(lexcs[:], lexc)
        d.start(tokb[:], tokbase)
        m_l0 = k.d.mark()
        v.wait(m_l0)
        v.memset(onesf[:], 1.0)
        v.memset(onescol[:], 1.0)
        v.tensor_tensor(out=cnts[:], in0=cnts[:], in1=cmsk[:], op=ALU.mult)
        v.tensor_reduce(out=base[:],
                        in_=cnts[:].rearrange("p (r e) -> p e r", r=8),
                        axis=AX.X, op=ALU.add)
        m_base = k.v.mark()
        t.wait(m_base)
        t.matmul(p_bc[:], onesf[0:1, :], base[0:1, :], start=True, stop=True)
        m_bc = t.mark()
        v.wait(m_bc)
        v.tensor_copy(out=runb[:], in_=p_bc[:])
        m_run = k.v.mark()

        m_scat_prev = None
        m_offs_dma = [None, None]
        for it in range(NTO):
            sl = k.ld_sems[it % 2]
            if it < 2:
                d.wait(entry_md)
            d.start(idxb[it % 2][:], idx8_in[it * 128:(it + 1) * 128, :], sem=sl)
            d.start(w8b[it % 2][:], w8_in[it * 128:(it + 1) * 128, :], sem=sl)
            m_li = d.mark(sl)
            v.wait(m_li)
            idq = idxb[it % 2][:].unsqueeze(2).to_broadcast([128, 8, 64])
            ioh = iota64s[:].unsqueeze(1).to_broadcast([128, 8, 64])
            v.tensor_tensor(out=OH[:].rearrange("p (j e) -> p j e", j=8),
                            in0=idq, in1=ioh, op=ALU.is_equal)
            m_oh = k.v.mark()
            t.wait(m_oh)
            if m_scat_prev is not None:
                t.wait(m_scat_prev)
            t.matmul(p_pre[:], lexcs[:], OH[:], start=True, stop=True)
            m_pre = t.mark()
            v.wait(m_pre)
            v.tensor_reduce(out=pallj[:],
                            in_=p_pre[:].rearrange("p (j e) -> p e j", j=8),
                            axis=AX.X, op=ALU.add)
            v.memset(jexc[:, 0:64], 0.0)
            for j in range(1, 8):
                v.tensor_tensor(out=jexc[:, j * 64:(j + 1) * 64],
                                in0=jexc[:, (j - 1) * 64:j * 64],
                                in1=OH[:, (j - 1) * 64:j * 64], op=ALU.add)
            pallj_b = pallj[:].unsqueeze(1).to_broadcast([128, 8, 64])
            runb_b = runb[:].unsqueeze(1).to_broadcast([128, 8, 64])
            v.tensor_tensor(out=posfull[:].rearrange("p (j e) -> p j e", j=8),
                            in0=jexc[:].rearrange("p (j e) -> p j e", j=8),
                            in1=pallj_b, op=ALU.add)
            v.tensor_tensor(out=posfull[:].rearrange("p (j e) -> p j e", j=8),
                            in0=posfull[:].rearrange("p (j e) -> p j e", j=8),
                            in1=runb_b, op=ALU.add)
            v.tensor_tensor(out=posfull[:], in0=posfull[:], in1=OH[:], op=ALU.mult)
            v.tensor_reduce(out=posj[:],
                            in_=posfull[:].rearrange("p (j e) -> p j e", j=8),
                            axis=AX.X, op=ALU.add)
            v.tensor_scalar(out=slot[:], in0=posj[:], scalar1=float(CAP),
                            scalar2=None, op0=ALU.min)
            v.tensor_scalar(out=keep[:], in0=posj[:], scalar1=float(CAP),
                            scalar2=None, op0=ALU.is_lt)
            if m_offs_dma[it % 2] is not None:
                v.wait(m_offs_dma[it % 2])
            v.tensor_scalar(out=offsb[it % 2][:], in0=idxb[it % 2][:],
                            scalar1=float(CAP), scalar2=None, op0=ALU.mult)
            v.tensor_tensor(out=offsb[it % 2][:], in0=offsb[it % 2][:], in1=slot[:],
                            op=ALU.add)
            v.tensor_scalar(out=w8k[:], in0=keep[:], scalar1=-1.0, scalar2=1.0,
                            op0=ALU.mult, op1=ALU.add)
            v.scalar_tensor_tensor(out=offsb[it % 2][:], in0=w8k[:],
                                   scalar=float(2 ** 30), in1=offsb[it % 2][:],
                                   op0=ALU.mult, op1=ALU.add)
            v.tensor_tensor(out=w8k[:], in0=w8b[it % 2][:], in1=keep[:], op=ALU.mult)
            v.tensor_scalar(out=tokid[:], in0=tokb[:], scalar1=float(it * 128),
                            scalar2=None, op0=ALU.add)
            m_pp = k.v.mark()
            d.wait(m_pp)
            d.start(o_offs[it * 128:(it + 1) * 128, :], offsb[it % 2][:],
                    sem=k.st_sems[it % 2])
            m_offs_dma[it % 2] = k.d.mark(k.st_sems[it % 2])
            v.tensor_reduce(out=OH[:, 0:64],
                            in_=OH[:].rearrange("p (j e) -> p e j", j=8),
                            axis=AX.X, op=ALU.add)
            m_ohr = k.v.mark()
            t.wait(m_ohr)
            t.matmul(p_tc[:], onescol[:, 0:1], OH[:, 0:64], start=True, stop=True)
            m_tc = t.mark()
            v.wait(m_tc)
            v.tensor_copy(out=tcnt[:], in_=p_tc[:])
            m_tcc = k.v.mark()
            t.wait(m_tcc)
            t.matmul(p_bc[:], onesf[0:1, :], tcnt[0:1, :], start=True, stop=True)
            m_bc2 = t.mark()
            v.wait(m_bc2)
            v.tensor_tensor(out=runb[:], in0=runb[:], in1=p_bc[:], op=ALU.add)
            v.tensor_tensor(out=OH[:].rearrange("p (j e) -> p j e", j=8),
                            in0=idq, in1=ioh, op=ALU.is_equal)
            m_oh2 = k.v.mark()
            for j in range(8):
                v.tensor_scalar(out=valT[:], in0=OH[:, j * 64:(j + 1) * 64],
                                scalar1=tokid[:], scalar2=None, op0=ALU.mult)
                v.tensor_scalar(out=valW[:], in0=OH[:, j * 64:(j + 1) * 64],
                                scalar1=w8k[:, j:j + 1], scalar2=None, op0=ALU.mult)
                v.tensor_scalar(out=OHS[:], in0=iota512s[:], scalar1=slot[:, j:j + 1],
                                scalar2=None, op0=ALU.is_equal)
                m_val = k.v.mark()
                t.wait(m_val)
                first = (it == 0 and j == 0)
                last = (it == NTO - 1 and j == 7)
                t.matmul(p_tok[:], valT[:], OHS[:], start=first, stop=last)
                t.matmul(p_wsc[:], valW[:], OHS[:], start=first, stop=last)
                m_scat_prev = t.mark()
                if j < 7:
                    v.wait(m_scat_prev)
        v.wait(m_scat_prev)
        v.tensor_copy(out=tokTo[:], in_=p_tok[:])
        v.tensor_copy(out=wscTo[:], in_=p_wsc[:])
        m_fin = k.v.mark()
        d.wait(m_fin)
        d.start(o_tokT, tokTo[:], sem=k.st_sems[0])
        d.start(o_wscT, wscTo[:], sem=k.st_sems[0])
    return k.d.mark(k.st_sems[0])




def build_launch_B(To=TPC):
    import os as _os
    BSTOP = int(_os.environ.get("BSTOP", "99"))
    k = K()
    TQ = 64 + To
    NTQ = (TQ + 127) // 128
    TQP = NTQ * 128
    KV = 128 * (NTQ - 1) + 640
    NKV = KV // 128
    NTO = To // 128

    n1own = k.din("n1own", [To, D], BF16)
    n1ctx = k.din("n1ctx", [512, D], BF16)
    memw = k.din("memw", [64, D])
    w1T = k.din("w1T", [D, HID1]); b1_r = k.din("b1_r", [1, HID1], BF16)
    w2T = k.din("w2T", [HID1, D]); b2_r = k.din("b2_r", [1, D], BF16)
    w2as = k.din("w2as", [D, HID1])
    wqT = k.din("wqT", [D, D]); wkT = k.din("wkT", [D, D])
    wvT = k.din("wvT", [D, D]); woT = k.din("woT", [D, D])
    sguw = k.din("sguw", [D, 2 * I_SH]); sdw = k.din("sdw", [I_SH, D])
    sgate_in = k.din("sgate_in", [To, 1])
    masks = k.din("masks", [NTQ * 128, 704], BF16)
    rowmask = k.din("rowmask", [TQP, 1])
    counts_all = k.din("counts_all", [1, 8 * 64])
    idx8_in = k.din("idx8_in", [To, TOPK])
    w8_in = k.din("w8_in", [To, TOPK])
    coremask = k.din("coremask", [1, 8 * 64])
    tokbase = k.din("tokbase", [128, 1])
    identb2 = k.din("identb2", [128, 128], BF16)
    iota64b = k.din("iota64b", [128, 64])
    iota512b = k.din("iota512b", [128, 512])
    lexc = k.din("lexc", [128, 128])

    o_swa = k.dout("swa_full", [TQP, D])
    o_dw1 = k.dout("dw1", [HID1, D])
    o_dw2T = k.dout("dw2T", [HID1, D])
    o_db1 = k.dout("db1", [1, HID1])
    o_db2 = k.dout("db2", [1, D])
    o_tokT = k.dout("tokT", [64, CAP])
    o_wscT = k.dout("wscT", [64, CAP])
    o_offs = k.dout("offs", [To, TOPK])
    o_sh = k.dout("sh", [To, D])

    nq_dram = k.dint("nq_dram", [TQP, D], BF16)
    kv_dram = k.dint("kv_dram", [KV, D], BF16)
    qbT_dram = k.dint("qbT_dram", [D, TQP], BF16)
    kbT_dram = k.dint("kbT_dram", [D, KV], BF16)
    vb_dram = k.dint("vb_dram", [KV, D], BF16)
    swah_dram = k.dint("swah_dram", [TQP, D], BF16)
    h1_dram = k.dint("h1_dram", [TQP, HID1], BF16)
    a1_dram = k.dint("a1_dram", [TQP, HID1], BF16)
    dpred_dram = k.dint("dpred_dram", [TQP, D], BF16)
    dh1_dram = k.dint("dh1_dram", [TQP, HID1], BF16)
    sgA_dram = k.dint("sgA_dram", [To, I_SH], BF16)
    shgu_dram = k.dint("shgu_dram", [To, I_SH], BF16)

    with k.es:
        k.build_seqs()
        t, v, s, g, d, gd = k.t, k.v, k.s, k.g, k.d, k.gd
        k.init_common(identb2)

        zb = k.sb("zb", [128, D], BF16)
        membf = k.sb("membf", [64, D], BF16)
        v.memset(zb[:], 0.0)
        m_z = k.v.mark()
        gd.start(membf[:], memw)
        m_mem = k.gd.mark()
        gd.wait(m_mem)
        d.wait(m_z)
        d.wait(m_mem)
        d.start(nq_dram[0:64, :], membf[:])
        d.start(kv_dram[0:64, :], membf[:])
        d.start(nq_dram[64:64 + To, :], n1own)
        d.start(kv_dram[64:576, :], n1ctx)
        d.start(kv_dram[576:576 + To, :], n1own)
        d.start(nq_dram[64 + To:TQP, :], zb[0:TQP - 64 - To, :])
        d.start(kv_dram[576 + To:KV, :], zb[0:KV - 576 - To, :])
        m_b0 = k.d.mark()

        es1w = ExitStack()
        wq_sb = k.sb("bwq", [128, 8 * D], BF16, es1w)
        wk_sb = k.sb("bwk", [128, 8 * D], BF16, es1w)
        wv_sb = k.sb("bwv", [128, 8 * D], BF16, es1w)
        guard_weights(k)
        load_cast(k, wqT, wq_sb, 8, D)
        load_cast(k, wkT, wk_sb, 8, D)
        load_cast(k, wvT, wv_sb, 8, D)
        m_w1 = gd.mark()
        t.wait(m_w1)
        pass_gemm(k, nq_dram, TQP, D, D, wq_sb, out=qbT_dram, src_mark=m_b0,
                  store_transposed=True)
        pass_gemm(k, kv_dram, KV, D, D, wk_sb, out=kbT_dram, src_mark=m_b0,
                  store_transposed=True)
        pass_gemm(k, kv_dram, KV, D, D, wv_sb, out=vb_dram, src_mark=m_b0)
        m_b1 = stmarks(k)
        es1w.close()

        if BSTOP < 1:
            k.finalize()
            return k
        with ExitStack() as esw:
            kfm = k.sb("kfm", [128, 2 * KV], BF16, esw)
            vh = k.sb("vh", [128, NKV * 256], BF16, esw)
            msk_all = k.sb("mskall", [128, NTQ * 704], BF16, esw)
            qfm = [k.sb("qfm", [128, 256], BF16, esw) for _ in range(2)]
            Pb = [k.sb("Pb", [128, 768], BF16, esw) for _ in range(2)]
            PT = [k.sb("PT", [128, 768], BF16, esw) for _ in range(2)]
            rs_ = [k.sb("rs_", [128, 2], F32, esw) for _ in range(2)]
            avout = [k.sb("avout", [128, 256], BF16, esw) for _ in range(2)]
            p_log = [k.ps(esw, "p_log", [128, 1024], F32) for _ in range(2)]
            p_av = [k.ps(esw, "p_av", [128, 256], F32) for _ in range(2)]

            gd.wait(m_b1)
            for i in range(NTQ):
                gd.start(msk_all[:, i * 704:(i + 1) * 704],
                         masks[i * 128:(i + 1) * 128, :], sem=k.gld_sems[0])
            m_msk = gd.mark(k.gld_sems[0])
            gd.wait(m_msk)
            v.wait(m_msk)
            v.memset(Pb[0][:, 704:768], 0.0)
            v.memset(Pb[1][:, 704:768], 0.0)
            m_pbz = k.v.mark()
            t.wait(m_pbz)

            sd = k.sd
            m_lq, m_lgt, m_vm, m_exp = {}, {}, {}, {}
            m_ptd, m_avmm, m_avev, m_avd = {}, {}, {}, {}
            m_kfmT = {}
            m_vh = {}
            for h in range(H1):
                hc = h * 256
                hit0 = h * NTQ
                if h == 0:
                    gd.wait(m_b1)
                else:
                    gd.wait(m_lgt[hit0 - 1])
                    gd.wait(m_avmm[hit0 - 1])
                for cc in range(2):
                    gd.start(kfm[:, cc * KV:(cc + 1) * KV],
                             kbT_dram[hc + cc * 128:hc + (cc + 1) * 128, :],
                             sem=k.gld_sems[h % 2])
                m_kfmT[h] = gd.mark(k.gld_sems[h % 2])
                gd.wait(m_kfmT[h])
                gd.start(vh[:].rearrange("p (c f) -> p c f", c=NKV),
                         vb_dram[:, hc:hc + 256].rearrange("(c p) f -> p c f", c=NKV),
                         sem=k.gld_sems[h % 2])
                m_vh[h] = gd.mark(k.gld_sems[h % 2])
                gd.wait(m_vh[h])

                if h == 0:
                    d.wait(m_b1)
                if hit0 - 2 >= 0:
                    d.wait(m_lgt[hit0 - 2])
                d.start(qfm[hit0 % 2][:].rearrange("p (c f) -> p c f", c=2),
                        qbT_dram[hc:hc + 256, 0:128]
                        .rearrange("(c p) f -> p c f", c=2),
                        sem=k.ld_sems[hit0 % 2])
                m_lq[hit0] = d.mark(k.ld_sems[hit0 % 2])

                for step in range(NTQ + 1):
                    i = step
                    j = step - 1
                    it = hit0 + i
                    jt = hit0 + j
                    if i < NTQ:
                        t.wait(m_lq[it])
                        if i < 2:
                            t.wait(m_kfmT[h])
                        if it - 2 >= 0:
                            t.wait(m_exp[it - 2])
                        for cc in range(2):
                            st_ = qfm[it % 2][:, cc * 128:(cc + 1) * 128]
                            base = cc * KV + i * 128
                            t.matmul(p_log[it % 2][:, 0:512], st_,
                                     kfm[:, base:base + 512],
                                     start=(cc == 0), stop=(cc == 1))
                            t.matmul(p_log[it % 2][:, 512:640], st_,
                                     kfm[:, base + 512:base + 640],
                                     start=(cc == 0), stop=False)
                            t.matmul(p_log[it % 2][:, 640:704], st_,
                                     kfm[:, cc * KV:cc * KV + 64],
                                     start=False, stop=(cc == 1))
                        m_lgt[it] = t.mark()
                        v.wait(m_lgt[it])
                        if it - 2 >= 0:
                            v.wait(m_exp[it - 2])
                        v.tensor_tensor(out=p_log[it % 2][:, 0:704],
                                        in0=p_log[it % 2][:, 0:704],
                                        in1=msk_all[:, i * 704:(i + 1) * 704],
                                        op=ALU.add)
                        v.tensor_reduce(out=rs_[it % 2][:, 0:1],
                                        in_=p_log[it % 2][:, 0:704], axis=AX.X,
                                        op=ALU.max, negate=True)
                        v.tensor_scalar(out=rs_[it % 2][:, 0:1],
                                        in0=rs_[it % 2][:, 0:1],
                                        scalar1=1.0 / 16.0, scalar2=None,
                                        op0=ALU.mult)
                        m_vm[it] = k.v.mark()
                    if j >= 0:
                        k.s.wait(m_exp[jt])
                        if jt - 2 >= 0:
                            k.s.wait(m_avmm[jt - 2])
                        if jt == 0:
                            k.s.wait(m_pbz)
                        sd.start(PT[jt % 2][:].rearrange("p (c f) -> p c f", c=6),
                                 Pb[jt % 2][:].rearrange("p (c f) -> p c f", c=6),
                                 transpose=True, sem=k.sd_sems[jt % 2])
                        m_ptd[jt] = sd.mark(k.sd_sems[jt % 2])
                    if i < NTQ:
                        s.wait(m_vm[it])
                        if it - 2 >= 0:
                            s.wait(m_ptd[it - 2])
                            s.wait(m_avev[it - 2])
                        s.activation(out=Pb[it % 2][:, 0:704],
                                     in_=p_log[it % 2][:, 0:704], func=AF.Exp,
                                     bias=rs_[it % 2][:, 0:1], scale=1.0 / 16.0,
                                     accum_out=rs_[it % 2][:, 1:2])
                        m_exp[it] = k.s.mark()
                        if i + 1 < NTQ:
                            d.wait(m_lgt[it - 1] if it - 1 >= 0 else m_b1)
                            d.start(qfm[(it + 1) % 2][:]
                                    .rearrange("p (c f) -> p c f", c=2),
                                    qbT_dram[hc:hc + 256,
                                             (i + 1) * 128:(i + 2) * 128]
                                    .rearrange("(c p) f -> p c f", c=2),
                                    sem=k.ld_sems[(it + 1) % 2])
                            m_lq[it + 1] = d.mark(k.ld_sems[(it + 1) % 2])
                    if j >= 0:
                        v.wait(m_exp[jt])
                        v.reciprocal(out=rs_[jt % 2][:, 1:2], in_=rs_[jt % 2][:, 1:2])
                        t.wait(m_ptd[jt])
                        if j < 2:
                            t.wait(m_vh[h])
                        if jt - 2 >= 0:
                            t.wait(m_avev[jt - 2])
                        for c5 in range(5):
                            t.matmul(p_av[jt % 2][:],
                                     PT[jt % 2][:, c5 * 128:(c5 + 1) * 128],
                                     vh[:, (j + c5) * 256:(j + c5 + 1) * 256],
                                     start=(c5 == 0), stop=False)
                        t.matmul(p_av[jt % 2][:], PT[jt % 2][0:64, 640:768],
                                 vh[0:64, 0:256], start=False, stop=True)
                        m_avmm[jt] = t.mark()
                        v.wait(m_avmm[jt])
                        if jt - 2 >= 0:
                            v.wait(m_avd[jt - 2])
                        v.tensor_scalar(out=avout[jt % 2][:], in0=p_av[jt % 2][:],
                                        scalar1=rs_[jt % 2][:, 1:2], scalar2=None,
                                        op0=ALU.mult)
                        m_avev[jt] = k.v.mark()
                        d.wait(m_avev[jt])
                        d.start(swah_dram[j * 128:(j + 1) * 128, hc:hc + 256],
                                avout[jt % 2][:], sem=k.st_sems[jt % 2])
                        m_avd[jt] = d.mark(k.st_sems[jt % 2])
        m_b2 = stmarks(k)

        if BSTOP < 2:
            k.finalize()
            return k
        es3w = ExitStack()
        wo_sb = k.sb("bwo", [128, 8 * D], BF16, es3w)
        guard_weights(k)
        load_cast(k, woT, wo_sb, 8, D)
        t.wait(gd.mark())
        pass_gemm(k, swah_dram, TQP, D, D, wo_sb, out=o_swa, out_dtype=F32,
                  src_mark=m_b2)
        m_b3 = stmarks(k)
        es3w.close()

        if BSTOP < 3:
            k.finalize()
            return k
        es4w = ExitStack()
        w1_sb = k.sb("bw1", [128, 8 * HID1], BF16, es4w)
        bias1 = k.sb("bb1", [1, HID1], BF16, es4w)
        guard_weights(k)
        load_cast(k, w1T, w1_sb, 8, HID1)
        gd.start(bias1[:], b1_r)
        t.wait(gd.mark())
        h1keep = [k.sb("h1keep", [128, 2048], BF16, es4w) for _ in range(2)]
        h1dma = {"m": [None, None]}

        def mk_h1_epi(half):
            def epi(k2, m_mm, psum_ap, out_ap, aux):
                it = aux["tile"]
                v, s, d = k2.v, k2.s, k2.d
                v.wait(m_mm)
                if h1dma["m"][it % 2] is not None:
                    v.wait(h1dma["m"][it % 2])
                v.tensor_copy(out=h1keep[it % 2][:], in_=psum_ap)
                m_c = k2.v.mark()
                d.wait(m_c)
                d.start(h1_dram[it * 128:(it + 1) * 128,
                                half * 2048:half * 2048 + 2048],
                        h1keep[it % 2][:], sem=k2.st2_sems[it % 2])
                h1dma["m"][it % 2] = k2.d.mark(k2.st2_sems[it % 2])
                s.wait(m_mm)
                s.wait(m_c)
                s.activation(out=out_ap, in_=psum_ap, func=AF.Silu)
                return k2.s.mark()
            return epi

        for half in range(2):
            pass_gemm(k, nq_dram, TQP, D, 2048, w1_sb, wcol0=half * 2048,
                      wstride=HID1, bias_sb=bias1, bias_col0=half * 2048,
                      out=a1_dram[:, half * 2048:half * 2048 + 2048],
                      epilogue=mk_h1_epi(half), src_mark=m_b0)
        m_b4 = stmarks(k)
        es4w.close()

        if BSTOP < 4:
            k.finalize()
            return k
        es5w = ExitStack()
        w2_sb = k.sb("bw2", [128, 32 * D], BF16, es5w)
        bias2 = k.sb("bb2", [1, D], BF16, es5w)
        guard_weights(k)
        load_cast(k, w2T, w2_sb, 32, D)
        gd.start(bias2[:], b2_r)
        t.wait(gd.mark())

        def dpred_epi(k2, m_mm, psum_ap, out_ap, aux):
            v = k2.v
            v.wait(m_mm)
            v.tensor_tensor(out=out_ap, in0=psum_ap, in1=aux["res"][0], op=ALU.subtract)
            v.tensor_scalar(out=out_ap, in0=out_ap, scalar1=aux["res"][1],
                            scalar2=None, op0=ALU.mult)
            return k2.v.mark()

        pass_gemm(k, a1_dram, TQP, HID1, D, w2_sb, bias_sb=bias2, out=dpred_dram,
                  epilogue=dpred_epi,
                  res_srcs=[(o_swa, F32), (rowmask, F32)],
                  src_mark=m_b4, wait_marks=(m_b3,))
        m_b5 = stmarks(k)
        es5w.close()

        if BSTOP < 5:
            k.finalize()
            return k
        es6w = ExitStack()
        w2a_sb = k.sb("bw2a", [128, 8 * HID1], BF16, es6w)
        guard_weights(k)
        load_cast(k, w2as, w2a_sb, 8, HID1)
        t.wait(gd.mark())
        sg6 = k.sb("sg6", [128, 2048], F32, es6w)
        t6 = k.sb("t6", [128, 2048], F32, es6w)

        def dh1_epi(k2, m_mm, psum_ap, out_ap, aux):
            v, s = k2.v, k2.s
            h1t = aux["res"][0]
            s.wait(m_mm)
            s.wait(aux["m_in"])
            s.activation(out=sg6[:], in_=h1t, func=AF.Sigmoid)
            m_sg = k2.s.mark()
            v.wait(m_sg)
            v.tensor_scalar(out=t6[:], in0=sg6[:], scalar1=-1.0, scalar2=1.0,
                            op0=ALU.mult, op1=ALU.add)
            v.tensor_tensor(out=t6[:], in0=t6[:], in1=h1t, op=ALU.mult)
            v.tensor_scalar(out=t6[:], in0=t6[:], scalar1=1.0, scalar2=None,
                            op0=ALU.add)
            v.tensor_tensor(out=t6[:], in0=t6[:], in1=sg6[:], op=ALU.mult)
            v.wait(m_mm)
            v.tensor_tensor(out=out_ap, in0=t6[:], in1=psum_ap, op=ALU.mult)
            return k2.v.mark()

        for half in range(2):
            pass_gemm(k, dpred_dram, TQP, D, 2048, w2a_sb, wcol0=half * 2048,
                      wstride=HID1,
                      out=dh1_dram[:, half * 2048:half * 2048 + 2048],
                      epilogue=dh1_epi,
                      res_srcs=[(h1_dram[:, half * 2048:half * 2048 + 2048], BF16)],
                      src_mark=m_b5)
        m_b6 = stmarks(k)
        es6w.close()

        if BSTOP < 6:
            k.finalize()
            return k
        pass_tokcontract(k, a1_dram, HID1, dpred_dram, D, o_dw2T, TQP,
                         src_mark=m_b6, group=4)
        pass_tokcontract(k, dh1_dram, HID1, nq_dram, D, o_dw1, TQP,
                         src_mark=m_b6, group=4)
        pass_colsum(k, dh1_dram, HID1, o_db1, TQP, src_mark=m_b6)
        pass_colsum(k, dpred_dram, D, o_db2, TQP, src_mark=m_b6)

        if BSTOP < 7:
            k.finalize()
            return k
        _moe_dispatch(k, To, NTO, idx8_in, w8_in, counts_all, coremask, tokbase,
                      iota64b, iota512b, lexc, o_tokT, o_wscT, o_offs)

        if BSTOP < 8:
            k.finalize()
            return k
        es11w = ExitStack()
        sgu_sb = k.sb("bsgu", [128, 8 * 2 * I_SH], BF16, es11w)
        guard_weights(k)
        load_cast(k, sguw, sgu_sb, 8, 2 * I_SH)
        t.wait(gd.mark())

        def silu_epi2(k2, m_mm, psum_ap, out_ap, aux):
            k2.s.wait(m_mm)
            k2.s.activation(out=out_ap, in_=psum_ap, func=AF.Silu)
            return k2.s.mark()

        for half in range(2):
            pass_gemm(k, n1own, To, D, 1024, sgu_sb, wcol0=half * 1024,
                      wstride=2 * I_SH,
                      out=sgA_dram[:, half * 1024:half * 1024 + 1024],
                      epilogue=silu_epi2, src_mark=None)
        m_sga = stmarks(k)

        def upmul_epi2(k2, m_mm, psum_ap, out_ap, aux):
            k2.v.wait(m_mm)
            k2.v.tensor_tensor(out=out_ap, in0=psum_ap, in1=aux["res"][0], op=ALU.mult)
            return k2.v.mark()

        for half in range(2):
            pass_gemm(k, n1own, To, D, 1024, sgu_sb, wcol0=I_SH + half * 1024,
                      wstride=2 * I_SH,
                      out=shgu_dram[:, half * 1024:half * 1024 + 1024],
                      epilogue=upmul_epi2,
                      res_srcs=[(sgA_dram[:, half * 1024:half * 1024 + 1024], BF16)],
                      src_mark=m_sga)
        m_shgu = stmarks(k)
        es11w.close()

        es12w = ExitStack()
        sdw_sb = k.sb("bsdw", [128, 16 * D], BF16, es12w)
        guard_weights(k)
        load_cast(k, sdw, sdw_sb, 16, D)
        t.wait(gd.mark())

        def shfinal_epi(k2, m_mm, psum_ap, out_ap, aux):
            k2.v.wait(m_mm)
            k2.v.tensor_scalar(out=out_ap, in0=psum_ap, scalar1=aux["res"][0],
                               scalar2=None, op0=ALU.mult)
            return k2.v.mark()

        pass_gemm(k, shgu_dram, To, I_SH, D, sdw_sb, out=o_sh, out_dtype=F32,
                  epilogue=shfinal_epi, res_srcs=[(sgate_in, F32)],
                  src_mark=m_shgu)
        es12w.close()

        k.finalize()
    return k


def build_masks(NTQ, parity, To):
    NEG = -1e30
    TQ = 64 + To
    M = np.full((NTQ * 128, 704), NEG, dtype=np.float32)
    for r in range(NTQ * 128):
        tau = r // 128
        p = r % 128
        if r >= TQ or (parity == 1 and r < 64):
            M[r, 0] = 0.0
            continue
        if r < 64:
            for j in range(0, r + 1):
                M[r, 640 + j] = 0.0
            continue
        sown = r - 64
        if parity == 0:
            lo = 576 + max(0, sown - 511)
            hi = 576 + sown
            for kvr in range(lo, hi + 1):
                c = kvr - 128 * tau
                if 0 <= c < 640:
                    M[r, c] = 0.0
            jlo = max(0, sown - 447)
            if jlo <= 63:
                M[r, 640 + jlo:640 + 64] = 0.0
        else:
            for c in range(p + 1, p + 513):
                if 0 <= c < 640:
                    M[r, c] = 0.0
    return M


def prep_B_inputs(core, full, a_outs):
    To = a_outs[core]["n1"].shape[0]
    TQ = 64 + To
    NTQ = (TQ + 127) // 128
    TQP = NTQ * 128
    parity = core % 2
    n1own = np.ascontiguousarray(a_outs[core]["n1"])
    if parity == 1:
        n1ctx = np.ascontiguousarray(a_outs[core - 1]["n1"][-512:])
    else:
        n1ctx = np.zeros((512, D), dtype=bfdt)
    rowmask = np.zeros((TQP, 1), np.float32)
    rowmask[64:64 + To] = 1.0
    if parity == 0:
        rowmask[0:64] = 1.0
    counts_all = np.zeros((1, 8 * 64), np.float32)
    for c in range(len(a_outs)):
        counts_all[0, c * 64:(c + 1) * 64] = np.asarray(a_outs[c]["counts"]).reshape(64)
    cm = np.zeros((1, 8 * 64), np.float32)
    cm[0, :core * 64] = 1.0
    eye = np.eye(128, dtype=np.float32)
    lexc = np.tril(np.ones((128, 128), np.float32), -1).T
    m = {
        "n1own": n1own, "n1ctx": n1ctx, "memw": full["L1_attn_mem"],
        "w1T": np.ascontiguousarray(full["L1_attn_w1"].T),
        "b1_r": _bf(full["L1_attn_b1"].reshape(1, -1)),
        "w2T": np.ascontiguousarray(full["L1_attn_w2"].T),
        "b2_r": _bf(full["L1_attn_b2"].reshape(1, -1)),
        "w2as": full["L1_attn_w2"],
        "wqT": np.ascontiguousarray(full["L1_attn_wq"].T),
        "wkT": np.ascontiguousarray(full["L1_attn_wk"].T),
        "wvT": np.ascontiguousarray(full["L1_attn_wv"].T),
        "woT": np.ascontiguousarray(full["L1_attn_wo"].T),
        "sguw": full["L1_ffn_shared_gate_up_w"],
        "sdw": full["L1_ffn_shared_down_w"],
        "sgate_in": np.asarray(a_outs[core]["sgate"], np.float32),
        "masks": build_masks(NTQ, parity, To).astype(bfdt),
        "rowmask": rowmask,
        "counts_all": counts_all.astype(np.float32),
        "idx8_in": np.asarray(a_outs[core]["idx8"], np.float32),
        "w8_in": np.asarray(a_outs[core]["w8"], np.float32),
        "coremask": cm,
        "tokbase": (core * To + np.arange(128, dtype=np.float32)).reshape(128, 1),
        "identb2": _bf(eye),
        "iota64b": np.broadcast_to(np.arange(64, dtype=np.float32), (128, 64)).copy(),
        "iota512b": np.broadcast_to(np.arange(512, dtype=np.float32), (128, 512)).copy(),
        "lexc": np.ascontiguousarray(lexc),
    }
    return m




def build_launch_C1(ncores=NCORES):
    k = K()
    SL = HID1 // ncores
    nch = SL // 128
    w1_sl = k.din("w1_sl", [SL, D])
    w2_sl = k.din("w2_sl", [SL, D])
    dw1_sl = k.din("dw1_sl", [ncores * SL, D])
    dw2_sl = k.din("dw2_sl", [ncores * SL, D])
    alpha128 = k.din("c1alpha", [128, 1])
    theta128 = k.din("c1theta", [128, 1])
    identb = k.din("c1ident", [128, 128], BF16)
    o_w1tp = k.dout("w1tp", [D, SL], BF16)
    o_w2p = k.dout("w2p", [SL, D], BF16)

    with k.es:
        k.build_seqs()
        t, v, s, g, d, gd = k.t, k.v, k.s, k.g, k.d, k.gd
        k.init_common(identb)
        al = k.sb("al", [128, 1], F32)
        th = k.sb("th", [128, 1], F32)
        oma = k.sb("oma", [128, 1], F32)
        nth = k.sb("nth", [128, 1], F32)
        d.start(al[:], alpha128)
        d.start(th[:], theta128)
        m_at = k.d.mark()
        v.wait(m_at)
        v.tensor_scalar(out=oma[:], in0=al[:], scalar1=-1.0, scalar2=1.0,
                        op0=ALU.mult, op1=ALU.add)
        v.tensor_scalar(out=nth[:], in0=th[:], scalar1=-1.0, scalar2=None,
                        op0=ALU.mult)
        m_consts = k.v.mark()

        def upd_slice(w_src, dparts_src, out_bf_dram, also_transpose_to):
            es1 = ExitStack()
            with es1:
                gb = [[k.sb("updg", [128, D], F32, es1) for _ in range(ncores + 1)]
                      for _ in range(2)]
                ob = [k.sb("updo", [128, D], BF16, es1) for _ in range(2)]
                tpb = [k.sb("updtp", [128, D], BF16, es1) for _ in range(2)]
                entry = stmarks(k)
                m_done = [None, None]
                m_st = [None, None]
                m_tp = [None, None]
                m_st2 = [None, None]
                for i in range(nch):
                    p = i % 2
                    sl = k.ld_sems[p]
                    if i < 2:
                        d.wait(entry)
                    if m_done[p] is not None:
                        d.wait(m_done[p])
                    d.start(gb[p][0][:], w_src[i * 128:(i + 1) * 128, :], sem=sl)
                    for r in range(ncores):
                        d.start(gb[p][r + 1][:],
                                dparts_src[r * SL + i * 128:r * SL + (i + 1) * 128, :],
                                sem=sl)
                    m_l = d.mark(sl)
                    v.wait(m_l)
                    v.wait(m_consts)
                    if m_st[p] is not None:
                        v.wait(m_st[p])
                    if m_tp[p] is not None:
                        v.wait(m_tp[p])
                    acc = gb[p][1]
                    for r in range(2, ncores + 1):
                        v.tensor_tensor(out=acc[:], in0=acc[:], in1=gb[p][r][:],
                                        op=ALU.add)
                    v.tensor_scalar(out=gb[p][0][:], in0=gb[p][0][:], scalar1=oma[:],
                                    scalar2=None, op0=ALU.mult)
                    v.scalar_tensor_tensor(out=ob[p][:], in0=acc[:], scalar=nth[:],
                                           in1=gb[p][0][:], op0=ALU.mult, op1=ALU.add)
                    m_u = k.v.mark()
                    m_done[p] = m_u
                    d.wait(m_u)
                    if out_bf_dram is not None:
                        d.start(out_bf_dram[i * 128:(i + 1) * 128, :], ob[p][:],
                                sem=k.st_sems[p])
                        m_st[p] = d.mark(k.st_sems[p])
                    if also_transpose_to is not None:
                        if m_st2[p] is not None:
                            d.wait(m_st2[p])
                        d.start(tpb[p][:].rearrange("p (c f) -> p c f", c=8),
                                ob[p][:].rearrange("p (c f) -> p c f", c=8),
                                transpose=True, sem=k.tpd_sems[p])
                        m_tp[p] = d.mark(k.tpd_sems[p])
                        d.wait(m_tp[p])
                        d.start(also_transpose_to[:, i * 128:(i + 1) * 128]
                                .rearrange("(c p) f -> p c f", c=8),
                                tpb[p][:].rearrange("p (c f) -> p c f", c=8),
                                sem=k.st2_sems[p])
                        m_st2[p] = d.mark(k.st2_sems[p])

        upd_slice(w1_sl, dw1_sl, None, o_w1tp)
        upd_slice(w2_sl, dw2_sl, o_w2p, None)
        k.finalize()
    return k


def build_launch_C(To=TPC, ncores=NCORES):
    k = K()
    NTO = To // 128
    TE = ncores * To

    w1tp_in = k.din("w1tp_in", [D, HID1], BF16)
    w2p_in = k.din("w2p_in", [HID1, D], BF16)
    db1_all = k.din("db1_all", [1, ncores * HID1])
    db2_all = k.din("db2_all", [1, ncores * D])
    b1_row = k.din("b1_row", [1, HID1])
    b2_row = k.din("b2_row", [1, D])
    alpha128 = k.din("alpha128", [128, 1])
    theta128 = k.din("theta128", [128, 1])
    lin1T = k.din("lin1T", [D, D])
    lin2T = k.din("lin2T", [D, D])
    n1own_c = k.din("n1own_c", [To, D], BF16)
    swa_own = k.din("swa_own", [To, D])
    tokT_all = k.din("tokT_all", [ncores * 64, CAP])
    wsc_all = k.din("wsc_all", [ncores * 64, CAP])
    xn_all = k.din("xn_all", [TE, D], BF16)
    egu_c = k.din("egu_c", [8 * D, 2 * I_EXP])
    edn_c = k.din("edn_c", [8 * I_EXP, D])
    ebase8 = k.din("ebase8", [8, 1])
    identb3 = k.din("identb3", [128, 128], BF16)

    o_merged = k.dout("merged", [To, D])
    o_outtab = k.dout("outtab", [8 * CAP, D])

    a2_dram = k.dint("a2_dram", [To, HID1], BF16)
    omem_dram = k.dint("omem_dram", [To, D], BF16)
    g1_dram = k.dint("g1_dram", [To, D], BF16)
    toksum_dram = k.dint("toksum_dram", [64, CAP])
    wscsum_dram = k.dint("wscsum_dram", [64, CAP])
    mytok_dram = k.dint("mytok_dram", [8, CAP])
    mywsc_dram = k.dint("mywsc_dram", [8, CAP])
    buf_dram = k.dint("buf_dram", [CAP, D], BF16)
    act_dram = k.dint("act_dram", [CAP, 2 * I_EXP], BF16)

    with k.es:
        k.build_seqs()
        t, v, s, g, d, gd = k.t, k.v, k.s, k.g, k.d, k.gd
        k.init_common(identb3)

        al = k.sb("al", [128, 1], F32)
        th = k.sb("th", [128, 1], F32)
        oma = k.sb("oma", [128, 1], F32)
        nth = k.sb("nth", [128, 1], F32)
        d.start(al[:], alpha128)
        d.start(th[:], theta128)
        m_at = k.d.mark()
        v.wait(m_at)
        v.tensor_scalar(out=oma[:], in0=al[:], scalar1=-1.0, scalar2=1.0,
                        op0=ALU.mult, op1=ALU.add)
        v.tensor_scalar(out=nth[:], in0=th[:], scalar1=-1.0, scalar2=None,
                        op0=ALU.mult)
        m_consts = k.v.mark()

        b1p = k.sb("b1p", [1, HID1], BF16)
        b2p = k.sb("b2p", [1, D], BF16)
        br = k.sb("br", [1, HID1], F32)
        bacc = k.sb("bacc", [1, HID1], F32)
        bld = k.sb("bld", [1, HID1], F32)

        def bias_upd(b_row_src, dball_src, width, out_bp):
            d.wait(k.v.mark())
            d.start(br[0:1, 0:width], b_row_src)
            d.start(bacc[0:1, 0:width], dball_src[0:1, 0:width])
            m_lb = k.d.mark()
            v.wait(m_lb)
            for r in range(1, ncores):
                d.start(bld[0:1, 0:width],
                        dball_src[0:1, r * width:(r + 1) * width],
                        sem=k.ld_sems[0])
                m_lr = d.mark(k.ld_sems[0])
                v.wait(m_lr)
                v.tensor_tensor(out=bacc[0:1, 0:width], in0=bacc[0:1, 0:width],
                                in1=bld[0:1, 0:width], op=ALU.add)
                m_a = k.v.mark()
                d.wait(m_a)
            v.wait(m_consts)
            v.tensor_scalar(out=br[0:1, 0:width], in0=br[0:1, 0:width],
                            scalar1=oma[0:1, :], scalar2=None, op0=ALU.mult)
            v.scalar_tensor_tensor(out=out_bp, in0=bacc[0:1, 0:width],
                                   scalar=nth[0:1, :], in1=br[0:1, 0:width],
                                   op0=ALU.mult, op1=ALU.add)
            return k.v.mark()

        m_b1p = bias_upd(b1_row, db1_all, HID1, b1p[:])
        m_b2p = bias_upd(b2_row, db2_all, D, b2p[:])

        es2 = ExitStack()
        w1p_sb = k.sb("cw1p", [128, 8 * HID1], BF16, es2)
        guard_weights(k)
        for i in range(8):
            gd.start(w1p_sb[:, i * HID1:(i + 1) * HID1],
                     w1tp_in[i * 128:(i + 1) * 128, :])
        m_w1p = gd.mark()
        t.wait(m_w1p)
        t.wait(m_b1p)

        def silu_epi3(k2, m_mm, psum_ap, out_ap, aux):
            k2.s.wait(m_mm)
            k2.s.activation(out=out_ap, in_=psum_ap, func=AF.Silu)
            return k2.s.mark()

        for half in range(2):
            pass_gemm(k, n1own_c, To, D, 2048, w1p_sb, wcol0=half * 2048,
                      wstride=HID1, bias_sb=b1p, bias_col0=half * 2048,
                      out=a2_dram[:, half * 2048:half * 2048 + 2048],
                      epilogue=silu_epi3)
        m_a2 = stmarks(k)
        es2.close()

        es3 = ExitStack()
        w2p_sb = k.sb("cw2p", [128, 32 * D], BF16, es3)
        guard_weights(k)
        for i in range(32):
            gd.start(w2p_sb[:, i * D:(i + 1) * D], w2p_in[i * 128:(i + 1) * 128, :])
        m_w2p = gd.mark()
        t.wait(m_w2p)
        t.wait(m_b2p)
        pass_gemm(k, a2_dram, To, HID1, D, w2p_sb, bias_sb=b2p, out=omem_dram,
                  src_mark=m_a2)
        m_omem = stmarks(k)
        es3.close()

        es4 = ExitStack()
        l1_sb = k.sb("cl1", [128, 8 * D], BF16, es4)
        guard_weights(k)
        load_cast(k, lin1T, l1_sb, 8, D)
        t.wait(gd.mark())

        def sig_epi(k2, m_mm, psum_ap, out_ap, aux):
            k2.s.wait(m_mm)
            k2.s.activation(out=out_ap, in_=psum_ap, func=AF.Sigmoid)
            return k2.s.mark()

        pass_gemm(k, swa_own, To, D, D, l1_sb, out=g1_dram, epilogue=sig_epi,
                  src_dtype=F32)
        m_g1 = stmarks(k)
        es4.close()

        es5 = ExitStack()
        l2_sb = k.sb("cl2", [128, 8 * D], BF16, es5)
        guard_weights(k)
        load_cast(k, lin2T, l2_sb, 8, D)
        t.wait(gd.mark())
        g2t = k.sb("cg2t", [128, D], F32, es5)
        mt = k.sb("cmt", [128, D], F32, es5)
        cf1 = k.sb("ccf1", [128, D], F32, es5)
        cf2 = k.sb("ccf2", [128, D], F32, es5)

        prev_vm = [None]

        def merge_epi(k2, m_mm, psum_ap, out_ap, aux):
            v, s = k2.v, k2.s
            s.wait(m_mm)
            if prev_vm[0] is not None:
                s.wait(prev_vm[0])
            s.activation(out=g2t[:], in_=psum_ap, func=AF.Sigmoid)
            m_sg = k2.s.mark()
            v.wait(m_sg)
            v.wait(aux["m_in"])
            v.tensor_copy(out=cf1[:], in_=aux["res"][0])
            v.tensor_copy(out=cf2[:], in_=aux["res"][2])
            v.tensor_tensor(out=mt[:], in0=g2t[:], in1=cf2[:], op=ALU.mult)
            v.tensor_tensor(out=out_ap, in0=cf1[:], in1=aux["res"][1], op=ALU.mult)
            v.tensor_tensor(out=out_ap, in0=out_ap, in1=mt[:], op=ALU.add)
            prev_vm[0] = k2.v.mark()
            return prev_vm[0]

        pass_gemm(k, omem_dram, To, D, D, l2_sb, out=o_merged, out_dtype=F32,
                  epilogue=merge_epi,
                  res_srcs=[(g1_dram, BF16), (swa_own, F32), (omem_dram, BF16)],
                  src_mark=m_omem, wait_marks=(m_g1,))
        es5.close()

        es6 = ExitStack()
        tok8 = k.sb("ctok8", [64, CAP], F32, es6)
        tokt = k.sb("ctokt", [64, CAP], F32, es6)
        wsc8 = k.sb("cwsc8", [64, CAP], F32, es6)
        myt = k.sb("cmyt", [8, CAP], F32, es6)
        myw = k.sb("cmyw", [8, CAP], F32, es6)
        eb8 = k.sb("ceb8", [8, 1], F32, es6)
        eb8i = k.sb("ceb8i", [8, 1], I32, es6)
        entry6 = stmarks(k)
        d.wait(entry6)
        d.start(eb8[:], ebase8)
        d.start(tok8[:], tokT_all[0:64, :])
        d.start(wsc8[:], wsc_all[0:64, :])
        m_l6 = k.d.mark()
        v.wait(m_l6)
        v.tensor_copy(out=eb8i[:], in_=eb8[:])
        d.wait(k.v.mark())
        for r in range(1, ncores):
            d.start(tokt[:], tokT_all[r * 64:(r + 1) * 64, :], sem=k.ld_sems[1])
            m_lr = d.mark(k.ld_sems[1])
            v.wait(m_lr)
            v.tensor_tensor(out=tok8[:], in0=tok8[:], in1=tokt[:], op=ALU.add)
            m_a = k.v.mark()
            d.wait(m_a)
            d.start(tokt[:], wsc_all[r * 64:(r + 1) * 64, :], sem=k.ld_sems[1])
            m_lw = d.mark(k.ld_sems[1])
            v.wait(m_lw)
            v.tensor_tensor(out=wsc8[:], in0=wsc8[:], in1=tokt[:], op=ALU.add)
            m_a2_ = k.v.mark()
            d.wait(m_a2_)
        m_sum = k.v.mark()
        d.wait(m_sum)
        d.start(toksum_dram[:, :], tok8[:])
        d.start(wscsum_dram[:, :], wsc8[:])
        m_tsd = k.d.mark()
        gd.wait(m_tsd)
        gd.wait(m_sum)
        gd.indirect(myt[:, :], None, toksum_dram,
                    IndirectOffsetOnAxis(ap=eb8i[:, 0:1], axis=0))
        gd.indirect(myw[:, :], None, wscsum_dram,
                    IndirectOffsetOnAxis(ap=eb8i[:, 0:1], axis=0))
        m_my = k.gd.mark()
        gd.wait(m_my)
        d.wait(m_my)
        d.start(mytok_dram[:, :], myt[:])
        d.start(mywsc_dram[:, :], myw[:])
        m_myd = k.d.mark()

        idxc = k.sb("cidxc", [128, 1], F32, es6)
        idxi = k.sb("cidxi", [128, 1], I32, es6)
        bufb = k.sb("cbufb", [128, D], BF16, es6)
        egu_sb = k.sb("cegu", [128, 8 * 2 * I_EXP], BF16, es6)
        edn_sb = k.sb("cedn", [128, 4 * D], BF16, es6)
        sc6 = k.sb("csc6", [128, I_EXP], BF16, es6)
        prev_e = [None]
        for e in range(8):
            guard_weights(k)
            load_cast(k, egu_c[e * D:(e + 1) * D, :], egu_sb, 8, 2 * I_EXP)
            load_cast(k, edn_c[e * I_EXP:(e + 1) * I_EXP, :], edn_sb, 4, D)
            m_we = gd.mark()
            gd.wait(m_we)
            m_ci_prev = None
            for q in range(4):
                d.wait(m_myd)
                d.wait(stmarks(k))
                if m_ci_prev is not None:
                    d.wait(m_ci_prev)
                d.start(idxc[:], mytok_dram[e, q * 128:(q + 1) * 128].rearrange(
                    "(p one) -> p one", one=1))
                m_lq = k.d.mark()
                v.wait(m_lq)
                v.tensor_copy(out=idxi[:], in_=idxc[:])
                m_ci = k.v.mark()
                m_ci_prev = m_ci
                gd.wait(m_ci)
                gd.indirect(bufb[:, :], None, xn_all,
                            IndirectOffsetOnAxis(ap=idxi[:, 0:1], axis=0))
                m_gq = k.gd.mark()
                gd.wait(m_gq)
                d.wait(m_gq)
                d.start(buf_dram[q * 128:(q + 1) * 128, :], bufb[:],
                        sem=k.st_sems[q % 2])
                gd.wait(d.mark(k.st_sems[q % 2]))
            m_bufd = stmarks(k)
            t.wait(m_we)

            def act_epi(k2, m_mm, psum_ap, out_ap, aux):
                v, s = k2.v, k2.s
                s.wait(m_mm)
                if prev_e[0] is not None:
                    s.wait(prev_e[0])
                s.activation(out=sc6[:], in_=psum_ap[:, 0:I_EXP], func=AF.Silu)
                m_si = k2.s.mark()
                v.wait(m_si)
                v.tensor_tensor(out=out_ap[:, 0:I_EXP], in0=sc6[:],
                                in1=psum_ap[:, I_EXP:2 * I_EXP], op=ALU.mult)
                v.memset(out_ap[:, I_EXP:2 * I_EXP], 0.0)
                prev_e[0] = k2.v.mark()
                return prev_e[0]

            pass_gemm(k, buf_dram, CAP, D, 2 * I_EXP, egu_sb, out=act_dram,
                      epilogue=act_epi, src_mark=m_bufd)
            m_act = stmarks(k)

            def wsc_epi(k2, m_mm, psum_ap, out_ap, aux):
                v = k2.v
                v.wait(m_mm)
                v.tensor_scalar(out=out_ap, in0=psum_ap, scalar1=aux["res"][0],
                                scalar2=None, op0=ALU.mult)
                return k2.v.mark()

            pass_gemm(k, act_dram[:, 0:I_EXP], CAP, I_EXP, D, edn_sb,
                      out=o_outtab[e * CAP:(e + 1) * CAP, :], out_dtype=F32,
                      epilogue=wsc_epi,
                      res_srcs=[(mywsc_dram[e, :].rearrange("(s one) -> s one", one=1),
                                 F32)],
                      src_mark=m_act, wait_marks=(m_myd,))
        es6.close()

        k.finalize()
    return k



def build_launch_D(To=TPC, ncores=NCORES):
    k = K()
    NTO = To // 128
    outtab_all = k.din("outtab_all", [ncores * 8 * CAP, D])
    offs_in = k.din("offs_in", [To, TOPK])
    x1_in = k.din("x1_in", [To, D])
    merged_in = k.din("merged_in", [To, D])
    sh_in = k.din("sh_in", [To, D])
    o_x2 = k.dout("x2", [To, D])

    with k.es:
        k.build_seqs()
        t, v, s, g, d, gd = k.t, k.v, k.s, k.g, k.d, k.gd
        offb = [k.sb("doff", [128, TOPK], F32) for _ in range(2)]
        offi = [k.sb("doffi", [128, TOPK], I32) for _ in range(2)]
        ysb = [k.sb("dy", [128, D], F32) for _ in range(2)]
        g8 = [[k.sb("dg8", [128, D], F32) for _ in range(TOPK)] for _ in range(2)]
        x1b = [k.sb("dx1", [128, D], F32) for _ in range(2)]
        mgb = [k.sb("dmg", [128, D], F32) for _ in range(2)]
        shb = [k.sb("dsh", [128, D], F32) for _ in range(2)]
        xo = [k.sb("dxo", [128, D], F32) for _ in range(2)]
        kmask8 = [k.sb("dkm", [128, TOPK], F32) for _ in range(2)]
        offc = k.sb("doffc", [128, TOPK], F32)
        m_xo_dma = [None, None]
        m_add_last = [None, None]
        m_g = [None, None]
        for it in range(NTO):
            p = it % 2
            sl = k.ld_sems[p]
            if m_add_last[p] is not None:
                d.wait(m_add_last[p])
            d.start(offb[p][:], offs_in[it * 128:(it + 1) * 128, :], sem=sl)
            d.start(x1b[p][:], x1_in[it * 128:(it + 1) * 128, :], sem=sl)
            d.start(mgb[p][:], merged_in[it * 128:(it + 1) * 128, :], sem=sl)
            d.start(shb[p][:], sh_in[it * 128:(it + 1) * 128, :], sem=sl)
            m_l = d.mark(sl)
            v.wait(m_l)
            v.tensor_scalar(out=offc[:], in0=offb[p][:],
                            scalar1=float(ncores * 8 * CAP - 1), scalar2=None,
                            op0=ALU.min)
            v.tensor_copy(out=offi[p][:], in_=offc[:])
            v.tensor_scalar(out=kmask8[p][:], in0=offb[p][:],
                            scalar1=float(2 ** 29), scalar2=None, op0=ALU.is_lt)
            m_ci = k.v.mark()
            gd.wait(m_ci)
            if m_add_last[p] is not None:
                gd.wait(m_add_last[p])
            for j in range(TOPK):
                gd.indirect(g8[p][j][:, :], None, outtab_all,
                            IndirectOffsetOnAxis(ap=offi[p][:, j:j + 1], axis=0),
                            sem=k.gld_sems[p])
            m_g[p] = gd.mark(k.gld_sems[p])
            gd.wait(m_g[p])
            v.wait(m_g[p])
            v.tensor_scalar(out=ysb[p][:], in0=g8[p][0][:],
                            scalar1=kmask8[p][:, 0:1], scalar2=None, op0=ALU.mult)
            for j in range(1, TOPK):
                v.scalar_tensor_tensor(out=ysb[p][:], in0=g8[p][j][:],
                                       scalar=kmask8[p][:, j:j + 1],
                                       in1=ysb[p][:], op0=ALU.mult, op1=ALU.add)
            if m_xo_dma[p] is not None:
                v.wait(m_xo_dma[p])
            v.tensor_tensor(out=xo[p][:], in0=x1b[p][:], in1=mgb[p][:], op=ALU.add)
            v.tensor_tensor(out=xo[p][:], in0=xo[p][:], in1=shb[p][:], op=ALU.add)
            v.tensor_tensor(out=xo[p][:], in0=xo[p][:], in1=ysb[p][:], op=ALU.add)
            m_add_last[p] = k.v.mark()
            d.wait(m_add_last[p])
            d.start(o_x2[it * 128:(it + 1) * 128, :], xo[p][:],
                    sem=k.st_sems[p])
            m_xo_dma[p] = d.mark(k.st_sems[p])
        k.finalize()
    return k
